# version 06
# baseline (speedup 1.0000x reference)
import numpy as np
import concourse.bass as bass
import concourse.mybir as mybir
from concourse.bass_utils import run_bass_kernel_spmd

F32 = mybir.dt.float32
F32R = mybir.dt.float32r
AX = mybir.AxisListType
AF = mybir.ActivationFunctionType
ALU = mybir.AluOpType

NCORES = 8
ROWS = 1024
DM = 2048
GROUPS = 32
GMICRO = 4096
NLOC = 131072
KC = 16
MC = 44
SCALE = float(8 ** -0.5)
EPS = 1e-5
VC = 4097.0
YW = 40


def _host_consts():
    c = {}
    c["ident"] = np.eye(128, dtype=np.float32)
    p = np.arange(128)
    c["mask1"] = (p[:, None] // 8 == p[None, :] // 8).astype(np.float32)
    c["mask2"] = (p[:, None] % 16 == p[None, :] % 16).astype(np.float32)

    def shift(step, win):
        m = np.zeros((128, 128), np.float32)
        for q in range(128):
            k = q + step
            if 0 <= k < 128 and win(k) == win(q):
                m[k, q] = 1.0
        return m
    c["cm1"] = shift(-1, lambda x: x // 8)
    c["cp1"] = shift(+1, lambda x: x // 8)
    c["cm2"] = shift(-16, lambda x: x % 16)
    c["cp2"] = shift(+16, lambda x: x % 16)
    return c


def _host_wq_wk(qkv_w):
    wq = np.zeros((2, 128, 40), np.float32)
    wk = np.zeros((2, 128, 40), np.float32)
    for par in range(2):
        for b in range(4):
            r0 = 32 * b + 16 * par
            wq[par, r0:r0 + 16, 0:8] = qkv_w[:, 0:8]
            wq[par, r0:r0 + 16, 32:40] = qkv_w[:, 8:16]
            wk[par, r0:r0 + 16, 0:8] = qkv_w[:, 16:24]
            wk[par, r0:r0 + 16, 32:40] = qkv_w[:, 24:32]
    return wq, wk


def _host_wb_skeleton():
    wb = np.zeros((2, 128, 128), np.float32)
    for par in range(2):
        for b in range(4):
            r0 = 32 * b + 16 * par
            wb[par, r0:r0 + 16, 48:64] = np.eye(16, dtype=np.float32)
    return wb


def build_program():
    nc = bass.Bass("TRN2", target_bir_lowering=False, debug=False)

    def din(name, shape):
        return nc.dram_tensor(name, list(shape), F32, kind="ExternalInput").ap()

    xatt = din("xatt", [ROWS, DM])
    xout = din("xout", [ROWS, DM])
    wgate = din("wgate", [MC, 128, KC, 128])
    wup = din("wup", [MC, 128, KC, 128])
    wdown = din("wdown", [MC, 128, KC, 128])
    gsc = din("gsc", [128, MC])
    usc = din("usc", [128, MC])
    dsc = din("dsc", [128, KC])
    wq_h = din("wq_h", [128, 2, 40])
    wk_h = din("wk_h", [128, 2, 40])
    wb_h = din("wb_h", [128, 2, 128])
    qkvw_h = din("qkvw_h", [16, 48])
    projw1_h = din("projw1_h", [8, 16])
    projw2_h = din("projw2_h", [8, 16])
    projb_h = din("projb_h", [1, 16])
    convw0_h = din("convw0_h", [8, 9])
    convb0_h = din("convb0_h", [8, 1])
    convw1_h = din("convw1_h", [8, 9])
    convb1_h = din("convb1_h", [8, 1])
    lnw_h = din("lnw_h", [128, 16])
    lnb_h = din("lnb_h", [128, 16])
    ones_h = din("ones_h", [1, 128])
    ident_h = din("ident_h", [128, 128])
    mask1_h = din("mask1_h", [128, 128])
    mask2_h = din("mask2_h", [128, 128])
    cm1_h = din("cm1_h", [128, 128])
    cp1_h = din("cp1_h", [128, 128])
    cm2_h = din("cm2_h", [128, 128])
    cp2_h = din("cp2_h", [128, 128])

    o = nc.dram_tensor("o", [ROWS, DM], F32, kind="ExternalOutput").ap()
    s_dram = nc.dram_tensor("s_dram", [16, NLOC], F32, kind="Internal").ap()
    wt_dram = nc.dram_tensor("wt_dram", [16, 128], F32, kind="Internal").ap()

    ctx = []

    def sb(name, shape, dtype=F32):
        g = nc.sbuf_tensor(name, list(shape), dtype)
        t = g.__enter__()
        ctx.append(g)
        return t

    def psalloc(name):
        g = nc.psum_tensor(name, [128, 512], F32)
        t = g.__enter__()
        ctx.append(g)
        return t

    ident = sb("ident", [128, 128])
    mask1 = sb("mask1", [128, 128])
    mask2 = sb("mask2", [128, 128])
    cm1 = sb("cm1", [128, 128])
    cp1 = sb("cp1", [128, 128])
    cm2 = sb("cm2", [128, 128])
    cp2 = sb("cp2", [128, 128])
    wq_s = sb("wq_s", [128, 2, 40])
    wk_s = sb("wk_s", [128, 2, 40])
    wb_s = sb("wb_s", [128, 2, 128])
    qkvw_s = sb("qkvw_s", [16, 48])
    projw1_s = sb("projw1_s", [8, 16])
    projw2_s = sb("projw2_s", [8, 16])
    projb_s = sb("projb_s", [1, 16])
    cw0_s = sb("cw0_s", [8, 9])
    cb0_s = sb("cb0_s", [8, 1])
    cw1_s = sb("cw1_s", [8, 9])
    cb1_s = sb("cb1_s", [8, 1])
    ones1 = sb("ones1", [1, 128])
    beff = sb("beff", [128, 16])
    gamma = sb("gamma", [128, 16])
    beta = sb("beta", [128, 16])
    gsc_s = sb("gsc_s", [128, MC])
    usc_s = sb("usc_s", [128, MC])
    dsc_s = sb("dsc_s", [128, KC])
    wvt = sb("wvt", [8, 16])
    winit = sb("winit", [16, 128])
    smallv = sb("smallv", [128, 64])

    arA = sb("arA", [128, 2048])
    arB = sb("arB", [128, 12288])
    arC = sb("arC", [128, 8192])
    arD = sb("arD", [128, 4096])
    arE = sb("arE", [128, 4096])
    arF = sb("arF", [128, 4096])
    xsr = sb("xsr", [128, 8192], F32R)
    hr = sb("hr", [128, 2048], F32R)
    wr = sb("wr", [128, 4096], F32R)

    xnfm = arA[:, :].rearrange("p (k r) -> p k r", k=KC)
    qs = arB[0:48, 0:4096]
    ks = arB[0:48, 4096:8192]
    vs = arB[:, 8192:12288]
    y2cm = arC[0:16, 0:4096]
    tocm = arC[0:16, 0:4096]
    v1tm = arD[:, 0:264].rearrange("p (t c) -> p t c", c=66)
    v2tm = arD[:, 264:528].rearrange("p (t c) -> p t c", c=66)
    yslab = arD[:, 528:1808].rearrange("p (t c) -> p t c", c=YW)
    tokr = arD[:, 1808:2320].rearrange("p (t c) -> p t c", c=16)
    pbuf = arE[:, 0:512]
    ptm = arE[:, 512:1024]
    zslab = arE[:, 1024:1536].rearrange("p (t c) -> p t c", c=16)
    zsq = arE[:, 1536:2048].rearrange("p (t c) -> p t c", c=16)
    tokout = arE[:, 2048:2560].rearrange("p (t c) -> p t c", c=16)
    xa_tm = arF[:, 0:2048]
    sq_tm = arF[:, 2048:4096]

    xs_hi = xsr[:, 0:4096].rearrange("p (k r) -> p k r", k=KC)
    xs_lo = xsr[:, 4096:8192].rearrange("p (k r) -> p k r", k=KC)
    hhi = hr[:, 0:1024].rearrange("p (j r) -> p j r", j=4)
    hlo = hr[:, 1024:2048].rearrange("p (j r) -> p j r", j=4)
    hraw = arB[:, 0:256]
    hv1 = arB[:, 256:512]
    dslab = arC[:, 0:4096].rearrange("p (k r) -> p k r", k=KC)
    xo_tm = arF[:, 0:2048]
    vhi_tm = arD[:, 0:2048]
    vlo_tm = arD[:, 2048:4096]
    x_tm = arD[:, 0:2048]
    wg_t = wr[:, 0:2048].rearrange("p (k r) -> p k r", k=KC)
    wu_t = wr[:, 2048:4096].rearrange("p (k r) -> p k r", k=KC)
    wd_t = wg_t
    g_tm = arF[:, 0:2048]
    s_tm = arF[:, 2048:4096]
    o_tm = s_tm

    zred = smallv[:, 0:32]
    negm = smallv[:, 32:40]
    rec4 = smallv[:, 60:64]
    mu_c = smallv[:, 41:42]
    rs_c = smallv[:, 42:43]
    y2z = smallv[:, 44:60]

    pA = psalloc("pA")
    pB = psalloc("pB")
    pC = psalloc("pC")
    pD = psalloc("pD")
    pE = psalloc("pE")
    pF = psalloc("pF")
    pG = psalloc("pG")
    pH = psalloc("pH")

    barg = nc.semaphore("bar")
    barsem = barg.__enter__()
    ctx.append(barg)
    gbarg = nc.semaphore("gbar")
    gbarsem = gbarg.__enter__()
    ctx.append(gbarg)
    blkg = nc.Block()
    block = blkg.__enter__()

    phases = []

    def phase(**fns):
        phases.append(fns)

    def dma_fn(n):
        def wrap(fn):
            fn.dma_incs = n
            return fn
        return wrap

    def sw_dma_fn(n):
        def wrap(fn):
            fn.sw_dma_incs = n
            return fn
        return wrap


    @dma_fn(24)
    def init_loads(e):
        for src, dst in (
            (ident_h, ident[:]), (mask1_h, mask1[:]), (mask2_h, mask2[:]),
            (cm1_h, cm1[:]), (cp1_h, cp1[:]), (cm2_h, cm2[:]), (cp2_h, cp2[:]),
            (wq_h, wq_s[:]), (wk_h, wk_s[:]), (wb_h, wb_s[:]),
            (qkvw_h, qkvw_s[:]), (projw1_h, projw1_s[:]),
            (projw2_h, projw2_s[:]), (projb_h, projb_s[:]),
            (convw0_h, cw0_s[:]), (convb0_h, cb0_s[:]),
            (convw1_h, cw1_s[:]), (convb1_h, cb1_s[:]),
            (lnw_h, gamma[:]), (lnb_h, beta[:]), (ones_h, ones1[:]),
            (gsc, gsc_s[:]), (usc, usc_s[:]), (dsc, dsc_s[:]),
        ):
            e.dma_start(out=dst, in_=src).then_inc(barsem, 16)
    phase(sync=init_loads)

    def init_memset(e):
        e.memset(winit[:], 0.0)
        e.memset(v1tm[:, :, 16:17], 1.0)
        e.memset(v2tm[:, :, 16:17], 1.0)
        e.memset(y2cm[:, :], 0.0)
    phase(gpsimd=init_memset)

    def init_scaleq(e):
        e.tensor_scalar_mul(wq_s[:, :, :], wq_s[:, :, :], SCALE)
    phase(vector=init_scaleq)

    def init_wvT(e):
        e.transpose(pA[0:8, 0:16], qkvw_s[0:16, 32:40], ident[0:16, 0:16])
        e.transpose(pA[0:8, 16:32], qkvw_s[0:16, 40:48], ident[0:16, 0:16])
    phase(tensor=init_wvT)

    def init_wvT_ev(e):
        e.tensor_copy(winit[0:8, 0:32], pA[0:8, 0:32])
    phase(vector=init_wvT_ev)

    jobs = [
        (0,   (0, 16),  None,            projw1_s, True, True),
        (16,  (0, 16),  cw0_s[:, 3:4],   projw1_s, True, True),
        (32,  (0, 16),  cw0_s[:, 5:6],   projw1_s, True, True),
        (64,  (16, 32), None,            projw2_s, True, True),
        (80,  (16, 32), cw1_s[:, 1:2],   projw2_s, True, True),
        (96,  (16, 32), cw1_s[:, 7:8],   projw2_s, True, True),
        (112, (0, 16),  cw0_s[:, 4:5],   projw1_s, True, False),
        (112, (16, 32), cw1_s[:, 4:5],   projw2_s, False, True),
    ]
    for col, (c0, c1), diag, pr, st, sp in jobs:
        def vj(e, c0=c0, c1=c1, diag=diag):
            if diag is None:
                e.tensor_copy(wvt[:], winit[0:8, c0:c1])
            else:
                e.tensor_scalar(wvt[:], winit[0:8, c0:c1], diag, None, ALU.mult)
        phase(vector=vj)

        def tj(e, col=col, pr=pr, st=st, sp=sp):
            e.matmul(pB[0:16, col:col + 16], wvt[:], pr[:],
                     start=st, stop=sp)
        phase(tensor=tj)

    def init_beff_mm(e):
        e.matmul(pC[0:1, 0:16], cb0_s[:], projw1_s[:], start=True, stop=False)
        e.matmul(pC[0:1, 0:16], cb1_s[:], projw2_s[:], start=False, stop=True)
    phase(tensor=init_beff_mm)

    def init_beff_add(e):
        e.tensor_add(winit[0:1, 48:64], pC[0:1, 0:16], projb_s[:])
    phase(vector=init_beff_add)

    def init_beff_bc(e):
        e.matmul(pD[0:128, 0:16], ones1[:], winit[0:1, 48:64],
                 start=True, stop=True)
    phase(tensor=init_beff_bc)

    def init_beff_ev(e):
        e.tensor_copy(beff[:], pD[0:128, 0:16])
    phase(vector=init_beff_ev)

    def init_vt_evac(e):
        e.tensor_copy(winit[:, 0:48], pB[0:16, 0:48])
        e.tensor_copy(winit[:, 64:128], pB[0:16, 64:128])
    phase(vector=init_vt_evac)

    @dma_fn(1)
    def init_vt_out(e):
        e.dma_start(out=wt_dram, in_=winit[:]).then_inc(barsem, 16)
    phase(sync=init_vt_out)

    @dma_fn(16)
    def init_vt_in(e):
        for par in range(2):
            for b in range(4):
                r0 = 32 * b + 16 * par
                e.dma_start(out=wb_s[r0:r0 + 16, par, 0:48],
                            in_=wt_dram[:, 0:48]).then_inc(barsem, 16)
                e.dma_start(out=wb_s[r0:r0 + 16, par, 64:128],
                            in_=wt_dram[:, 64:128]).then_inc(barsem, 16)
    phase(sync=init_vt_in)

    def ln_tile(src_dram, dst_tm, tile):
        @dma_fn(1)
        def load(e, tile=tile):
            e.dma_start(out=dst_tm, in_=src_dram[tile * 128:(tile + 1) * 128, :]
                        ).then_inc(barsem, 16)
        phase(sync=load)

        def mean(e):
            e.reduce_sum(mu_c[:], dst_tm, axis=AX.X, negate=True)
            e.drain()
            e.tensor_scalar_mul(mu_c[:], mu_c[:], 1.0 / DM)
            e.drain()
            e.tensor_scalar(dst_tm, dst_tm, mu_c[:], None, ALU.add)
        phase(vector=mean)

        def sq(e):
            e.activation(sq_tm[:], dst_tm, AF.Square)
        phase(scalar=sq)

        def var(e):
            e.reduce_sum(rs_c[:], sq_tm[:], axis=AX.X)
            e.drain()
            e.tensor_scalar(rs_c[:], rs_c[:], 1.0 / DM, EPS, ALU.mult, ALU.add)
        phase(vector=var)

        def rsq(e):
            e.activation(rs_c[:], rs_c[:], AF.Ln)
            e.drain()
            e.activation(rs_c[:], rs_c[:], AF.Exp, scale=-0.5)
        phase(scalar=rsq)

        def fin(e):
            e.tensor_scalar(dst_tm, dst_tm, rs_c[:], None, ALU.mult)
        phase(vector=fin)

    qps = (pA, pB, pC, pD)
    kps = (pE, pF, pG, pH)
    for m in range(GROUPS):
        if m % 4 == 0:
            tile8 = m // 4
            ln_tile(xatt, xa_tm[:, :], tile8)
            for jh in range(4):
                def tp(e, jh=jh):
                    for j2 in range(4):
                        j = jh * 4 + j2
                        e.transpose(pA[:, j2 * 128:(j2 + 1) * 128],
                                    xa_tm[:, j * 128:(j + 1) * 128], ident[:])
                phase(tensor=tp)

                def tpe(e, jh=jh):
                    for j2 in range(4):
                        j = jh * 4 + j2
                        e.tensor_copy(xnfm[:, j, :],
                                      pA[:, j2 * 128:(j2 + 1) * 128])
                phase(vector=tpe)

        def rhs_ap(blkb, m=m):
            mm4 = m % 4
            return xnfm[32 * blkb:32 * blkb + 32, :, mm4 * 32:mm4 * 32 + 32]

        for par in range(2):
            def qkv_mm(e, par=par, rhs_ap=rhs_ap):
                for blkb in range(4):
                    tp = (32 * blkb, 0)
                    r = rhs_ap(blkb)
                    e.matmul(qps[blkb][0:40, :], wq_s[32 * blkb:32 * blkb + 32, par, :],
                             r, start=True, stop=True, tile_position=tp)
                    e.matmul(kps[blkb][0:40, :], wk_s[32 * blkb:32 * blkb + 32, par, :],
                             r, start=True, stop=True, tile_position=tp)
            phase(tensor=qkv_mm)

            def qkv_ev(e, par=par):
                for blkb in range(4):
                    sp = 2 * blkb + par
                    for slab, psl in ((qs, qps[blkb]), (ks, kps[blkb])):
                        d1 = slab[0:8, :].rearrange(
                            "p (i b s) -> p i b s", i=32, b=16)[
                            :, :, :, sp].transpose([0, 2, 1])
                        e.tensor_copy(d1, psl[0:8, :].rearrange(
                            "p (b i) -> p b i", b=16))
                        d2 = slab[32:40, :].rearrange(
                            "p (b s r g) -> p b s r g", b=16, s=8, r=4)[
                            :, :, sp, :, :].transpose([0, 1, 3, 2])
                        e.tensor_copy(d2, psl[32:40, :].rearrange(
                            "p (b g r) -> p b g r", b=16, g=8))
            phase(vector=qkv_ev)

        for par in range(2):
            def qkvB_mm(e, par=par, rhs_ap=rhs_ap):
                for blkb in range(4):
                    e.matmul(qps[blkb][:, :], wb_s[32 * blkb:32 * blkb + 32, par, :],
                             rhs_ap(blkb), start=True, stop=True,
                             tile_position=(32 * blkb, 0))
            phase(tensor=qkvB_mm)

            def qkvB_ev(e, par=par):
                for blkb in range(4):
                    sp = 2 * blkb + par
                    d1 = vs[0:64, :].rearrange(
                        "p (i b s) -> p i b s", i=32, b=16)[
                        :, :, :, sp].transpose([0, 2, 1])
                    e.tensor_copy(d1, qps[blkb][0:64, :].rearrange(
                        "p (b i) -> p b i", b=16))
                    d2 = vs[64:128, :].rearrange(
                        "p (b s r g) -> p b s r g", b=16, s=8, r=4)[
                        :, :, sp, :, :].transpose([0, 1, 3, 2])
                    e.tensor_copy(d2, qps[blkb][64:128, :].rearrange(
                        "p (b g r) -> p b g r", b=16, g=8))
            phase(vector=qkvB_ev)

        for st in range(8):
            tl = [4 * st + i for i in range(4)]

            def mmS2(e, tl=tl):
                for i, t2 in enumerate(tl):
                    e.matmul(pB[:, i * 128:(i + 1) * 128],
                             qs[32:40, t2 * 128:(t2 + 1) * 128],
                             ks[32:40, t2 * 128:(t2 + 1) * 128],
                             start=True, stop=True, tile_position=(32, 0))
            phase(tensor=mmS2)

            def negmax2(e):
                e.reduce_max(negm[:, 4:8],
                             pB[:, :].rearrange("p (t x) -> p t x", t=4),
                             axis=AX.X, negate=True)
            phase(vector=negmax2)

            def expS2(e):
                for i in range(4):
                    e.activation(pbuf[:, i * 128:(i + 1) * 128],
                                 pB[:, i * 128:(i + 1) * 128], AF.Exp,
                                 bias=negm[:, 4 + i:5 + i])
            phase(scalar=expS2)

            def ptrans2(e, tl=tl):
                for i in range(4):
                    e.transpose(pD[:, i * 128:(i + 1) * 128],
                                pbuf[:, i * 128:(i + 1) * 128],
                                ident[:])
                for i, t2 in enumerate(tl):
                    e.transpose(pF[:, i * 128:i * 128 + 64],
                                vs[64:128, t2 * 128:(t2 + 1) * 128],
                                ident[64:128, 64:128])
            phase(tensor=ptrans2)

            def pt_evac2(e, tl=tl):
                e.tensor_mul(ptm[:, 0:512].rearrange("p (t x) -> p t x", t=4),
                             pD[:, :].rearrange("p (t x) -> p t x", t=4),
                             mask1[:].unsqueeze(1).broadcast_to(
                                 [128, 4, 128]))
                for i in range(4):
                    e.tensor_copy(v2tm[:, i, 0:16], pF[:, i * 128:i * 128 + 16])
                    e.tensor_copy(v2tm[:, i, 17:65],
                                  pF[:, i * 128 + 16:i * 128 + 64])
            phase(vector=pt_evac2)

            def av2(e, tl=tl):
                for i in range(4):
                    lhs = ptm[:, i * 128:(i + 1) * 128]
                    e.matmul(pG[:, i * YW:i * YW + 17], lhs, v2tm[:, i, 0:17],
                             start=True, stop=True)
                    e.matmul(pG[:, i * YW + 20:i * YW + 36], cm1[:],
                             v2tm[:, i, 17:33], start=True, stop=False)
                    e.matmul(pG[:, i * YW + 20:i * YW + 36], cp1[:],
                             v2tm[:, i, 33:49], start=False, stop=False)
                    e.matmul(pG[:, i * YW + 20:i * YW + 36], ident[:],
                             v2tm[:, i, 49:65], start=False, stop=True)
            phase(tensor=av2)

            def recs2(e):
                dv = pG[:, 0:4 * YW].rearrange("p (t c) -> p t c", c=YW)[
                    :, :, 16]
                e.activation(rec4[:], dv, AF.Ln)
                e.drain()
                e.activation(rec4[:], rec4[:], AF.Exp, scale=-1.0)
            phase(scalar=recs2)

            for i in range(4):
                def y2div(e, i=i):
                    e.tensor_scalar(y2z[:], pG[:, i * YW:i * YW + 16],
                                    rec4[:, i:i + 1], None, ALU.mult)
                    e.drain()
                    e.tensor_add(y2z[:], y2z[:],
                                 pG[:, i * YW + 20:i * YW + 36])
                phase(vector=y2div)

                def y2t(e, i=i):
                    e.transpose(pH[0:16, i * 128:(i + 1) * 128], y2z[:], ident[:])
                phase(tensor=y2t)

            def y2ev(e, tl=tl):
                for i, t2 in enumerate(tl):
                    b0, h = t2 // 2, t2 % 2
                    dst = y2cm[:, :].rearrange(
                        "p (g r b s) -> p g r b s", g=8, r=4, b=16)[
                        :, :, :, b0, 4 * h:4 * h + 4].transpose([0, 3, 2, 1])
                    e.tensor_copy(dst, pH[0:16, i * 128:(i + 1) * 128].rearrange(
                        "p (s r g) -> p s r g", s=4, r=4))
            phase(vector=y2ev)

        for st in range(8):
            tl = [4 * st + i for i in range(4)]

            def mmS1(e, tl=tl):
                for i, ip in enumerate(tl):
                    e.matmul(pA[:, i * 128:(i + 1) * 128],
                             qs[0:8, ip * 128:(ip + 1) * 128],
                             ks[0:8, ip * 128:(ip + 1) * 128],
                             start=True, stop=True)
            phase(tensor=mmS1)

            def negmax1(e):
                e.reduce_max(negm[:, 0:4],
                             pA[:, :].rearrange("p (t x) -> p t x", t=4),
                             axis=AX.X, negate=True)
            phase(vector=negmax1)

            def expS1(e):
                for i in range(4):
                    e.activation(pbuf[:, i * 128:(i + 1) * 128],
                                 pA[:, i * 128:(i + 1) * 128], AF.Exp,
                                 bias=negm[:, i:i + 1])
            phase(scalar=expS1)

            def ptrans1(e, tl=tl):
                for i in range(4):
                    e.transpose(pC[:, i * 128:(i + 1) * 128],
                                pbuf[:, i * 128:(i + 1) * 128], ident[:])
                for i, ip in enumerate(tl):
                    e.transpose(pE[:, i * 128:i * 128 + 64],
                                vs[0:64, ip * 128:(ip + 1) * 128],
                                ident[0:64, 0:64])
            phase(tensor=ptrans1)

            def pt_evac1(e, tl=tl):
                e.tensor_mul(ptm[:, 0:512].rearrange("p (t x) -> p t x", t=4),
                             pC[:, :].rearrange("p (t x) -> p t x", t=4),
                             mask1[:].unsqueeze(1).broadcast_to(
                                 [128, 4, 128]))
                for i, ip in enumerate(tl):
                    e.tensor_add(v1tm[:, i, 0:16],
                                 pE[:, i * 128:i * 128 + 16], beff[:])
                    e.tensor_copy(v1tm[:, i, 17:49],
                                  pE[:, i * 128 + 16:i * 128 + 48])
                    e.tensor_copy(tokr[:, ip, :],
                                  pE[:, i * 128 + 48:i * 128 + 64])
            phase(vector=pt_evac1)

            def av1(e, tl=tl):
                for i, ip in enumerate(tl):
                    lhs = ptm[:, i * 128:(i + 1) * 128]
                    e.matmul(pC[:, i * YW:i * YW + 17], lhs, v1tm[:, i, 0:17],
                             start=True, stop=True)
                    e.matmul(pC[:, i * YW + 20:i * YW + 36], cm1[:],
                             v1tm[:, i, 17:33], start=True, stop=False)
                    e.matmul(pC[:, i * YW + 20:i * YW + 36], cp1[:],
                             v1tm[:, i, 33:49], start=False, stop=False)
                    e.matmul(pC[:, i * YW + 20:i * YW + 36],
                             y2cm[:, ip * 128:(ip + 1) * 128],
                             ident[0:16, 0:16], start=False, stop=True,
                             is_transpose=True)
            phase(tensor=av1)

            def y1ev(e, tl=tl):
                for i, ip in enumerate(tl):
                    e.copy(yslab[:, ip, :], pC[:, i * YW:(i + 1) * YW])
            phase(scalar=y1ev)

        def ln16r(e):
            e.activation(zred[:], yslab[:, :, 16], AF.Ln)
            e.drain()
            e.activation(zred[:], zred[:], AF.Exp, scale=-1.0)
        phase(scalar=ln16r)

        def ln16a(e):
            e.tensor_mul(zslab[:, :, :], yslab[:, :, 0:16],
                         zred[:].unsqueeze(2).broadcast_to(
                             [128, 32, 16]))
            e.drain()
            e.tensor_add(zslab[:, :, :], zslab[:, :, :], yslab[:, :, 20:36])
            e.drain()
            e.reduce_sum(zred[:], zslab[:, :, :], axis=AX.X, negate=True)
            e.drain()
            e.tensor_scalar_mul(zred[:], zred[:], 1.0 / 16.0)
            e.drain()
            e.tensor_add(zslab[:, :, :], zslab[:, :, :],
                         zred[:].unsqueeze(2).broadcast_to(
                             [128, 32, 16]))
        phase(vector=ln16a)

        def ln16sq(e):
            e.activation(zsq[:, :, :], zslab[:, :, :], AF.Square)
        phase(scalar=ln16sq)

        def ln16b(e):
            e.reduce_sum(zred[:], zsq[:, :, :], axis=AX.X)
            e.drain()
            e.tensor_scalar(zred[:], zred[:], 1.0 / 16.0, EPS, ALU.mult, ALU.add)
        phase(vector=ln16b)

        def ln16c(e):
            e.activation(zred[:], zred[:], AF.Ln)
            e.drain()
            e.activation(zred[:], zred[:], AF.Exp, scale=-0.5)
        phase(scalar=ln16c)

        def ln16d(e):
            e.tensor_mul(zslab[:, :, :], zslab[:, :, :],
                         zred[:].unsqueeze(2).broadcast_to(
                             [128, 32, 16]))
            e.drain()
            e.tensor_mul(zslab[:, :, :], zslab[:, :, :],
                         gamma[:].unsqueeze(1).broadcast_to(
                             [128, 32, 16]))
            e.drain()
            e.tensor_add(zslab[:, :, :], zslab[:, :, :],
                         beta[:].unsqueeze(1).broadcast_to(
                             [128, 32, 16]))
            e.drain()
            e.tensor_add(tokout[:, :, :], zslab[:, :, :], tokr[:, :, :])
        phase(vector=ln16d)

        for q in range(4):
            def toT(e, q=q):
                for j in range(4):
                    e.transpose(pA[0:16, j * 128:(j + 1) * 128],
                                tokout[:, q * 8 + j, :], ident[:])
                for j in range(4):
                    e.transpose(pB[0:16, j * 128:(j + 1) * 128],
                                tokout[:, q * 8 + 4 + j, :], ident[:])
            phase(tensor=toT)

            def toEv(e, q=q):
                e.tensor_copy(tocm[:, q * 1024:q * 1024 + 512], pA[0:16, :])
                e.tensor_copy(tocm[:, q * 1024 + 512:(q + 1) * 1024], pB[0:16, :])
            phase(vector=toEv)

        @dma_fn(1)
        def scat(e, m=m):
            e.dma_start(out=s_dram[:, m * GMICRO:(m + 1) * GMICRO],
                        in_=tocm[:, :]).then_inc(barsem, 16)
        phase(sync=scat)


    allp = (pA, pB, pC, pD, pE, pF, pG, pH)
    for tq in range(4):
        for tile in range(2):
            ln_tile(xout, xo_tm, tq * 2 + tile)

            def velt(e):
                e.tensor_scalar_mul(sq_tm[:], xo_tm[:], VC)
                e.drain()
                e.tensor_sub(vlo_tm[:], sq_tm[:], xo_tm[:])
                e.drain()
                e.tensor_sub(vhi_tm[:], sq_tm[:], vlo_tm[:])
                e.drain()
                e.tensor_sub(vlo_tm[:], xo_tm[:], vhi_tm[:])
            phase(vector=velt)

            for jh in range(4):
                def ftp(e, jh=jh):
                    for j2 in range(4):
                        j = jh * 4 + j2
                        e.transpose(pA[:, j2 * 128:(j2 + 1) * 128],
                                    vhi_tm[:, j * 128:(j + 1) * 128], ident[:])
                        e.transpose(pB[:, j2 * 128:(j2 + 1) * 128],
                                    vlo_tm[:, j * 128:(j + 1) * 128], ident[:])
                phase(tensor=ftp)

                def ftpe(e, tile=tile, jh=jh):
                    for j2 in range(4):
                        j = jh * 4 + j2
                        e.tensor_copy(xs_hi[:, j, tile * 128:(tile + 1) * 128],
                                      pA[:, j2 * 128:(j2 + 1) * 128])
                        e.tensor_copy(xs_lo[:, j, tile * 128:(tile + 1) * 128],
                                      pB[:, j2 * 128:(j2 + 1) * 128])
                phase(vector=ftpe)

        for jb in range(11):
            for jj in range(4):
                j = jb * 4 + jj

                @sw_dma_fn(2)
                def wload(e, j=j):
                    e.dma_start(out=wg_t[:, :, :], in_=wgate[j, :, :, :]
                                ).then_inc(gbarsem, 16)
                    e.dma_start(out=wu_t[:, :, :], in_=wup[j, :, :, :]
                                ).then_inc(gbarsem, 16)
                phase(gpsimd=wload)

                def gu_mm(e):
                    for k in range(KC):
                        e.matmul(pA[:, 0:256], wg_t[:, k, :], xs_hi[:, k, :],
                                 start=(k == 0), stop=False)
                        e.matmul(pA[:, 0:256], wg_t[:, k, :], xs_lo[:, k, :],
                                 start=False, stop=(k == KC - 1))
                    for k in range(KC):
                        e.matmul(pB[:, 0:256], wu_t[:, k, :], xs_hi[:, k, :],
                                 start=(k == 0), stop=False)
                        e.matmul(pB[:, 0:256], wu_t[:, k, :], xs_lo[:, k, :],
                                 start=False, stop=(k == KC - 1))
                phase(tensor=gu_mm)

                def gsilu(e, j=j):
                    e.activation(hv1[:], pA[:, 0:256], AF.Silu,
                                 scale=gsc_s[:, j:j + 1])
                phase(scalar=gsilu)

                def hmake(e, j=j, jj=jj):
                    e.scalar_tensor_tensor(hraw[:], pB[:, 0:256],
                                           usc_s[:, j:j + 1], hv1[:],
                                           ALU.mult, ALU.mult)
                    e.drain()
                    e.tensor_scalar_mul(sq_tm[:, 0:256], hraw[:], VC)
                    e.drain()
                    e.tensor_sub(sq_tm[:, 256:512], sq_tm[:, 0:256], hraw[:])
                    e.drain()
                    e.tensor_sub(hhi[:, jj, :], sq_tm[:, 0:256],
                                 sq_tm[:, 256:512])
                    e.drain()
                    e.tensor_sub(hlo[:, jj, :], hraw[:], hhi[:, jj, :])
                phase(vector=hmake)

            for jj in range(4):
                j = jb * 4 + jj

                @sw_dma_fn(1)
                def dload(e, j=j):
                    e.dma_start(out=wd_t[:, :, :], in_=wdown[j, :, :, :]
                                ).then_inc(gbarsem, 16)
                phase(gpsimd=dload)

                def d_mm(e, jj=jj):
                    for oc in range(KC):
                        pp = allp[oc % 8]
                        c0 = (oc // 8) * 256
                        e.matmul(pp[:, c0:c0 + 256], wd_t[:, oc, :],
                                 hhi[:, jj, :],
                                 start=(jj == 0 and oc < 8), stop=False)
                        e.matmul(pp[:, c0:c0 + 256], wd_t[:, oc, :],
                                 hlo[:, jj, :], start=False,
                                 stop=(jj == 3 and oc >= 8))
                phase(tensor=d_mm)

            def dacc(e, jb=jb):
                for oc in range(KC):
                    pp = allp[oc % 8]
                    c0 = (oc // 8) * 256
                    if jb == 0:
                        e.tensor_copy(dslab[:, oc, :], pp[:, c0:c0 + 256])
                    else:
                        e.tensor_add(dslab[:, oc, :], dslab[:, oc, :],
                                     pp[:, c0:c0 + 256])
            phase(vector=dacc)

        def gfin(e):
            for oc in range(KC):
                e.activation(dslab[:, oc, :], dslab[:, oc, :], AF.Silu,
                             scale=dsc_s[:, oc:oc + 1])
        phase(scalar=gfin)

        for tb in range(2):
            row0 = tq * 256 + tb * 128

            def gT(e, tb=tb):
                for oc in range(4):
                    e.transpose(pA[:, oc * 128:(oc + 1) * 128],
                                dslab[:, oc, tb * 128:(tb + 1) * 128], ident[:])
                for oc in range(4):
                    e.transpose(pB[:, oc * 128:(oc + 1) * 128],
                                dslab[:, 4 + oc, tb * 128:(tb + 1) * 128],
                                ident[:])
                for oc in range(4):
                    e.transpose(pC[:, oc * 128:(oc + 1) * 128],
                                dslab[:, 8 + oc, tb * 128:(tb + 1) * 128],
                                ident[:])
                for oc in range(4):
                    e.transpose(pD[:, oc * 128:(oc + 1) * 128],
                                dslab[:, 12 + oc, tb * 128:(tb + 1) * 128],
                                ident[:])
            phase(tensor=gT)

            def gEv(e):
                e.tensor_copy(g_tm[:, 0:512], pA[:, :])
                e.tensor_copy(g_tm[:, 512:1024], pB[:, :])
                e.tensor_copy(g_tm[:, 1024:1536], pC[:, :])
                e.tensor_copy(g_tm[:, 1536:2048], pD[:, :])
            phase(vector=gEv)

            @dma_fn(2)
            def fload(e, row0=row0):
                e.dma_start(out=x_tm[:], in_=xout[row0:row0 + 128, :]
                            ).then_inc(barsem, 16)
                ta = row0 // 64
                e.dma_start(
                    out=s_tm[:],
                    in_=s_dram.rearrange("c (r j) -> c r j", j=DM)[
                        ta:ta + 2, 0:64, :]).then_inc(barsem, 16)
            phase(sync=fload)

            def fuse(e):
                e.tensor_mul(o_tm[:], s_tm[:], g_tm[:])
                e.drain()
                e.tensor_add(o_tm[:], o_tm[:], x_tm[:])
            phase(vector=fuse)

            @dma_fn(1)
            def fstore(e, row0=row0):
                e.dma_start(out=o[row0:row0 + 128, :], in_=o_tm[:]
                            ).then_inc(barsem, 16)
            phase(sync=fstore)

    if _PHASE_LIMIT is not None:
        del phases[_PHASE_LIMIT:]
    targets = []
    t, tg = 0, 0
    for p in phases:
        targets.append((t, tg))
        for fn in p.values():
            d = getattr(fn, "dma_incs", 0)
            g = getattr(fn, "sw_dma_incs", 0)
            if g:
                tg += 16 * g
            elif d:
                t += 16 * d
            else:
                t += 1
    final_target, final_gtarget = t, tg

    for ename in ("sync", "gpsimd", "tensor", "vector", "scalar"):
        dec = getattr(block, ename)

        def mk(ename=ename):
            def body(eng):
                for pi, p in enumerate(phases):
                    if ename not in p:
                        continue
                    tb, tgb = targets[pi]
                    if tb > 0:
                        eng.wait_ge(barsem, tb)
                    if tgb > 0:
                        eng.wait_ge(gbarsem, tgb)
                    fn = p[ename]
                    fn(eng)
                    if (getattr(fn, "dma_incs", 0) == 0
                            and getattr(fn, "sw_dma_incs", 0) == 0):
                        eng.drain().then_inc(barsem, 1)
                if ename == "sync":
                    eng.wait_ge(barsem, final_target)
                    if final_gtarget > 0:
                        eng.wait_ge(gbarsem, final_gtarget)
            return body
        dec(mk())

    blkg.__exit__(None, None, None)
    for g in reversed(ctx):
        g.__exit__(None, None, None)
    return nc


_PHASE_LIMIT = None
_PROG = None


def _get_prog():
    global _PROG
    if _PROG is None:
        _PROG = build_program()
    return _PROG


def kernel(L0_attn_x, L0_attn_qkv_w, L0_attn_conv_w0, L0_attn_conv_b0,
           L0_attn_conv_w1, L0_attn_conv_b1, L0_attn_proj_w, L0_attn_proj_b,
           L0_attn_ln_w, L0_attn_ln_b, L0_ffn_gate_w, L0_ffn_up_w,
           L0_ffn_down_w, L0_ffn_gate_s, L0_ffn_up_s, L0_ffn_down_s):
    f32 = np.float32
    X = np.ascontiguousarray(np.asarray(L0_attn_x, dtype=f32).reshape(8192, DM))
    qkv_w = np.asarray(L0_attn_qkv_w, dtype=f32)
    consts = _host_consts()
    wq, wk = _host_wq_wk(qkv_w)
    wb = _host_wb_skeleton()

    gate_w = np.asarray(L0_ffn_gate_w, dtype=f32)
    up_w = np.asarray(L0_ffn_up_w, dtype=f32)
    down_w = np.asarray(L0_ffn_down_w, dtype=f32)
    WG = np.ascontiguousarray(
        gate_w.reshape(MC, 128, KC, 128).transpose(0, 3, 2, 1))
    WU = np.ascontiguousarray(
        up_w.reshape(MC, 128, KC, 128).transpose(0, 3, 2, 1))
    WD = np.ascontiguousarray(
        down_w.reshape(KC, 128, MC, 128).transpose(2, 3, 0, 1))

    gsch = np.ascontiguousarray(
        np.asarray(L0_ffn_gate_s, dtype=f32).reshape(MC, 128).T)
    usch = np.ascontiguousarray(
        np.asarray(L0_ffn_up_s, dtype=f32).reshape(MC, 128).T)
    dsch = np.ascontiguousarray(
        np.asarray(L0_ffn_down_s, dtype=f32).reshape(KC, 128).T)

    shared = {
        "wgate": WG, "wup": WU, "wdown": WD,
        "gsc": gsch, "usc": usch, "dsc": dsch,
        "wq_h": np.ascontiguousarray(wq.transpose(1, 0, 2)),
        "wk_h": np.ascontiguousarray(wk.transpose(1, 0, 2)),
        "wb_h": np.ascontiguousarray(wb.transpose(1, 0, 2)),
        "qkvw_h": qkv_w,
        "projw1_h": np.ascontiguousarray(np.asarray(L0_attn_proj_w, f32)[0:8]),
        "projw2_h": np.ascontiguousarray(np.asarray(L0_attn_proj_w, f32)[8:16]),
        "projb_h": np.asarray(L0_attn_proj_b, f32).reshape(1, 16),
        "convw0_h": np.asarray(L0_attn_conv_w0, f32).reshape(8, 9),
        "convb0_h": np.asarray(L0_attn_conv_b0, f32).reshape(8, 1),
        "convw1_h": np.asarray(L0_attn_conv_w1, f32).reshape(8, 9),
        "convb1_h": np.asarray(L0_attn_conv_b1, f32).reshape(8, 1),
        "lnw_h": np.ascontiguousarray(
            np.broadcast_to(np.asarray(L0_attn_ln_w, f32).reshape(1, 16),
                            (128, 16))),
        "lnb_h": np.ascontiguousarray(
            np.broadcast_to(np.asarray(L0_attn_ln_b, f32).reshape(1, 16),
                            (128, 16))),
        "ones_h": np.ones((1, 128), f32),
        "ident_h": consts["ident"], "mask1_h": consts["mask1"],
        "mask2_h": consts["mask2"], "cm1_h": consts["cm1"],
        "cp1_h": consts["cp1"], "cm2_h": consts["cm2"], "cp2_h": consts["cp2"],
    }

    in_maps = []
    rowsets = []
    for k in range(NCORES):
        rows = np.arange(8192)
        sel = (rows % 512 >= 64 * k) & (rows % 512 < 64 * (k + 1))
        rout = rows[sel]
        rowsets.append(rout)
        m = dict(shared)
        m["xatt"] = np.ascontiguousarray(X[1024 * k:1024 * (k + 1)])
        m["xout"] = np.ascontiguousarray(X[rout])
        in_maps.append(m)

    nc = _get_prog()
    res = run_bass_kernel_spmd(nc, in_maps, core_ids=list(range(NCORES)))

    out = np.empty((8192, DM), dtype=f32)
    for k in range(NCORES):
        out[rowsets[k]] = res.results[k]["o"]
    return out.reshape(1, 8192, DM)


# version 07
# speedup vs baseline: 1.1084x; 1.1084x over previous
import numpy as np
import concourse.bass as bass
import concourse.mybir as mybir
from concourse.bass_utils import run_bass_kernel_spmd

F32 = mybir.dt.float32
F32R = mybir.dt.float32r
AX = mybir.AxisListType
AF = mybir.ActivationFunctionType
ALU = mybir.AluOpType

NCORES = 8
ROWS = 1024
DM = 2048
GROUPS = 32
GMICRO = 4096
NLOC = 131072
KC = 16
MC = 44
SCALE = float(8 ** -0.5)
EPS = 1e-5
VC = 4097.0
YW = 40


def _host_consts():
    c = {}
    c["ident"] = np.eye(128, dtype=np.float32)
    p = np.arange(128)
    c["mask1"] = (p[:, None] // 8 == p[None, :] // 8).astype(np.float32)
    c["mask2"] = (p[:, None] % 16 == p[None, :] % 16).astype(np.float32)

    def shift(step, win):
        m = np.zeros((128, 128), np.float32)
        for q in range(128):
            k = q + step
            if 0 <= k < 128 and win(k) == win(q):
                m[k, q] = 1.0
        return m
    c["cm1"] = shift(-1, lambda x: x // 8)
    c["cp1"] = shift(+1, lambda x: x // 8)
    c["cm2"] = shift(-16, lambda x: x % 16)
    c["cp2"] = shift(+16, lambda x: x % 16)
    return c


def _host_wq_wk(qkv_w):
    wq = np.zeros((2, 128, 40), np.float32)
    wk = np.zeros((2, 128, 40), np.float32)
    for par in range(2):
        for b in range(4):
            r0 = 32 * b + 16 * par
            wq[par, r0:r0 + 16, 0:8] = qkv_w[:, 0:8]
            wq[par, r0:r0 + 16, 32:40] = qkv_w[:, 8:16]
            wk[par, r0:r0 + 16, 0:8] = qkv_w[:, 16:24]
            wk[par, r0:r0 + 16, 32:40] = qkv_w[:, 24:32]
    return wq, wk


def _host_wb_skeleton():
    wb = np.zeros((2, 128, 128), np.float32)
    for par in range(2):
        for b in range(4):
            r0 = 32 * b + 16 * par
            wb[par, r0:r0 + 16, 48:64] = np.eye(16, dtype=np.float32)
    return wb


def build_program():
    nc = bass.Bass("TRN2", target_bir_lowering=False, debug=False)

    def din(name, shape):
        return nc.dram_tensor(name, list(shape), F32, kind="ExternalInput").ap()

    xatt = din("xatt", [ROWS, DM])
    xout = din("xout", [ROWS, DM])
    wgate = din("wgate", [MC, 128, KC, 128])
    wup = din("wup", [MC, 128, KC, 128])
    wdown = din("wdown", [MC, 128, KC, 128])
    gsc = din("gsc", [128, MC])
    usc = din("usc", [128, MC])
    dsc = din("dsc", [128, KC])
    wq_h = din("wq_h", [128, 2, 40])
    wk_h = din("wk_h", [128, 2, 40])
    wb_h = din("wb_h", [128, 2, 128])
    qkvw_h = din("qkvw_h", [16, 48])
    projw1_h = din("projw1_h", [8, 16])
    projw2_h = din("projw2_h", [8, 16])
    projb_h = din("projb_h", [1, 16])
    convw0_h = din("convw0_h", [8, 9])
    convb0_h = din("convb0_h", [8, 1])
    convw1_h = din("convw1_h", [8, 9])
    convb1_h = din("convb1_h", [8, 1])
    lnw_h = din("lnw_h", [128, 16])
    lnb_h = din("lnb_h", [128, 16])
    ones_h = din("ones_h", [1, 128])
    ident_h = din("ident_h", [128, 128])
    mask1_h = din("mask1_h", [128, 128])
    mask2_h = din("mask2_h", [128, 128])
    cm1_h = din("cm1_h", [128, 128])
    cp1_h = din("cp1_h", [128, 128])
    cm2_h = din("cm2_h", [128, 128])
    cp2_h = din("cp2_h", [128, 128])

    o = nc.dram_tensor("o", [ROWS, DM], F32, kind="ExternalOutput").ap()
    s_dram = nc.dram_tensor("s_dram", [16, NLOC], F32, kind="Internal").ap()
    wt_dram = nc.dram_tensor("wt_dram", [16, 128], F32, kind="Internal").ap()

    ctx = []

    def sb(name, shape, dtype=F32):
        g = nc.sbuf_tensor(name, list(shape), dtype)
        t = g.__enter__()
        ctx.append(g)
        return t

    def psalloc(name):
        g = nc.psum_tensor(name, [128, 512], F32)
        t = g.__enter__()
        ctx.append(g)
        return t

    ident = sb("ident", [128, 128])
    mask1 = sb("mask1", [128, 128])
    mask2 = sb("mask2", [128, 128])
    cm1 = sb("cm1", [128, 128])
    cp1 = sb("cp1", [128, 128])
    cm2 = sb("cm2", [128, 128])
    cp2 = sb("cp2", [128, 128])
    wq_s = sb("wq_s", [128, 2, 40])
    wk_s = sb("wk_s", [128, 2, 40])
    wb_s = sb("wb_s", [128, 2, 128])
    qkvw_s = sb("qkvw_s", [16, 48])
    projw1_s = sb("projw1_s", [8, 16])
    projw2_s = sb("projw2_s", [8, 16])
    projb_s = sb("projb_s", [1, 16])
    cw0_s = sb("cw0_s", [8, 9])
    cb0_s = sb("cb0_s", [8, 1])
    cw1_s = sb("cw1_s", [8, 9])
    cb1_s = sb("cb1_s", [8, 1])
    ones1 = sb("ones1", [1, 128])
    beff = sb("beff", [128, 16])
    gamma = sb("gamma", [128, 16])
    beta = sb("beta", [128, 16])
    gsc_s = sb("gsc_s", [128, MC])
    usc_s = sb("usc_s", [128, MC])
    dsc_s = sb("dsc_s", [128, KC])
    wvt = sb("wvt", [8, 16])
    winit = sb("winit", [16, 128])
    smallv = sb("smallv", [128, 64])

    arA = sb("arA", [128, 2048])
    arB = sb("arB", [128, 12288])
    arC = sb("arC", [128, 4096])
    arD = sb("arD", [128, 4096])
    arE = sb("arE", [128, 4096])
    arF = sb("arF", [128, 4096])
    xsr = sb("xsr", [128, 8192], F32R)
    hr = sb("hr", [128, 2048], F32R)
    wr = sb("wr", [128, 8192], F32R)

    xnfm = arA[:, :].rearrange("p (k r) -> p k r", k=KC)
    qs = arB[0:48, 0:4096]
    ks = arB[0:48, 4096:8192]
    vs = arB[:, 8192:12288]
    y2cm = arC[0:16, 0:4096]
    tocm = arC[0:16, 0:4096]
    v1tm = arD[:, 0:264].rearrange("p (t c) -> p t c", c=66)
    v2tm = arD[:, 264:528].rearrange("p (t c) -> p t c", c=66)
    yslab = arD[:, 528:1808].rearrange("p (t c) -> p t c", c=YW)
    tokr = arD[:, 1808:2320].rearrange("p (t c) -> p t c", c=16)
    pbuf = arE[:, 0:512]
    ptm = arE[:, 512:1024]
    zslab = arE[:, 1024:1536].rearrange("p (t c) -> p t c", c=16)
    zsq = arE[:, 1536:2048].rearrange("p (t c) -> p t c", c=16)
    tokout = arE[:, 2048:2560].rearrange("p (t c) -> p t c", c=16)
    xa_tm = arF[:, 0:2048]
    sq_tm = arF[:, 2048:4096]

    xs_hi = xsr[:, 0:4096].rearrange("p (k r) -> p k r", k=KC)
    xs_lo = xsr[:, 4096:8192].rearrange("p (k r) -> p k r", k=KC)
    hhi = hr[:, 0:1024].rearrange("p (j r) -> p j r", j=4)
    hlo = hr[:, 1024:2048].rearrange("p (j r) -> p j r", j=4)
    hraw = [arB[:, 0:256], arB[:, 256:512]]
    hv1 = [arB[:, 512:768], arB[:, 768:1024]]
    dslab = arC[:, 0:4096].rearrange("p (k r) -> p k r", k=KC)
    xo_tm = arF[:, 0:2048]
    vhi_tm = arD[:, 0:2048]
    vlo_tm = arD[:, 2048:4096]
    x_tm = arD[:, 0:2048]
    wg_t = [wr[:, 0:2048].rearrange("p (k r) -> p k r", k=KC),
            wr[:, 2048:4096].rearrange("p (k r) -> p k r", k=KC)]
    wu_t = [wr[:, 4096:6144].rearrange("p (k r) -> p k r", k=KC),
            wr[:, 6144:8192].rearrange("p (k r) -> p k r", k=KC)]
    wd_t = wg_t
    g_tm = arF[:, 0:2048]
    s_tm = arF[:, 2048:4096]
    o_tm = s_tm

    zred = smallv[:, 0:32]
    negm = smallv[:, 32:40]
    rec4 = smallv[:, 60:64]
    mu_c = smallv[:, 41:42]
    rs_c = smallv[:, 42:43]
    y2z = smallv[:, 44:60]

    pA = psalloc("pA")
    pB = psalloc("pB")
    pC = psalloc("pC")
    pD = psalloc("pD")
    pE = psalloc("pE")
    pF = psalloc("pF")
    pG = psalloc("pG")
    pH = psalloc("pH")

    barg = nc.semaphore("bar")
    barsem = barg.__enter__()
    ctx.append(barg)
    gsems = []
    for gi in range(2):
        gg = nc.semaphore(f"gbar{gi}")
        gsems.append(gg.__enter__())
        ctx.append(gg)
    blkg = nc.Block()
    block = blkg.__enter__()

    phases = []
    phase_deps = []

    def phase(dep=None, **fns):
        phases.append(fns)
        phase_deps.append(len(phases) - 2 if dep is None else dep)
        return len(phases) - 1

    def dma_fn(n):
        def wrap(fn):
            fn.dma_incs = n
            return fn
        return wrap

    def sw_dma_fn(n, gi=0):
        def wrap(fn):
            fn.sw_dma_incs = n
            fn.sw_sem = gi
            return fn
        return wrap


    @dma_fn(24)
    def init_loads(e):
        for src, dst in (
            (ident_h, ident[:]), (mask1_h, mask1[:]), (mask2_h, mask2[:]),
            (cm1_h, cm1[:]), (cp1_h, cp1[:]), (cm2_h, cm2[:]), (cp2_h, cp2[:]),
            (wq_h, wq_s[:]), (wk_h, wk_s[:]), (wb_h, wb_s[:]),
            (qkvw_h, qkvw_s[:]), (projw1_h, projw1_s[:]),
            (projw2_h, projw2_s[:]), (projb_h, projb_s[:]),
            (convw0_h, cw0_s[:]), (convb0_h, cb0_s[:]),
            (convw1_h, cw1_s[:]), (convb1_h, cb1_s[:]),
            (lnw_h, gamma[:]), (lnb_h, beta[:]), (ones_h, ones1[:]),
            (gsc, gsc_s[:]), (usc, usc_s[:]), (dsc, dsc_s[:]),
        ):
            e.dma_start(out=dst, in_=src).then_inc(barsem, 16)
    phase(sync=init_loads)

    def init_memset(e):
        e.memset(winit[:], 0.0)
        e.memset(v1tm[:, :, 16:17], 1.0)
        e.memset(v2tm[:, :, 16:17], 1.0)
        e.memset(y2cm[:, :], 0.0)
    phase(gpsimd=init_memset)

    def init_scaleq(e):
        e.tensor_scalar_mul(wq_s[:, :, :], wq_s[:, :, :], SCALE)
    phase(vector=init_scaleq)

    def init_wvT(e):
        e.transpose(pA[0:8, 0:16], qkvw_s[0:16, 32:40], ident[0:16, 0:16])
        e.transpose(pA[0:8, 16:32], qkvw_s[0:16, 40:48], ident[0:16, 0:16])
    phase(tensor=init_wvT)

    def init_wvT_ev(e):
        e.tensor_copy(winit[0:8, 0:32], pA[0:8, 0:32])
    phase(vector=init_wvT_ev)

    jobs = [
        (0,   (0, 16),  None,            projw1_s, True, True),
        (16,  (0, 16),  cw0_s[:, 3:4],   projw1_s, True, True),
        (32,  (0, 16),  cw0_s[:, 5:6],   projw1_s, True, True),
        (64,  (16, 32), None,            projw2_s, True, True),
        (80,  (16, 32), cw1_s[:, 1:2],   projw2_s, True, True),
        (96,  (16, 32), cw1_s[:, 7:8],   projw2_s, True, True),
        (112, (0, 16),  cw0_s[:, 4:5],   projw1_s, True, False),
        (112, (16, 32), cw1_s[:, 4:5],   projw2_s, False, True),
    ]
    for col, (c0, c1), diag, pr, st, sp in jobs:
        def vj(e, c0=c0, c1=c1, diag=diag):
            if diag is None:
                e.tensor_copy(wvt[:], winit[0:8, c0:c1])
            else:
                e.tensor_scalar(wvt[:], winit[0:8, c0:c1], diag, None, ALU.mult)
        phase(vector=vj)

        def tj(e, col=col, pr=pr, st=st, sp=sp):
            e.matmul(pB[0:16, col:col + 16], wvt[:], pr[:],
                     start=st, stop=sp)
        phase(tensor=tj)

    def init_beff_mm(e):
        e.matmul(pC[0:1, 0:16], cb0_s[:], projw1_s[:], start=True, stop=False)
        e.matmul(pC[0:1, 0:16], cb1_s[:], projw2_s[:], start=False, stop=True)
    phase(tensor=init_beff_mm)

    def init_beff_add(e):
        e.tensor_add(winit[0:1, 48:64], pC[0:1, 0:16], projb_s[:])
    phase(vector=init_beff_add)

    def init_beff_bc(e):
        e.matmul(pD[0:128, 0:16], ones1[:], winit[0:1, 48:64],
                 start=True, stop=True)
    phase(tensor=init_beff_bc)

    def init_beff_ev(e):
        e.tensor_copy(beff[:], pD[0:128, 0:16])
    phase(vector=init_beff_ev)

    def init_vt_evac(e):
        e.tensor_copy(winit[:, 0:48], pB[0:16, 0:48])
        e.tensor_copy(winit[:, 64:128], pB[0:16, 64:128])
    phase(vector=init_vt_evac)

    @dma_fn(1)
    def init_vt_out(e):
        e.dma_start(out=wt_dram, in_=winit[:]).then_inc(barsem, 16)
    phase(sync=init_vt_out)

    @dma_fn(16)
    def init_vt_in(e):
        for par in range(2):
            for b in range(4):
                r0 = 32 * b + 16 * par
                e.dma_start(out=wb_s[r0:r0 + 16, par, 0:48],
                            in_=wt_dram[:, 0:48]).then_inc(barsem, 16)
                e.dma_start(out=wb_s[r0:r0 + 16, par, 64:128],
                            in_=wt_dram[:, 64:128]).then_inc(barsem, 16)
    phase(sync=init_vt_in)

    def ln_tile(src_dram, dst_tm, tile):
        @dma_fn(1)
        def load(e, tile=tile):
            e.dma_start(out=dst_tm, in_=src_dram[tile * 128:(tile + 1) * 128, :]
                        ).then_inc(barsem, 16)
        phase(sync=load)

        def mean(e):
            e.reduce_sum(mu_c[:], dst_tm, axis=AX.X, negate=True)
            e.drain()
            e.tensor_scalar_mul(mu_c[:], mu_c[:], 1.0 / DM)
            e.drain()
            e.tensor_scalar(dst_tm, dst_tm, mu_c[:], None, ALU.add)
        phase(vector=mean)

        def sq(e):
            e.activation(sq_tm[:], dst_tm, AF.Square)
        phase(scalar=sq)

        def var(e):
            e.reduce_sum(rs_c[:], sq_tm[:], axis=AX.X)
            e.drain()
            e.tensor_scalar(rs_c[:], rs_c[:], 1.0 / DM, EPS, ALU.mult, ALU.add)
        phase(vector=var)

        def rsq(e):
            e.activation(rs_c[:], rs_c[:], AF.Ln)
            e.drain()
            e.activation(rs_c[:], rs_c[:], AF.Exp, scale=-0.5)
        phase(scalar=rsq)

        def fin(e):
            e.tensor_scalar(dst_tm, dst_tm, rs_c[:], None, ALU.mult)
        phase(vector=fin)

    qps = (pA, pB, pC, pD)
    kps = (pE, pF, pG, pH)
    for m in range(GROUPS):
        if m % 4 == 0:
            tile8 = m // 4
            ln_tile(xatt, xa_tm[:, :], tile8)
            for jh in range(4):
                def tp(e, jh=jh):
                    for j2 in range(4):
                        j = jh * 4 + j2
                        e.transpose(pA[:, j2 * 128:(j2 + 1) * 128],
                                    xa_tm[:, j * 128:(j + 1) * 128], ident[:])
                phase(tensor=tp)

                def tpe(e, jh=jh):
                    for j2 in range(4):
                        j = jh * 4 + j2
                        e.tensor_copy(xnfm[:, j, :],
                                      pA[:, j2 * 128:(j2 + 1) * 128])
                phase(vector=tpe)

        def rhs_ap(blkb, m=m):
            mm4 = m % 4
            return xnfm[32 * blkb:32 * blkb + 32, :, mm4 * 32:mm4 * 32 + 32]

        for par in range(2):
            def qkv_mm(e, par=par, rhs_ap=rhs_ap):
                for blkb in range(4):
                    tp = (32 * blkb, 0)
                    r = rhs_ap(blkb)
                    e.matmul(qps[blkb][0:40, :], wq_s[32 * blkb:32 * blkb + 32, par, :],
                             r, start=True, stop=True, tile_position=tp)
                    e.matmul(kps[blkb][0:40, :], wk_s[32 * blkb:32 * blkb + 32, par, :],
                             r, start=True, stop=True, tile_position=tp)
            phase(tensor=qkv_mm)

            def qkv_ev(e, par=par):
                for blkb in range(4):
                    sp = 2 * blkb + par
                    for slab, psl in ((qs, qps[blkb]), (ks, kps[blkb])):
                        d1 = slab[0:8, :].rearrange(
                            "p (i b s) -> p i b s", i=32, b=16)[
                            :, :, :, sp].transpose([0, 2, 1])
                        e.tensor_copy(d1, psl[0:8, :].rearrange(
                            "p (b i) -> p b i", b=16))
                        d2 = slab[32:40, :].rearrange(
                            "p (b s r g) -> p b s r g", b=16, s=8, r=4)[
                            :, :, sp, :, :].transpose([0, 1, 3, 2])
                        e.tensor_copy(d2, psl[32:40, :].rearrange(
                            "p (b g r) -> p b g r", b=16, g=8))
            phase(vector=qkv_ev)

        for par in range(2):
            def qkvB_mm(e, par=par, rhs_ap=rhs_ap):
                for blkb in range(4):
                    e.matmul(qps[blkb][:, :], wb_s[32 * blkb:32 * blkb + 32, par, :],
                             rhs_ap(blkb), start=True, stop=True,
                             tile_position=(32 * blkb, 0))
            phase(tensor=qkvB_mm)

            def qkvB_ev(e, par=par):
                for blkb in range(4):
                    sp = 2 * blkb + par
                    d1 = vs[0:64, :].rearrange(
                        "p (i b s) -> p i b s", i=32, b=16)[
                        :, :, :, sp].transpose([0, 2, 1])
                    e.tensor_copy(d1, qps[blkb][0:64, :].rearrange(
                        "p (b i) -> p b i", b=16))
                    d2 = vs[64:128, :].rearrange(
                        "p (b s r g) -> p b s r g", b=16, s=8, r=4)[
                        :, :, sp, :, :].transpose([0, 1, 3, 2])
                    e.tensor_copy(d2, qps[blkb][64:128, :].rearrange(
                        "p (b g r) -> p b g r", b=16, g=8))
            phase(vector=qkvB_ev)

        for st in range(8):
            tl = [4 * st + i for i in range(4)]

            def mmS2(e, tl=tl):
                for i, t2 in enumerate(tl):
                    e.matmul(pB[:, i * 128:(i + 1) * 128],
                             qs[32:40, t2 * 128:(t2 + 1) * 128],
                             ks[32:40, t2 * 128:(t2 + 1) * 128],
                             start=True, stop=True, tile_position=(32, 0))
            phase(tensor=mmS2)

            def negmax2(e):
                e.reduce_max(negm[:, 4:8],
                             pB[:, :].rearrange("p (t x) -> p t x", t=4),
                             axis=AX.X, negate=True)
            phase(vector=negmax2)

            def expS2(e):
                for i in range(4):
                    e.activation(pbuf[:, i * 128:(i + 1) * 128],
                                 pB[:, i * 128:(i + 1) * 128], AF.Exp,
                                 bias=negm[:, 4 + i:5 + i])
            phase(scalar=expS2)

            def ptrans2(e, tl=tl):
                for i in range(4):
                    e.transpose(pD[:, i * 128:(i + 1) * 128],
                                pbuf[:, i * 128:(i + 1) * 128],
                                ident[:])
                for i, t2 in enumerate(tl):
                    e.transpose(pF[:, i * 128:i * 128 + 64],
                                vs[64:128, t2 * 128:(t2 + 1) * 128],
                                ident[64:128, 64:128])
            phase(tensor=ptrans2)

            def pt_evac2(e, tl=tl):
                e.tensor_mul(ptm[:, 0:512].rearrange("p (t x) -> p t x", t=4),
                             pD[:, :].rearrange("p (t x) -> p t x", t=4),
                             mask1[:].unsqueeze(1).broadcast_to(
                                 [128, 4, 128]))
                for i in range(4):
                    e.tensor_copy(v2tm[:, i, 0:16], pF[:, i * 128:i * 128 + 16])
                    e.tensor_copy(v2tm[:, i, 17:65],
                                  pF[:, i * 128 + 16:i * 128 + 64])
            phase(vector=pt_evac2)

            def av2(e, tl=tl):
                for i in range(4):
                    lhs = ptm[:, i * 128:(i + 1) * 128]
                    e.matmul(pG[:, i * YW:i * YW + 17], lhs, v2tm[:, i, 0:17],
                             start=True, stop=True)
                    e.matmul(pG[:, i * YW + 20:i * YW + 36], cm1[:],
                             v2tm[:, i, 17:33], start=True, stop=False)
                    e.matmul(pG[:, i * YW + 20:i * YW + 36], cp1[:],
                             v2tm[:, i, 33:49], start=False, stop=False)
                    e.matmul(pG[:, i * YW + 20:i * YW + 36], ident[:],
                             v2tm[:, i, 49:65], start=False, stop=True)
            phase(tensor=av2)

            def recs2(e):
                dv = pG[:, 0:4 * YW].rearrange("p (t c) -> p t c", c=YW)[
                    :, :, 16]
                e.activation(rec4[:], dv, AF.Ln)
                e.drain()
                e.activation(rec4[:], rec4[:], AF.Exp, scale=-1.0)
            phase(scalar=recs2)

            for i in range(4):
                def y2div(e, i=i):
                    e.tensor_scalar(y2z[:], pG[:, i * YW:i * YW + 16],
                                    rec4[:, i:i + 1], None, ALU.mult)
                    e.drain()
                    e.tensor_add(y2z[:], y2z[:],
                                 pG[:, i * YW + 20:i * YW + 36])
                phase(vector=y2div)

                def y2t(e, i=i):
                    e.transpose(pH[0:16, i * 128:(i + 1) * 128], y2z[:], ident[:])
                phase(tensor=y2t)

            def y2ev(e, tl=tl):
                for i, t2 in enumerate(tl):
                    b0, h = t2 // 2, t2 % 2
                    dst = y2cm[:, :].rearrange(
                        "p (g r b s) -> p g r b s", g=8, r=4, b=16)[
                        :, :, :, b0, 4 * h:4 * h + 4].transpose([0, 3, 2, 1])
                    e.tensor_copy(dst, pH[0:16, i * 128:(i + 1) * 128].rearrange(
                        "p (s r g) -> p s r g", s=4, r=4))
            phase(vector=y2ev)

        for st in range(8):
            tl = [4 * st + i for i in range(4)]

            def mmS1(e, tl=tl):
                for i, ip in enumerate(tl):
                    e.matmul(pA[:, i * 128:(i + 1) * 128],
                             qs[0:8, ip * 128:(ip + 1) * 128],
                             ks[0:8, ip * 128:(ip + 1) * 128],
                             start=True, stop=True)
            phase(tensor=mmS1)

            def negmax1(e):
                e.reduce_max(negm[:, 0:4],
                             pA[:, :].rearrange("p (t x) -> p t x", t=4),
                             axis=AX.X, negate=True)
            phase(vector=negmax1)

            def expS1(e):
                for i in range(4):
                    e.activation(pbuf[:, i * 128:(i + 1) * 128],
                                 pA[:, i * 128:(i + 1) * 128], AF.Exp,
                                 bias=negm[:, i:i + 1])
            phase(scalar=expS1)

            def ptrans1(e, tl=tl):
                for i in range(4):
                    e.transpose(pC[:, i * 128:(i + 1) * 128],
                                pbuf[:, i * 128:(i + 1) * 128], ident[:])
                for i, ip in enumerate(tl):
                    e.transpose(pE[:, i * 128:i * 128 + 64],
                                vs[0:64, ip * 128:(ip + 1) * 128],
                                ident[0:64, 0:64])
            phase(tensor=ptrans1)

            def pt_evac1(e, tl=tl):
                e.tensor_mul(ptm[:, 0:512].rearrange("p (t x) -> p t x", t=4),
                             pC[:, :].rearrange("p (t x) -> p t x", t=4),
                             mask1[:].unsqueeze(1).broadcast_to(
                                 [128, 4, 128]))
                for i, ip in enumerate(tl):
                    e.tensor_add(v1tm[:, i, 0:16],
                                 pE[:, i * 128:i * 128 + 16], beff[:])
                    e.tensor_copy(v1tm[:, i, 17:49],
                                  pE[:, i * 128 + 16:i * 128 + 48])
                    e.tensor_copy(tokr[:, ip, :],
                                  pE[:, i * 128 + 48:i * 128 + 64])
            phase(vector=pt_evac1)

            def av1(e, tl=tl):
                for i, ip in enumerate(tl):
                    lhs = ptm[:, i * 128:(i + 1) * 128]
                    e.matmul(pC[:, i * YW:i * YW + 17], lhs, v1tm[:, i, 0:17],
                             start=True, stop=True)
                    e.matmul(pC[:, i * YW + 20:i * YW + 36], cm1[:],
                             v1tm[:, i, 17:33], start=True, stop=False)
                    e.matmul(pC[:, i * YW + 20:i * YW + 36], cp1[:],
                             v1tm[:, i, 33:49], start=False, stop=False)
                    e.matmul(pC[:, i * YW + 20:i * YW + 36],
                             y2cm[:, ip * 128:(ip + 1) * 128],
                             ident[0:16, 0:16], start=False, stop=True,
                             is_transpose=True)
            phase(tensor=av1)

            def y1ev(e, tl=tl):
                for i, ip in enumerate(tl):
                    e.copy(yslab[:, ip, :], pC[:, i * YW:(i + 1) * YW])
            phase(scalar=y1ev)

        def ln16r(e):
            e.activation(zred[:], yslab[:, :, 16], AF.Ln)
            e.drain()
            e.activation(zred[:], zred[:], AF.Exp, scale=-1.0)
        phase(scalar=ln16r)

        def ln16a(e):
            e.tensor_mul(zslab[:, :, :], yslab[:, :, 0:16],
                         zred[:].unsqueeze(2).broadcast_to(
                             [128, 32, 16]))
            e.drain()
            e.tensor_add(zslab[:, :, :], zslab[:, :, :], yslab[:, :, 20:36])
            e.drain()
            e.reduce_sum(zred[:], zslab[:, :, :], axis=AX.X, negate=True)
            e.drain()
            e.tensor_scalar_mul(zred[:], zred[:], 1.0 / 16.0)
            e.drain()
            e.tensor_add(zslab[:, :, :], zslab[:, :, :],
                         zred[:].unsqueeze(2).broadcast_to(
                             [128, 32, 16]))
        phase(vector=ln16a)

        def ln16sq(e):
            e.activation(zsq[:, :, :], zslab[:, :, :], AF.Square)
        phase(scalar=ln16sq)

        def ln16b(e):
            e.reduce_sum(zred[:], zsq[:, :, :], axis=AX.X)
            e.drain()
            e.tensor_scalar(zred[:], zred[:], 1.0 / 16.0, EPS, ALU.mult, ALU.add)
        phase(vector=ln16b)

        def ln16c(e):
            e.activation(zred[:], zred[:], AF.Ln)
            e.drain()
            e.activation(zred[:], zred[:], AF.Exp, scale=-0.5)
        phase(scalar=ln16c)

        def ln16d(e):
            e.tensor_mul(zslab[:, :, :], zslab[:, :, :],
                         zred[:].unsqueeze(2).broadcast_to(
                             [128, 32, 16]))
            e.drain()
            e.tensor_mul(zslab[:, :, :], zslab[:, :, :],
                         gamma[:].unsqueeze(1).broadcast_to(
                             [128, 32, 16]))
            e.drain()
            e.tensor_add(zslab[:, :, :], zslab[:, :, :],
                         beta[:].unsqueeze(1).broadcast_to(
                             [128, 32, 16]))
            e.drain()
            e.tensor_add(tokout[:, :, :], zslab[:, :, :], tokr[:, :, :])
        phase(vector=ln16d)

        for q in range(4):
            def toT(e, q=q):
                for j in range(4):
                    e.transpose(pA[0:16, j * 128:(j + 1) * 128],
                                tokout[:, q * 8 + j, :], ident[:])
                for j in range(4):
                    e.transpose(pB[0:16, j * 128:(j + 1) * 128],
                                tokout[:, q * 8 + 4 + j, :], ident[:])
            phase(tensor=toT)

            def toEv(e, q=q):
                e.tensor_copy(tocm[:, q * 1024:q * 1024 + 512], pA[0:16, :])
                e.tensor_copy(tocm[:, q * 1024 + 512:(q + 1) * 1024], pB[0:16, :])
            phase(vector=toEv)

        @dma_fn(1)
        def scat(e, m=m):
            e.dma_start(out=s_dram[:, m * GMICRO:(m + 1) * GMICRO],
                        in_=tocm[:, :]).then_inc(barsem, 16)
        phase(sync=scat)


    allp = (pA, pB, pC, pD, pE, pF, pG, pH)
    for tq in range(4):
        for tile in range(2):
            ln_tile(xout, xo_tm, tq * 2 + tile)

            def velt(e):
                e.tensor_scalar_mul(sq_tm[:], xo_tm[:], VC)
                e.drain()
                e.tensor_sub(vlo_tm[:], sq_tm[:], xo_tm[:])
                e.drain()
                e.tensor_sub(vhi_tm[:], sq_tm[:], vlo_tm[:])
                e.drain()
                e.tensor_sub(vlo_tm[:], xo_tm[:], vhi_tm[:])
            phase(vector=velt)

            for jh in range(4):
                def ftp(e, jh=jh):
                    for j2 in range(4):
                        j = jh * 4 + j2
                        e.transpose(pA[:, j2 * 128:(j2 + 1) * 128],
                                    vhi_tm[:, j * 128:(j + 1) * 128], ident[:])
                        e.transpose(pB[:, j2 * 128:(j2 + 1) * 128],
                                    vlo_tm[:, j * 128:(j + 1) * 128], ident[:])
                phase(tensor=ftp)

                def ftpe(e, tile=tile, jh=jh):
                    for j2 in range(4):
                        j = jh * 4 + j2
                        e.tensor_copy(xs_hi[:, j, tile * 128:(tile + 1) * 128],
                                      pA[:, j2 * 128:(j2 + 1) * 128])
                        e.tensor_copy(xs_lo[:, j, tile * 128:(tile + 1) * 128],
                                      pB[:, j2 * 128:(j2 + 1) * 128])
                phase(vector=ftpe)

        gu_banks = ((pA, pB), (pC, pD))
        ph_wload = {}
        ph_gu = {}
        ph_hmake = {-1: None, -2: None}
        ph_dacc_last = None
        ph_dmm_last = None
        for jb in range(11):
            for jj in range(4):
                j = jb * 4 + jj

                @sw_dma_fn(2, j % 2)
                def wload(e, j=j):
                    e.dma_start(out=wg_t[j % 2][:, :, :], in_=wgate[j, :, :, :]
                                ).then_inc(gsems[j % 2], 16)
                    e.dma_start(out=wu_t[j % 2][:, :, :], in_=wup[j, :, :, :]
                                ).then_inc(gsems[j % 2], 16)
                dep = ph_gu.get(j - 2)
                if dep is not None and ph_dmm_last is not None:
                    dep = max(dep, ph_dmm_last)
                ph_wload[j] = phase(dep=dep, gpsimd=wload)

                def gu_mm(e, j=j):
                    bg, bu = gu_banks[j % 2]
                    for k in range(KC):
                        e.matmul(bg[:, 0:256], wg_t[j % 2][:, k, :],
                                 xs_hi[:, k, :], start=(k == 0), stop=False)
                        e.matmul(bg[:, 0:256], wg_t[j % 2][:, k, :],
                                 xs_lo[:, k, :], start=False, stop=(k == KC - 1))
                    for k in range(KC):
                        e.matmul(bu[:, 0:256], wu_t[j % 2][:, k, :],
                                 xs_hi[:, k, :], start=(k == 0), stop=False)
                        e.matmul(bu[:, 0:256], wu_t[j % 2][:, k, :],
                                 xs_lo[:, k, :], start=False, stop=(k == KC - 1))
                gdep = ph_wload[j]
                if ph_hmake.get(j - 2) is not None:
                    gdep = max(gdep, ph_hmake[j - 2])
                if ph_dacc_last is not None:
                    gdep = max(gdep, ph_dacc_last)
                ph_gu[j] = phase(dep=gdep, tensor=gu_mm)

                def gsilu(e, j=j):
                    bg, bu = gu_banks[j % 2]
                    e.activation(hv1[j % 2][:], bg[:, 0:256], AF.Silu,
                                 scale=gsc_s[:, j:j + 1])
                phase(dep=ph_gu[j], scalar=gsilu)

                def hmake(e, j=j, jj=jj):
                    bg, bu = gu_banks[j % 2]
                    hw = hraw[j % 2]
                    e.scalar_tensor_tensor(hw[:], bu[:, 0:256],
                                           usc_s[:, j:j + 1], hv1[j % 2][:],
                                           ALU.mult, ALU.mult)
                    e.drain()
                    e.tensor_scalar_mul(sq_tm[:, 0:256], hw[:], VC)
                    e.drain()
                    e.tensor_sub(sq_tm[:, 256:512], sq_tm[:, 0:256], hw[:])
                    e.drain()
                    e.tensor_sub(hhi[:, jj, :], sq_tm[:, 0:256],
                                 sq_tm[:, 256:512])
                    e.drain()
                    e.tensor_sub(hlo[:, jj, :], hw[:], hhi[:, jj, :])
                ph_hmake[j] = phase(vector=hmake)

            ph_dload = {}
            ph_dmm = {}
            for jj in range(4):
                j = jb * 4 + jj

                @sw_dma_fn(1, j % 2)
                def dload(e, j=j):
                    e.dma_start(out=wd_t[j % 2][:, :, :], in_=wdown[j, :, :, :]
                                ).then_inc(gsems[j % 2], 16)
                ddep = ph_dmm.get(jj - 2)
                ph_dload[jj] = phase(dep=ddep, gpsimd=dload)

                def d_mm(e, jj=jj, j=j):
                    for oc in range(KC):
                        pp = allp[oc % 8]
                        c0 = (oc // 8) * 256
                        e.matmul(pp[:, c0:c0 + 256], wd_t[j % 2][:, oc, :],
                                 hhi[:, jj, :],
                                 start=(jj == 0 and oc < 8), stop=False)
                        e.matmul(pp[:, c0:c0 + 256], wd_t[j % 2][:, oc, :],
                                 hlo[:, jj, :], start=False,
                                 stop=(jj == 3 and oc >= 8))
                ph_dmm[jj] = phase(dep=ph_dload[jj], tensor=d_mm)
                ph_dmm_last = ph_dmm[jj]

            def dacc(e, jb=jb):
                for oc in range(KC):
                    pp = allp[oc % 8]
                    c0 = (oc // 8) * 256
                    if jb == 0:
                        e.tensor_copy(dslab[:, oc, :], pp[:, c0:c0 + 256])
                    else:
                        e.tensor_add(dslab[:, oc, :], dslab[:, oc, :],
                                     pp[:, c0:c0 + 256])
            ph_dacc_last = phase(dep=ph_dmm[3], vector=dacc)

        def gfin(e):
            for oc in range(KC):
                e.activation(dslab[:, oc, :], dslab[:, oc, :], AF.Silu,
                             scale=dsc_s[:, oc:oc + 1])
        phase(scalar=gfin)

        for tb in range(2):
            row0 = tq * 256 + tb * 128

            def gT(e, tb=tb):
                for oc in range(4):
                    e.transpose(pA[:, oc * 128:(oc + 1) * 128],
                                dslab[:, oc, tb * 128:(tb + 1) * 128], ident[:])
                for oc in range(4):
                    e.transpose(pB[:, oc * 128:(oc + 1) * 128],
                                dslab[:, 4 + oc, tb * 128:(tb + 1) * 128],
                                ident[:])
                for oc in range(4):
                    e.transpose(pC[:, oc * 128:(oc + 1) * 128],
                                dslab[:, 8 + oc, tb * 128:(tb + 1) * 128],
                                ident[:])
                for oc in range(4):
                    e.transpose(pD[:, oc * 128:(oc + 1) * 128],
                                dslab[:, 12 + oc, tb * 128:(tb + 1) * 128],
                                ident[:])
            phase(tensor=gT)

            def gEv(e):
                e.tensor_copy(g_tm[:, 0:512], pA[:, :])
                e.tensor_copy(g_tm[:, 512:1024], pB[:, :])
                e.tensor_copy(g_tm[:, 1024:1536], pC[:, :])
                e.tensor_copy(g_tm[:, 1536:2048], pD[:, :])
            phase(vector=gEv)

            @dma_fn(2)
            def fload(e, row0=row0):
                e.dma_start(out=x_tm[:], in_=xout[row0:row0 + 128, :]
                            ).then_inc(barsem, 16)
                ta = row0 // 64
                e.dma_start(
                    out=s_tm[:],
                    in_=s_dram.rearrange("c (r j) -> c r j", j=DM)[
                        ta:ta + 2, 0:64, :]).then_inc(barsem, 16)
            phase(sync=fload)

            def fuse(e):
                e.tensor_mul(o_tm[:], s_tm[:], g_tm[:])
                e.drain()
                e.tensor_add(o_tm[:], o_tm[:], x_tm[:])
            phase(vector=fuse)

            @dma_fn(1)
            def fstore(e, row0=row0):
                e.dma_start(out=o[row0:row0 + 128, :], in_=o_tm[:]
                            ).then_inc(barsem, 16)
            phase(sync=fstore)

    if _PHASE_LIMIT is not None:
        del phases[_PHASE_LIMIT:]
    allsems = [barsem] + gsems
    cum = []
    tot = [0, 0, 0]
    for p in phases:
        for fn in p.values():
            d = getattr(fn, "dma_incs", 0)
            g = getattr(fn, "sw_dma_incs", 0)
            if g:
                tot[1 + getattr(fn, "sw_sem", 0)] += 16 * g
            elif d:
                tot[0] += 16 * d
            else:
                tot[0] += 1
        cum.append(tuple(tot))
    finals = tuple(tot)

    for ename in ("sync", "gpsimd", "tensor", "vector", "scalar"):
        dec = getattr(block, ename)

        def mk(ename=ename):
            def body(eng):
                lastw = [0, 0, 0]
                for pi, p in enumerate(phases):
                    if ename not in p:
                        continue
                    dep = phase_deps[pi]
                    if dep is not None and dep >= len(phases):
                        dep = len(phases) - 1
                    tgt = cum[dep] if (dep is not None and dep >= 0) \
                        else (0, 0, 0)
                    for si in range(3):
                        if tgt[si] > lastw[si] and tgt[si] > 0:
                            eng.wait_ge(allsems[si], tgt[si])
                            lastw[si] = tgt[si]
                    fn = p[ename]
                    fn(eng)
                    if (getattr(fn, "dma_incs", 0) == 0
                            and getattr(fn, "sw_dma_incs", 0) == 0):
                        eng.drain().then_inc(barsem, 1)
                if ename == "sync":
                    for si in range(3):
                        if finals[si] > 0:
                            eng.wait_ge(allsems[si], finals[si])
            return body
        dec(mk())

    blkg.__exit__(None, None, None)
    for g in reversed(ctx):
        g.__exit__(None, None, None)
    return nc


_PHASE_LIMIT = None
_PROG = None


def _get_prog():
    global _PROG
    if _PROG is None:
        _PROG = build_program()
    return _PROG


def kernel(L0_attn_x, L0_attn_qkv_w, L0_attn_conv_w0, L0_attn_conv_b0,
           L0_attn_conv_w1, L0_attn_conv_b1, L0_attn_proj_w, L0_attn_proj_b,
           L0_attn_ln_w, L0_attn_ln_b, L0_ffn_gate_w, L0_ffn_up_w,
           L0_ffn_down_w, L0_ffn_gate_s, L0_ffn_up_s, L0_ffn_down_s):
    f32 = np.float32
    X = np.ascontiguousarray(np.asarray(L0_attn_x, dtype=f32).reshape(8192, DM))
    qkv_w = np.asarray(L0_attn_qkv_w, dtype=f32)
    consts = _host_consts()
    wq, wk = _host_wq_wk(qkv_w)
    wb = _host_wb_skeleton()

    gate_w = np.asarray(L0_ffn_gate_w, dtype=f32)
    up_w = np.asarray(L0_ffn_up_w, dtype=f32)
    down_w = np.asarray(L0_ffn_down_w, dtype=f32)
    WG = np.ascontiguousarray(
        gate_w.reshape(MC, 128, KC, 128).transpose(0, 3, 2, 1))
    WU = np.ascontiguousarray(
        up_w.reshape(MC, 128, KC, 128).transpose(0, 3, 2, 1))
    WD = np.ascontiguousarray(
        down_w.reshape(KC, 128, MC, 128).transpose(2, 3, 0, 1))

    gsch = np.ascontiguousarray(
        np.asarray(L0_ffn_gate_s, dtype=f32).reshape(MC, 128).T)
    usch = np.ascontiguousarray(
        np.asarray(L0_ffn_up_s, dtype=f32).reshape(MC, 128).T)
    dsch = np.ascontiguousarray(
        np.asarray(L0_ffn_down_s, dtype=f32).reshape(KC, 128).T)

    shared = {
        "wgate": WG, "wup": WU, "wdown": WD,
        "gsc": gsch, "usc": usch, "dsc": dsch,
        "wq_h": np.ascontiguousarray(wq.transpose(1, 0, 2)),
        "wk_h": np.ascontiguousarray(wk.transpose(1, 0, 2)),
        "wb_h": np.ascontiguousarray(wb.transpose(1, 0, 2)),
        "qkvw_h": qkv_w,
        "projw1_h": np.ascontiguousarray(np.asarray(L0_attn_proj_w, f32)[0:8]),
        "projw2_h": np.ascontiguousarray(np.asarray(L0_attn_proj_w, f32)[8:16]),
        "projb_h": np.asarray(L0_attn_proj_b, f32).reshape(1, 16),
        "convw0_h": np.asarray(L0_attn_conv_w0, f32).reshape(8, 9),
        "convb0_h": np.asarray(L0_attn_conv_b0, f32).reshape(8, 1),
        "convw1_h": np.asarray(L0_attn_conv_w1, f32).reshape(8, 9),
        "convb1_h": np.asarray(L0_attn_conv_b1, f32).reshape(8, 1),
        "lnw_h": np.ascontiguousarray(
            np.broadcast_to(np.asarray(L0_attn_ln_w, f32).reshape(1, 16),
                            (128, 16))),
        "lnb_h": np.ascontiguousarray(
            np.broadcast_to(np.asarray(L0_attn_ln_b, f32).reshape(1, 16),
                            (128, 16))),
        "ones_h": np.ones((1, 128), f32),
        "ident_h": consts["ident"], "mask1_h": consts["mask1"],
        "mask2_h": consts["mask2"], "cm1_h": consts["cm1"],
        "cp1_h": consts["cp1"], "cm2_h": consts["cm2"], "cp2_h": consts["cp2"],
    }

    in_maps = []
    rowsets = []
    for k in range(NCORES):
        rows = np.arange(8192)
        sel = (rows % 512 >= 64 * k) & (rows % 512 < 64 * (k + 1))
        rout = rows[sel]
        rowsets.append(rout)
        m = dict(shared)
        m["xatt"] = np.ascontiguousarray(X[1024 * k:1024 * (k + 1)])
        m["xout"] = np.ascontiguousarray(X[rout])
        in_maps.append(m)

    nc = _get_prog()
    res = run_bass_kernel_spmd(nc, in_maps, core_ids=list(range(NCORES)))

    out = np.empty((8192, DM), dtype=f32)
    for k in range(NCORES):
        out[rowsets[k]] = res.results[k]["o"]
    return out.reshape(1, 8192, DM)


# version 08
# speedup vs baseline: 1.1307x; 1.0201x over previous
import numpy as np
import concourse.bass as bass
import concourse.mybir as mybir
from concourse.bass_utils import run_bass_kernel_spmd

F32 = mybir.dt.float32
F32R = mybir.dt.float32r
AX = mybir.AxisListType
AF = mybir.ActivationFunctionType
ALU = mybir.AluOpType

NCORES = 8
ROWS = 1024
DM = 2048
GROUPS = 32
GMICRO = 4096
NLOC = 131072
KC = 16
MC = 44
SCALE = float(8 ** -0.5)
EPS = 1e-5
VC = 4097.0
YW = 40


def _host_consts():
    c = {}
    c["ident"] = np.eye(128, dtype=np.float32)
    p = np.arange(128)
    c["mask1"] = (p[:, None] // 8 == p[None, :] // 8).astype(np.float32)
    c["mask2"] = (p[:, None] % 16 == p[None, :] % 16).astype(np.float32)

    def shift(step, win):
        m = np.zeros((128, 128), np.float32)
        for q in range(128):
            k = q + step
            if 0 <= k < 128 and win(k) == win(q):
                m[k, q] = 1.0
        return m
    c["cm1"] = shift(-1, lambda x: x // 8)
    c["cp1"] = shift(+1, lambda x: x // 8)
    c["cm2"] = shift(-16, lambda x: x % 16)
    c["cp2"] = shift(+16, lambda x: x % 16)
    return c


def _host_wq_wk(qkv_w):
    wq = np.zeros((2, 128, 40), np.float32)
    wk = np.zeros((2, 128, 40), np.float32)
    for par in range(2):
        for b in range(4):
            r0 = 32 * b + 16 * par
            wq[par, r0:r0 + 16, 0:8] = qkv_w[:, 0:8]
            wq[par, r0:r0 + 16, 32:40] = qkv_w[:, 8:16]
            wk[par, r0:r0 + 16, 0:8] = qkv_w[:, 16:24]
            wk[par, r0:r0 + 16, 32:40] = qkv_w[:, 24:32]
    return wq, wk


def _host_wb_skeleton():
    wb = np.zeros((2, 128, 128), np.float32)
    for par in range(2):
        for b in range(4):
            r0 = 32 * b + 16 * par
            wb[par, r0:r0 + 16, 48:64] = np.eye(16, dtype=np.float32)
    return wb


def build_program():
    nc = bass.Bass("TRN2", target_bir_lowering=False, debug=False)

    def din(name, shape):
        return nc.dram_tensor(name, list(shape), F32, kind="ExternalInput").ap()

    xatt = din("xatt", [ROWS, DM])
    xout = din("xout", [ROWS, DM])
    wgate = din("wgate", [MC, 128, KC, 128])
    wup = din("wup", [MC, 128, KC, 128])
    wdown = din("wdown", [MC, 128, KC, 128])
    gsc = din("gsc", [128, MC])
    usc = din("usc", [128, MC])
    dsc = din("dsc", [128, KC])
    wq_h = din("wq_h", [128, 2, 40])
    wk_h = din("wk_h", [128, 2, 40])
    wb_h = din("wb_h", [128, 2, 128])
    qkvw_h = din("qkvw_h", [16, 48])
    projw1_h = din("projw1_h", [8, 16])
    projw2_h = din("projw2_h", [8, 16])
    projb_h = din("projb_h", [1, 16])
    convw0_h = din("convw0_h", [8, 9])
    convb0_h = din("convb0_h", [8, 1])
    convw1_h = din("convw1_h", [8, 9])
    convb1_h = din("convb1_h", [8, 1])
    lnw_h = din("lnw_h", [128, 16])
    lnb_h = din("lnb_h", [128, 16])
    ones_h = din("ones_h", [1, 128])
    ident_h = din("ident_h", [128, 128])
    mask1_h = din("mask1_h", [128, 128])
    mask2_h = din("mask2_h", [128, 128])
    cm1_h = din("cm1_h", [128, 128])
    cp1_h = din("cp1_h", [128, 128])
    cm2_h = din("cm2_h", [128, 128])
    cp2_h = din("cp2_h", [128, 128])

    o = nc.dram_tensor("o", [ROWS, DM], F32, kind="ExternalOutput").ap()
    s_dram = nc.dram_tensor("s_dram", [16, NLOC], F32, kind="Internal").ap()
    wt_dram = nc.dram_tensor("wt_dram", [16, 128], F32, kind="Internal").ap()

    ctx = []

    def sb(name, shape, dtype=F32):
        g = nc.sbuf_tensor(name, list(shape), dtype)
        t = g.__enter__()
        ctx.append(g)
        return t

    def psalloc(name):
        g = nc.psum_tensor(name, [128, 512], F32)
        t = g.__enter__()
        ctx.append(g)
        return t

    ident = sb("ident", [128, 128])
    mask1 = sb("mask1", [128, 128])
    mask2 = sb("mask2", [128, 128])
    cm1 = sb("cm1", [128, 128])
    cp1 = sb("cp1", [128, 128])
    cm2 = sb("cm2", [128, 128])
    cp2 = sb("cp2", [128, 128])
    wq_s = sb("wq_s", [128, 2, 40])
    wk_s = sb("wk_s", [128, 2, 40])
    wb_s = sb("wb_s", [128, 2, 128])
    qkvw_s = sb("qkvw_s", [16, 48])
    projw1_s = sb("projw1_s", [8, 16])
    projw2_s = sb("projw2_s", [8, 16])
    projb_s = sb("projb_s", [1, 16])
    cw0_s = sb("cw0_s", [8, 9])
    cb0_s = sb("cb0_s", [8, 1])
    cw1_s = sb("cw1_s", [8, 9])
    cb1_s = sb("cb1_s", [8, 1])
    ones1 = sb("ones1", [1, 128])
    beff = sb("beff", [128, 16])
    gamma = sb("gamma", [128, 16])
    beta = sb("beta", [128, 16])
    gsc_s = sb("gsc_s", [128, MC])
    usc_s = sb("usc_s", [128, MC])
    dsc_s = sb("dsc_s", [128, KC])
    wvt = sb("wvt", [8, 16])
    winit = sb("winit", [16, 128])
    smallv = sb("smallv", [128, 64])

    arA = sb("arA", [128, 2048])
    arB = sb("arB", [128, 12288])
    arC = sb("arC", [128, 4096])
    arD = sb("arD", [128, 4096])
    arE = sb("arE", [128, 4096])
    arF = sb("arF", [128, 4096])
    xsr = sb("xsr", [128, 8192], F32R)
    hr = sb("hr", [128, 2048], F32R)
    wr = sb("wr", [128, 8192], F32R)

    xnfm = arA[:, :].rearrange("p (k r) -> p k r", k=KC)
    qs = arB[0:48, 0:4096]
    ks = arB[0:48, 4096:8192]
    vs = arB[:, 8192:12288]
    y2cm = arC[0:16, 0:4096]
    tocm = arC[0:16, 0:4096]
    v1tm = arD[:, 0:528].rearrange("p (t c) -> p t c", c=66)
    v2tm = arD[:, 528:1056].rearrange("p (t c) -> p t c", c=66)
    yslab = arD[:, 1056:2336].rearrange("p (t c) -> p t c", c=YW)
    tokr = arD[:, 2336:2848].rearrange("p (t c) -> p t c", c=16)
    pbuf = [arE[:, 0:512], arE[:, 512:1024]]
    ptm = [arE[:, 1024:1536], arE[:, 1536:2048]]
    zslab = arE[:, 2048:2560].rearrange("p (t c) -> p t c", c=16)
    zsq = arE[:, 2560:3072].rearrange("p (t c) -> p t c", c=16)
    tokout = arE[:, 3072:3584].rearrange("p (t c) -> p t c", c=16)
    zred = arE[:, 3584:3616]
    xa_tm = arF[:, 0:2048]
    sq_tm = arF[:, 2048:4096]

    xs_hi = xsr[:, 0:4096].rearrange("p (k r) -> p k r", k=KC)
    xs_lo = xsr[:, 4096:8192].rearrange("p (k r) -> p k r", k=KC)
    hhi = hr[:, 0:1024].rearrange("p (j r) -> p j r", j=4)
    hlo = hr[:, 1024:2048].rearrange("p (j r) -> p j r", j=4)
    hraw = [arB[:, 0:256], arB[:, 256:512]]
    hv1 = [arB[:, 512:768], arB[:, 768:1024]]
    dslab = arC[:, 0:4096].rearrange("p (k r) -> p k r", k=KC)
    xo_tm = arF[:, 0:2048]
    vhi_tm = arD[:, 0:2048]
    vlo_tm = arD[:, 2048:4096]
    x_tm = arD[:, 0:2048]
    wg_t = [wr[:, 0:2048].rearrange("p (k r) -> p k r", k=KC),
            wr[:, 2048:4096].rearrange("p (k r) -> p k r", k=KC)]
    wu_t = [wr[:, 4096:6144].rearrange("p (k r) -> p k r", k=KC),
            wr[:, 6144:8192].rearrange("p (k r) -> p k r", k=KC)]
    wd_t = wg_t
    g_tm = arF[:, 0:2048]
    s_tm = arF[:, 2048:4096]
    o_tm = s_tm

    y2z = [smallv[:, 0:16], smallv[:, 16:32]]
    rec4 = [smallv[:, 32:36], smallv[:, 36:40]]
    negm2 = [smallv[:, 40:44], smallv[:, 44:48]]
    negm1 = [smallv[:, 48:52], smallv[:, 52:56]]
    mu_c = smallv[:, 56:57]
    rs_c = smallv[:, 57:58]

    pA = psalloc("pA")
    pB = psalloc("pB")
    pC = psalloc("pC")
    pD = psalloc("pD")
    pE = psalloc("pE")
    pF = psalloc("pF")
    pG = psalloc("pG")
    pH = psalloc("pH")

    barg = nc.semaphore("bar")
    barsem = barg.__enter__()
    ctx.append(barg)
    gsems = []
    for gi in range(2):
        gg = nc.semaphore(f"gbar{gi}")
        gsems.append(gg.__enter__())
        ctx.append(gg)
    blkg = nc.Block()
    block = blkg.__enter__()

    phases = []
    phase_deps = []

    def phase(dep=None, **fns):
        phases.append(fns)
        phase_deps.append(len(phases) - 2 if dep is None else dep)
        return len(phases) - 1

    def dma_fn(n):
        def wrap(fn):
            fn.dma_incs = n
            return fn
        return wrap

    def sw_dma_fn(n, gi=0):
        def wrap(fn):
            fn.sw_dma_incs = n
            fn.sw_sem = gi
            return fn
        return wrap


    @dma_fn(24)
    def init_loads(e):
        for src, dst in (
            (ident_h, ident[:]), (mask1_h, mask1[:]), (mask2_h, mask2[:]),
            (cm1_h, cm1[:]), (cp1_h, cp1[:]), (cm2_h, cm2[:]), (cp2_h, cp2[:]),
            (wq_h, wq_s[:]), (wk_h, wk_s[:]), (wb_h, wb_s[:]),
            (qkvw_h, qkvw_s[:]), (projw1_h, projw1_s[:]),
            (projw2_h, projw2_s[:]), (projb_h, projb_s[:]),
            (convw0_h, cw0_s[:]), (convb0_h, cb0_s[:]),
            (convw1_h, cw1_s[:]), (convb1_h, cb1_s[:]),
            (lnw_h, gamma[:]), (lnb_h, beta[:]), (ones_h, ones1[:]),
            (gsc, gsc_s[:]), (usc, usc_s[:]), (dsc, dsc_s[:]),
        ):
            e.dma_start(out=dst, in_=src).then_inc(barsem, 16)
    phase(sync=init_loads)

    def init_memset(e):
        e.memset(winit[:], 0.0)
        e.memset(v1tm[:, :, 16:17], 1.0)
        e.memset(v2tm[:, :, 16:17], 1.0)
        e.memset(y2cm[:, :], 0.0)
    phase(gpsimd=init_memset)

    def init_scaleq(e):
        e.tensor_scalar_mul(wq_s[:, :, :], wq_s[:, :, :], SCALE)
    phase(vector=init_scaleq)

    def init_wvT(e):
        e.transpose(pA[0:8, 0:16], qkvw_s[0:16, 32:40], ident[0:16, 0:16])
        e.transpose(pA[0:8, 16:32], qkvw_s[0:16, 40:48], ident[0:16, 0:16])
    phase(tensor=init_wvT)

    def init_wvT_ev(e):
        e.tensor_copy(winit[0:8, 0:32], pA[0:8, 0:32])
    phase(vector=init_wvT_ev)

    jobs = [
        (0,   (0, 16),  None,            projw1_s, True, True),
        (16,  (0, 16),  cw0_s[:, 3:4],   projw1_s, True, True),
        (32,  (0, 16),  cw0_s[:, 5:6],   projw1_s, True, True),
        (64,  (16, 32), None,            projw2_s, True, True),
        (80,  (16, 32), cw1_s[:, 1:2],   projw2_s, True, True),
        (96,  (16, 32), cw1_s[:, 7:8],   projw2_s, True, True),
        (112, (0, 16),  cw0_s[:, 4:5],   projw1_s, True, False),
        (112, (16, 32), cw1_s[:, 4:5],   projw2_s, False, True),
    ]
    for col, (c0, c1), diag, pr, st, sp in jobs:
        def vj(e, c0=c0, c1=c1, diag=diag):
            if diag is None:
                e.tensor_copy(wvt[:], winit[0:8, c0:c1])
            else:
                e.tensor_scalar(wvt[:], winit[0:8, c0:c1], diag, None, ALU.mult)
        phase(vector=vj)

        def tj(e, col=col, pr=pr, st=st, sp=sp):
            e.matmul(pB[0:16, col:col + 16], wvt[:], pr[:],
                     start=st, stop=sp)
        phase(tensor=tj)

    def init_beff_mm(e):
        e.matmul(pC[0:1, 0:16], cb0_s[:], projw1_s[:], start=True, stop=False)
        e.matmul(pC[0:1, 0:16], cb1_s[:], projw2_s[:], start=False, stop=True)
    phase(tensor=init_beff_mm)

    def init_beff_add(e):
        e.tensor_add(winit[0:1, 48:64], pC[0:1, 0:16], projb_s[:])
    phase(vector=init_beff_add)

    def init_beff_bc(e):
        e.matmul(pD[0:128, 0:16], ones1[:], winit[0:1, 48:64],
                 start=True, stop=True)
    phase(tensor=init_beff_bc)

    def init_beff_ev(e):
        e.tensor_copy(beff[:], pD[0:128, 0:16])
    phase(vector=init_beff_ev)

    def init_vt_evac(e):
        e.tensor_copy(winit[:, 0:48], pB[0:16, 0:48])
        e.tensor_copy(winit[:, 64:128], pB[0:16, 64:128])
    phase(vector=init_vt_evac)

    @dma_fn(1)
    def init_vt_out(e):
        e.dma_start(out=wt_dram, in_=winit[:]).then_inc(barsem, 16)
    phase(sync=init_vt_out)

    @dma_fn(16)
    def init_vt_in(e):
        for par in range(2):
            for b in range(4):
                r0 = 32 * b + 16 * par
                e.dma_start(out=wb_s[r0:r0 + 16, par, 0:48],
                            in_=wt_dram[:, 0:48]).then_inc(barsem, 16)
                e.dma_start(out=wb_s[r0:r0 + 16, par, 64:128],
                            in_=wt_dram[:, 64:128]).then_inc(barsem, 16)
    phase(sync=init_vt_in)

    def ln_tile(src_dram, dst_tm, tile):
        @dma_fn(1)
        def load(e, tile=tile):
            e.dma_start(out=dst_tm, in_=src_dram[tile * 128:(tile + 1) * 128, :]
                        ).then_inc(barsem, 16)
        phase(sync=load)

        def mean(e):
            e.reduce_sum(mu_c[:], dst_tm, axis=AX.X, negate=True)
            e.drain()
            e.tensor_scalar_mul(mu_c[:], mu_c[:], 1.0 / DM)
            e.drain()
            e.tensor_scalar(dst_tm, dst_tm, mu_c[:], None, ALU.add)
        phase(vector=mean)

        def sq(e):
            e.activation(sq_tm[:], dst_tm, AF.Square)
        phase(scalar=sq)

        def var(e):
            e.reduce_sum(rs_c[:], sq_tm[:], axis=AX.X)
            e.drain()
            e.tensor_scalar(rs_c[:], rs_c[:], 1.0 / DM, EPS, ALU.mult, ALU.add)
        phase(vector=var)

        def rsq(e):
            e.activation(rs_c[:], rs_c[:], AF.Ln)
            e.drain()
            e.activation(rs_c[:], rs_c[:], AF.Exp, scale=-0.5)
        phase(scalar=rsq)

        def fin(e):
            e.tensor_scalar(dst_tm, dst_tm, rs_c[:], None, ALU.mult)
        phase(vector=fin)

    qps = (pA, pB, pC, pD)
    kps = (pE, pF, pG, pH)
    for m in range(GROUPS):
        if m % 4 == 0:
            tile8 = m // 4
            ln_tile(xatt, xa_tm[:, :], tile8)
            for jh in range(4):
                def tp(e, jh=jh):
                    for j2 in range(4):
                        j = jh * 4 + j2
                        e.transpose(pA[:, j2 * 128:(j2 + 1) * 128],
                                    xa_tm[:, j * 128:(j + 1) * 128], ident[:])
                phase(tensor=tp)

                def tpe(e, jh=jh):
                    for j2 in range(4):
                        j = jh * 4 + j2
                        e.tensor_copy(xnfm[:, j, :],
                                      pA[:, j2 * 128:(j2 + 1) * 128])
                phase(vector=tpe)

        def rhs_ap(blkb, m=m):
            mm4 = m % 4
            return xnfm[32 * blkb:32 * blkb + 32, :, mm4 * 32:mm4 * 32 + 32]

        for par in range(2):
            def qkv_mm(e, par=par, rhs_ap=rhs_ap):
                for blkb in range(4):
                    tp = (32 * blkb, 0)
                    r = rhs_ap(blkb)
                    e.matmul(qps[blkb][0:40, :], wq_s[32 * blkb:32 * blkb + 32, par, :],
                             r, start=True, stop=True, tile_position=tp)
                    e.matmul(kps[blkb][0:40, :], wk_s[32 * blkb:32 * blkb + 32, par, :],
                             r, start=True, stop=True, tile_position=tp)
            phase(tensor=qkv_mm)

            def qkv_ev(e, par=par):
                for blkb in range(4):
                    sp = 2 * blkb + par
                    for slab, psl in ((qs, qps[blkb]), (ks, kps[blkb])):
                        d1 = slab[0:8, :].rearrange(
                            "p (i b s) -> p i b s", i=32, b=16)[
                            :, :, :, sp].transpose([0, 2, 1])
                        e.tensor_copy(d1, psl[0:8, :].rearrange(
                            "p (b i) -> p b i", b=16))
                        d2 = slab[32:40, :].rearrange(
                            "p (b s r g) -> p b s r g", b=16, s=8, r=4)[
                            :, :, sp, :, :].transpose([0, 1, 3, 2])
                        e.tensor_copy(d2, psl[32:40, :].rearrange(
                            "p (b g r) -> p b g r", b=16, g=8))
            phase(vector=qkv_ev)

        for par in range(2):
            def qkvB_mm(e, par=par, rhs_ap=rhs_ap):
                for blkb in range(4):
                    e.matmul(qps[blkb][:, :], wb_s[32 * blkb:32 * blkb + 32, par, :],
                             rhs_ap(blkb), start=True, stop=True,
                             tile_position=(32 * blkb, 0))
            phase(tensor=qkvB_mm)

            def qkvB_ev(e, par=par):
                for blkb in range(4):
                    sp = 2 * blkb + par
                    d1 = vs[0:64, :].rearrange(
                        "p (i b s) -> p i b s", i=32, b=16)[
                        :, :, :, sp].transpose([0, 2, 1])
                    e.tensor_copy(d1, qps[blkb][0:64, :].rearrange(
                        "p (b i) -> p b i", b=16))
                    d2 = vs[64:128, :].rearrange(
                        "p (b s r g) -> p b s r g", b=16, s=8, r=4)[
                        :, :, sp, :, :].transpose([0, 1, 3, 2])
                    e.tensor_copy(d2, qps[blkb][64:128, :].rearrange(
                        "p (b g r) -> p b g r", b=16, g=8))
            phase(vector=qkvB_ev)

        Sb = (pA, pB)
        PTb = (pC, pD)
        VTb = (pE, pF)
        Yb = (pG, pH)
        p_exp2 = {}
        p_ptr2 = {}
        p_pev2 = {}
        p_av2 = {}
        p_y2t3 = {}
        p_y2ev = {}
        for st in range(8):
            par = st % 2
            tl = [4 * st + i for i in range(4)]

            def mmS2(e, tl=tl, par=par):
                for i, t2 in enumerate(tl):
                    e.matmul(Sb[par][:, i * 128:(i + 1) * 128],
                             qs[32:40, t2 * 128:(t2 + 1) * 128],
                             ks[32:40, t2 * 128:(t2 + 1) * 128],
                             start=True, stop=True, tile_position=(32, 0))
            p_mm = phase(dep=p_exp2.get(st - 2), tensor=mmS2)

            def negmax2(e, par=par):
                e.reduce_max(negm2[par][:],
                             Sb[par][:, :].rearrange("p (t x) -> p t x", t=4),
                             axis=AX.X, negate=True)
            p_ng = phase(dep=p_mm, vector=negmax2)

            def expS2(e, par=par):
                for i in range(4):
                    e.activation(pbuf[par][:, i * 128:(i + 1) * 128],
                                 Sb[par][:, i * 128:(i + 1) * 128], AF.Exp,
                                 bias=negm2[par][:, i:i + 1])
            dep = p_ng
            if st - 2 in p_ptr2:
                dep = max(dep, p_ptr2[st - 2])
            p_exp2[st] = phase(dep=dep, scalar=expS2)

            def ptrans2(e, tl=tl, par=par):
                for i in range(4):
                    e.transpose(PTb[par][:, i * 128:(i + 1) * 128],
                                pbuf[par][:, i * 128:(i + 1) * 128], ident[:])
                for i, t2 in enumerate(tl):
                    e.transpose(VTb[par][:, i * 128:i * 128 + 64],
                                vs[64:128, t2 * 128:(t2 + 1) * 128],
                                ident[64:128, 64:128])
            dep = p_exp2[st]
            if st - 2 in p_pev2:
                dep = max(dep, p_pev2[st - 2])
            if st - 2 in p_y2ev:
                dep = max(dep, p_y2ev[st - 2])
            p_ptr2[st] = phase(dep=dep, tensor=ptrans2)

            def pt_evac2(e, par=par):
                e.tensor_mul(ptm[par][:, :].rearrange("p (t x) -> p t x", t=4),
                             PTb[par][:, :].rearrange("p (t x) -> p t x", t=4),
                             mask1[:].unsqueeze(1).broadcast_to(
                                 [128, 4, 128]))
                for i in range(4):
                    e.tensor_copy(v2tm[:, par * 4 + i, 0:16],
                                  VTb[par][:, i * 128:i * 128 + 16])
                    e.tensor_copy(v2tm[:, par * 4 + i, 17:65],
                                  VTb[par][:, i * 128 + 16:i * 128 + 64])
            dep = p_ptr2[st]
            if st - 2 in p_av2:
                dep = max(dep, p_av2[st - 2])
            p_pev2[st] = phase(dep=dep, vector=pt_evac2)

            def av2(e, par=par):
                for i in range(4):
                    lhs = ptm[par][:, i * 128:(i + 1) * 128]
                    e.matmul(Yb[par][:, i * YW:i * YW + 17], lhs,
                             v2tm[:, par * 4 + i, 0:17], start=True, stop=True)
                    e.matmul(Yb[par][:, i * YW + 20:i * YW + 36], cm1[:],
                             v2tm[:, par * 4 + i, 17:33], start=True, stop=False)
                    e.matmul(Yb[par][:, i * YW + 20:i * YW + 36], cp1[:],
                             v2tm[:, par * 4 + i, 33:49], start=False, stop=False)
                    e.matmul(Yb[par][:, i * YW + 20:i * YW + 36], ident[:],
                             v2tm[:, par * 4 + i, 49:65], start=False, stop=True)
            dep = p_pev2[st]
            if st - 2 in p_y2t3:
                dep = max(dep, p_y2t3[st - 2])
            p_av2[st] = phase(dep=dep, tensor=av2)

            def recs2(e, par=par):
                dv = Yb[par][:, 0:4 * YW].rearrange("p (t c) -> p t c", c=YW)[
                    :, :, 16]
                e.activation(rec4[par][:], dv, AF.Ln)
                e.drain()
                e.activation(rec4[par][:], rec4[par][:], AF.Exp, scale=-1.0)
            p_rc = phase(dep=p_av2[st], scalar=recs2)

            prev = p_rc
            for i in range(4):
                def y2div(e, i=i, par=par):
                    e.tensor_scalar(y2z[par][:], Yb[par][:, i * YW:i * YW + 16],
                                    rec4[par][:, i:i + 1], None, ALU.mult)
                    e.drain()
                    e.tensor_add(y2z[par][:], y2z[par][:],
                                 Yb[par][:, i * YW + 20:i * YW + 36])
                dep = prev
                if i == 0 and st - 2 in p_y2t3:
                    dep = max(dep, p_y2t3[st - 2])
                p_dv = phase(dep=dep, vector=y2div)

                def y2t(e, i=i, par=par):
                    e.transpose(PTb[par][0:16, i * 128:(i + 1) * 128],
                                y2z[par][:], ident[:])
                prev = phase(dep=p_dv, tensor=y2t)
            p_y2t3[st] = prev

            def y2ev(e, tl=tl, par=par):
                for i, t2 in enumerate(tl):
                    b0, h = t2 // 2, t2 % 2
                    dst = y2cm[:, :].rearrange(
                        "p (g r b s) -> p g r b s", g=8, r=4, b=16)[
                        :, :, :, b0, 4 * h:4 * h + 4].transpose([0, 3, 2, 1])
                    e.tensor_copy(dst, PTb[par][0:16, i * 128:(i + 1) * 128]
                                  .rearrange("p (s r g) -> p s r g", s=4, r=4))
            p_y2ev[st] = phase(dep=p_y2t3[st], vector=y2ev)

        p_exp1 = {}
        p_ptr1 = {}
        p_pev1 = {}
        p_av1 = {}
        p_y1e = {}
        for st in range(8):
            par = st % 2
            tl = [4 * st + i for i in range(4)]

            def mmS1(e, tl=tl, par=par):
                for i, ip in enumerate(tl):
                    e.matmul(Sb[par][:, i * 128:(i + 1) * 128],
                             qs[0:8, ip * 128:(ip + 1) * 128],
                             ks[0:8, ip * 128:(ip + 1) * 128],
                             start=True, stop=True)
            p_mm = phase(dep=p_exp1.get(st - 2), tensor=mmS1)

            def negmax1(e, par=par):
                e.reduce_max(negm1[par][:],
                             Sb[par][:, :].rearrange("p (t x) -> p t x", t=4),
                             axis=AX.X, negate=True)
            p_ng = phase(dep=p_mm, vector=negmax1)

            def expS1(e, par=par):
                for i in range(4):
                    e.activation(pbuf[par][:, i * 128:(i + 1) * 128],
                                 Sb[par][:, i * 128:(i + 1) * 128], AF.Exp,
                                 bias=negm1[par][:, i:i + 1])
            dep = p_ng
            if st - 2 in p_ptr1:
                dep = max(dep, p_ptr1[st - 2])
            p_exp1[st] = phase(dep=dep, scalar=expS1)

            def ptrans1(e, tl=tl, par=par):
                for i in range(4):
                    e.transpose(PTb[par][:, i * 128:(i + 1) * 128],
                                pbuf[par][:, i * 128:(i + 1) * 128], ident[:])
                for i, ip in enumerate(tl):
                    e.transpose(VTb[par][:, i * 128:i * 128 + 64],
                                vs[0:64, ip * 128:(ip + 1) * 128],
                                ident[0:64, 0:64])
            dep = p_exp1[st]
            if st - 2 in p_pev1:
                dep = max(dep, p_pev1[st - 2])
            p_ptr1[st] = phase(dep=dep, tensor=ptrans1)

            def pt_evac1(e, tl=tl, par=par):
                e.tensor_mul(ptm[par][:, :].rearrange("p (t x) -> p t x", t=4),
                             PTb[par][:, :].rearrange("p (t x) -> p t x", t=4),
                             mask1[:].unsqueeze(1).broadcast_to(
                                 [128, 4, 128]))
                for i, ip in enumerate(tl):
                    e.tensor_add(v1tm[:, par * 4 + i, 0:16],
                                 VTb[par][:, i * 128:i * 128 + 16], beff[:])
                    e.tensor_copy(v1tm[:, par * 4 + i, 17:49],
                                  VTb[par][:, i * 128 + 16:i * 128 + 48])
                    e.tensor_copy(tokr[:, ip, :],
                                  VTb[par][:, i * 128 + 48:i * 128 + 64])
            dep = p_ptr1[st]
            if st - 2 in p_av1:
                dep = max(dep, p_av1[st - 2])
            p_pev1[st] = phase(dep=dep, vector=pt_evac1)

            def av1(e, tl=tl, par=par):
                for i, ip in enumerate(tl):
                    lhs = ptm[par][:, i * 128:(i + 1) * 128]
                    e.matmul(Yb[par][:, i * YW:i * YW + 17], lhs,
                             v1tm[:, par * 4 + i, 0:17], start=True, stop=True)
                    e.matmul(Yb[par][:, i * YW + 20:i * YW + 36], cm1[:],
                             v1tm[:, par * 4 + i, 17:33], start=True, stop=False)
                    e.matmul(Yb[par][:, i * YW + 20:i * YW + 36], cp1[:],
                             v1tm[:, par * 4 + i, 33:49], start=False, stop=False)
                    e.matmul(Yb[par][:, i * YW + 20:i * YW + 36],
                             y2cm[:, ip * 128:(ip + 1) * 128],
                             ident[0:16, 0:16], start=False, stop=True,
                             is_transpose=True)
            dep = p_pev1[st]
            if st - 2 in p_y1e:
                dep = max(dep, p_y1e[st - 2])
            p_av1[st] = phase(dep=dep, tensor=av1)

            def y1ev(e, tl=tl, par=par):
                for i, ip in enumerate(tl):
                    e.copy(yslab[:, ip, 0:17], Yb[par][:, i * YW:i * YW + 17])
                    e.copy(yslab[:, ip, 20:36],
                           Yb[par][:, i * YW + 20:i * YW + 36])
            p_y1e[st] = phase(dep=p_av1[st], scalar=y1ev)

        def ln16r(e):
            e.activation(zred[:], yslab[:, :, 16], AF.Ln)
            e.drain()
            e.activation(zred[:], zred[:], AF.Exp, scale=-1.0)
        phase(scalar=ln16r)

        def ln16a(e):
            e.tensor_mul(zslab[:, :, :], yslab[:, :, 0:16],
                         zred[:].unsqueeze(2).broadcast_to(
                             [128, 32, 16]))
            e.drain()
            e.tensor_add(zslab[:, :, :], zslab[:, :, :], yslab[:, :, 20:36])
            e.drain()
            e.reduce_sum(zred[:], zslab[:, :, :], axis=AX.X, negate=True)
            e.drain()
            e.tensor_scalar_mul(zred[:], zred[:], 1.0 / 16.0)
            e.drain()
            e.tensor_add(zslab[:, :, :], zslab[:, :, :],
                         zred[:].unsqueeze(2).broadcast_to(
                             [128, 32, 16]))
        phase(vector=ln16a)

        def ln16sq(e):
            e.activation(zsq[:, :, :], zslab[:, :, :], AF.Square)
        phase(scalar=ln16sq)

        def ln16b(e):
            e.reduce_sum(zred[:], zsq[:, :, :], axis=AX.X)
            e.drain()
            e.tensor_scalar(zred[:], zred[:], 1.0 / 16.0, EPS, ALU.mult, ALU.add)
        phase(vector=ln16b)

        def ln16c(e):
            e.activation(zred[:], zred[:], AF.Ln)
            e.drain()
            e.activation(zred[:], zred[:], AF.Exp, scale=-0.5)
        phase(scalar=ln16c)

        def ln16d(e):
            e.tensor_mul(zslab[:, :, :], zslab[:, :, :],
                         zred[:].unsqueeze(2).broadcast_to(
                             [128, 32, 16]))
            e.drain()
            e.tensor_mul(zslab[:, :, :], zslab[:, :, :],
                         gamma[:].unsqueeze(1).broadcast_to(
                             [128, 32, 16]))
            e.drain()
            e.tensor_add(zslab[:, :, :], zslab[:, :, :],
                         beta[:].unsqueeze(1).broadcast_to(
                             [128, 32, 16]))
            e.drain()
            e.tensor_add(tokout[:, :, :], zslab[:, :, :], tokr[:, :, :])
        phase(vector=ln16d)

        for q in range(4):
            def toT(e, q=q):
                for j in range(4):
                    e.transpose(pA[0:16, j * 128:(j + 1) * 128],
                                tokout[:, q * 8 + j, :], ident[:])
                for j in range(4):
                    e.transpose(pB[0:16, j * 128:(j + 1) * 128],
                                tokout[:, q * 8 + 4 + j, :], ident[:])
            phase(tensor=toT)

            def toEv(e, q=q):
                e.tensor_copy(tocm[:, q * 1024:q * 1024 + 512], pA[0:16, :])
                e.tensor_copy(tocm[:, q * 1024 + 512:(q + 1) * 1024], pB[0:16, :])
            phase(vector=toEv)

        @dma_fn(1)
        def scat(e, m=m):
            e.dma_start(out=s_dram[:, m * GMICRO:(m + 1) * GMICRO],
                        in_=tocm[:, :]).then_inc(barsem, 16)
        phase(sync=scat)


    allp = (pA, pB, pC, pD, pE, pF, pG, pH)
    for tq in range(4):
        for tile in range(2):
            ln_tile(xout, xo_tm, tq * 2 + tile)

            def velt(e):
                e.tensor_scalar_mul(sq_tm[:], xo_tm[:], VC)
                e.drain()
                e.tensor_sub(vlo_tm[:], sq_tm[:], xo_tm[:])
                e.drain()
                e.tensor_sub(vhi_tm[:], sq_tm[:], vlo_tm[:])
                e.drain()
                e.tensor_sub(vlo_tm[:], xo_tm[:], vhi_tm[:])
            phase(vector=velt)

            for jh in range(4):
                def ftp(e, jh=jh):
                    for j2 in range(4):
                        j = jh * 4 + j2
                        e.transpose(pA[:, j2 * 128:(j2 + 1) * 128],
                                    vhi_tm[:, j * 128:(j + 1) * 128], ident[:])
                        e.transpose(pB[:, j2 * 128:(j2 + 1) * 128],
                                    vlo_tm[:, j * 128:(j + 1) * 128], ident[:])
                phase(tensor=ftp)

                def ftpe(e, tile=tile, jh=jh):
                    for j2 in range(4):
                        j = jh * 4 + j2
                        e.tensor_copy(xs_hi[:, j, tile * 128:(tile + 1) * 128],
                                      pA[:, j2 * 128:(j2 + 1) * 128])
                        e.tensor_copy(xs_lo[:, j, tile * 128:(tile + 1) * 128],
                                      pB[:, j2 * 128:(j2 + 1) * 128])
                phase(vector=ftpe)

        gu_banks = ((pA, pB), (pC, pD))
        ph_wload = {}
        ph_gu = {}
        ph_hmake = {-1: None, -2: None}
        ph_dacc_last = None
        ph_dmm_last = None
        for jb in range(11):
            for jj in range(4):
                j = jb * 4 + jj

                @sw_dma_fn(2, j % 2)
                def wload(e, j=j):
                    e.dma_start(out=wg_t[j % 2][:, :, :], in_=wgate[j, :, :, :]
                                ).then_inc(gsems[j % 2], 16)
                    e.dma_start(out=wu_t[j % 2][:, :, :], in_=wup[j, :, :, :]
                                ).then_inc(gsems[j % 2], 16)
                dep = ph_gu.get(j - 2)
                if dep is not None and ph_dmm_last is not None:
                    dep = max(dep, ph_dmm_last)
                ph_wload[j] = phase(dep=dep, gpsimd=wload)

                def gu_mm(e, j=j):
                    bg, bu = gu_banks[j % 2]
                    for k in range(KC):
                        e.matmul(bg[:, 0:256], wg_t[j % 2][:, k, :],
                                 xs_hi[:, k, :], start=(k == 0), stop=False)
                        e.matmul(bg[:, 0:256], wg_t[j % 2][:, k, :],
                                 xs_lo[:, k, :], start=False, stop=(k == KC - 1))
                    for k in range(KC):
                        e.matmul(bu[:, 0:256], wu_t[j % 2][:, k, :],
                                 xs_hi[:, k, :], start=(k == 0), stop=False)
                        e.matmul(bu[:, 0:256], wu_t[j % 2][:, k, :],
                                 xs_lo[:, k, :], start=False, stop=(k == KC - 1))
                gdep = ph_wload[j]
                if ph_hmake.get(j - 2) is not None:
                    gdep = max(gdep, ph_hmake[j - 2])
                if ph_dacc_last is not None:
                    gdep = max(gdep, ph_dacc_last)
                ph_gu[j] = phase(dep=gdep, tensor=gu_mm)

                def gsilu(e, j=j):
                    bg, bu = gu_banks[j % 2]
                    e.activation(hv1[j % 2][:], bg[:, 0:256], AF.Silu,
                                 scale=gsc_s[:, j:j + 1])
                phase(dep=ph_gu[j], scalar=gsilu)

                def hmake(e, j=j, jj=jj):
                    bg, bu = gu_banks[j % 2]
                    hw = hraw[j % 2]
                    e.scalar_tensor_tensor(hw[:], bu[:, 0:256],
                                           usc_s[:, j:j + 1], hv1[j % 2][:],
                                           ALU.mult, ALU.mult)
                    e.drain()
                    e.tensor_scalar_mul(sq_tm[:, 0:256], hw[:], VC)
                    e.drain()
                    e.tensor_sub(sq_tm[:, 256:512], sq_tm[:, 0:256], hw[:])
                    e.drain()
                    e.tensor_sub(hhi[:, jj, :], sq_tm[:, 0:256],
                                 sq_tm[:, 256:512])
                    e.drain()
                    e.tensor_sub(hlo[:, jj, :], hw[:], hhi[:, jj, :])
                ph_hmake[j] = phase(vector=hmake)

            ph_dload = {}
            ph_dmm = {}
            for jj in range(4):
                j = jb * 4 + jj

                @sw_dma_fn(1, j % 2)
                def dload(e, j=j):
                    e.dma_start(out=wd_t[j % 2][:, :, :], in_=wdown[j, :, :, :]
                                ).then_inc(gsems[j % 2], 16)
                ddep = ph_dmm.get(jj - 2)
                ph_dload[jj] = phase(dep=ddep, gpsimd=dload)

                def d_mm(e, jj=jj, j=j):
                    for oc in range(KC):
                        pp = allp[oc % 8]
                        c0 = (oc // 8) * 256
                        e.matmul(pp[:, c0:c0 + 256], wd_t[j % 2][:, oc, :],
                                 hhi[:, jj, :],
                                 start=(jj == 0 and oc < 8), stop=False)
                        e.matmul(pp[:, c0:c0 + 256], wd_t[j % 2][:, oc, :],
                                 hlo[:, jj, :], start=False,
                                 stop=(jj == 3 and oc >= 8))
                ph_dmm[jj] = phase(dep=ph_dload[jj], tensor=d_mm)
                ph_dmm_last = ph_dmm[jj]

            def dacc(e, jb=jb):
                for oc in range(KC):
                    pp = allp[oc % 8]
                    c0 = (oc // 8) * 256
                    if jb == 0:
                        e.tensor_copy(dslab[:, oc, :], pp[:, c0:c0 + 256])
                    else:
                        e.tensor_add(dslab[:, oc, :], dslab[:, oc, :],
                                     pp[:, c0:c0 + 256])
            ph_dacc_last = phase(dep=ph_dmm[3], vector=dacc)

        def gfin(e):
            for oc in range(KC):
                e.activation(dslab[:, oc, :], dslab[:, oc, :], AF.Silu,
                             scale=dsc_s[:, oc:oc + 1])
        phase(scalar=gfin)

        for tb in range(2):
            row0 = tq * 256 + tb * 128

            def gT(e, tb=tb):
                for oc in range(4):
                    e.transpose(pA[:, oc * 128:(oc + 1) * 128],
                                dslab[:, oc, tb * 128:(tb + 1) * 128], ident[:])
                for oc in range(4):
                    e.transpose(pB[:, oc * 128:(oc + 1) * 128],
                                dslab[:, 4 + oc, tb * 128:(tb + 1) * 128],
                                ident[:])
                for oc in range(4):
                    e.transpose(pC[:, oc * 128:(oc + 1) * 128],
                                dslab[:, 8 + oc, tb * 128:(tb + 1) * 128],
                                ident[:])
                for oc in range(4):
                    e.transpose(pD[:, oc * 128:(oc + 1) * 128],
                                dslab[:, 12 + oc, tb * 128:(tb + 1) * 128],
                                ident[:])
            phase(tensor=gT)

            def gEv(e):
                e.tensor_copy(g_tm[:, 0:512], pA[:, :])
                e.tensor_copy(g_tm[:, 512:1024], pB[:, :])
                e.tensor_copy(g_tm[:, 1024:1536], pC[:, :])
                e.tensor_copy(g_tm[:, 1536:2048], pD[:, :])
            phase(vector=gEv)

            @dma_fn(2)
            def fload(e, row0=row0):
                e.dma_start(out=x_tm[:], in_=xout[row0:row0 + 128, :]
                            ).then_inc(barsem, 16)
                ta = row0 // 64
                e.dma_start(
                    out=s_tm[:],
                    in_=s_dram.rearrange("c (r j) -> c r j", j=DM)[
                        ta:ta + 2, 0:64, :]).then_inc(barsem, 16)
            phase(sync=fload)

            def fuse(e):
                e.tensor_mul(o_tm[:], s_tm[:], g_tm[:])
                e.drain()
                e.tensor_add(o_tm[:], o_tm[:], x_tm[:])
            phase(vector=fuse)

            @dma_fn(1)
            def fstore(e, row0=row0):
                e.dma_start(out=o[row0:row0 + 128, :], in_=o_tm[:]
                            ).then_inc(barsem, 16)
            phase(sync=fstore)

    if _PHASE_LIMIT is not None:
        del phases[_PHASE_LIMIT:]
    allsems = [barsem] + gsems
    cum = []
    tot = [0, 0, 0]
    for p in phases:
        for fn in p.values():
            d = getattr(fn, "dma_incs", 0)
            g = getattr(fn, "sw_dma_incs", 0)
            if g:
                tot[1 + getattr(fn, "sw_sem", 0)] += 16 * g
            elif d:
                tot[0] += 16 * d
            else:
                tot[0] += 1
        cum.append(tuple(tot))
    finals = tuple(tot)

    for ename in ("sync", "gpsimd", "tensor", "vector", "scalar"):
        dec = getattr(block, ename)

        def mk(ename=ename):
            def body(eng):
                lastw = [0, 0, 0]
                for pi, p in enumerate(phases):
                    if ename not in p:
                        continue
                    dep = phase_deps[pi]
                    if dep is not None and dep >= len(phases):
                        dep = len(phases) - 1
                    tgt = cum[dep] if (dep is not None and dep >= 0) \
                        else (0, 0, 0)
                    for si in range(3):
                        if tgt[si] > lastw[si] and tgt[si] > 0:
                            eng.wait_ge(allsems[si], tgt[si])
                            lastw[si] = tgt[si]
                    fn = p[ename]
                    fn(eng)
                    if (getattr(fn, "dma_incs", 0) == 0
                            and getattr(fn, "sw_dma_incs", 0) == 0):
                        eng.drain().then_inc(barsem, 1)
                if ename == "sync":
                    for si in range(3):
                        if finals[si] > 0:
                            eng.wait_ge(allsems[si], finals[si])
            return body
        dec(mk())

    blkg.__exit__(None, None, None)
    for g in reversed(ctx):
        g.__exit__(None, None, None)
    return nc


_PHASE_LIMIT = None
_PROG = None


def _get_prog():
    global _PROG
    if _PROG is None:
        _PROG = build_program()
    return _PROG


def kernel(L0_attn_x, L0_attn_qkv_w, L0_attn_conv_w0, L0_attn_conv_b0,
           L0_attn_conv_w1, L0_attn_conv_b1, L0_attn_proj_w, L0_attn_proj_b,
           L0_attn_ln_w, L0_attn_ln_b, L0_ffn_gate_w, L0_ffn_up_w,
           L0_ffn_down_w, L0_ffn_gate_s, L0_ffn_up_s, L0_ffn_down_s):
    f32 = np.float32
    X = np.ascontiguousarray(np.asarray(L0_attn_x, dtype=f32).reshape(8192, DM))
    qkv_w = np.asarray(L0_attn_qkv_w, dtype=f32)
    consts = _host_consts()
    wq, wk = _host_wq_wk(qkv_w)
    wb = _host_wb_skeleton()

    gate_w = np.asarray(L0_ffn_gate_w, dtype=f32)
    up_w = np.asarray(L0_ffn_up_w, dtype=f32)
    down_w = np.asarray(L0_ffn_down_w, dtype=f32)
    WG = np.ascontiguousarray(
        gate_w.reshape(MC, 128, KC, 128).transpose(0, 3, 2, 1))
    WU = np.ascontiguousarray(
        up_w.reshape(MC, 128, KC, 128).transpose(0, 3, 2, 1))
    WD = np.ascontiguousarray(
        down_w.reshape(KC, 128, MC, 128).transpose(2, 3, 0, 1))

    gsch = np.ascontiguousarray(
        np.asarray(L0_ffn_gate_s, dtype=f32).reshape(MC, 128).T)
    usch = np.ascontiguousarray(
        np.asarray(L0_ffn_up_s, dtype=f32).reshape(MC, 128).T)
    dsch = np.ascontiguousarray(
        np.asarray(L0_ffn_down_s, dtype=f32).reshape(KC, 128).T)

    shared = {
        "wgate": WG, "wup": WU, "wdown": WD,
        "gsc": gsch, "usc": usch, "dsc": dsch,
        "wq_h": np.ascontiguousarray(wq.transpose(1, 0, 2)),
        "wk_h": np.ascontiguousarray(wk.transpose(1, 0, 2)),
        "wb_h": np.ascontiguousarray(wb.transpose(1, 0, 2)),
        "qkvw_h": qkv_w,
        "projw1_h": np.ascontiguousarray(np.asarray(L0_attn_proj_w, f32)[0:8]),
        "projw2_h": np.ascontiguousarray(np.asarray(L0_attn_proj_w, f32)[8:16]),
        "projb_h": np.asarray(L0_attn_proj_b, f32).reshape(1, 16),
        "convw0_h": np.asarray(L0_attn_conv_w0, f32).reshape(8, 9),
        "convb0_h": np.asarray(L0_attn_conv_b0, f32).reshape(8, 1),
        "convw1_h": np.asarray(L0_attn_conv_w1, f32).reshape(8, 9),
        "convb1_h": np.asarray(L0_attn_conv_b1, f32).reshape(8, 1),
        "lnw_h": np.ascontiguousarray(
            np.broadcast_to(np.asarray(L0_attn_ln_w, f32).reshape(1, 16),
                            (128, 16))),
        "lnb_h": np.ascontiguousarray(
            np.broadcast_to(np.asarray(L0_attn_ln_b, f32).reshape(1, 16),
                            (128, 16))),
        "ones_h": np.ones((1, 128), f32),
        "ident_h": consts["ident"], "mask1_h": consts["mask1"],
        "mask2_h": consts["mask2"], "cm1_h": consts["cm1"],
        "cp1_h": consts["cp1"], "cm2_h": consts["cm2"], "cp2_h": consts["cp2"],
    }

    in_maps = []
    rowsets = []
    for k in range(NCORES):
        rows = np.arange(8192)
        sel = (rows % 512 >= 64 * k) & (rows % 512 < 64 * (k + 1))
        rout = rows[sel]
        rowsets.append(rout)
        m = dict(shared)
        m["xatt"] = np.ascontiguousarray(X[1024 * k:1024 * (k + 1)])
        m["xout"] = np.ascontiguousarray(X[rout])
        in_maps.append(m)

    nc = _get_prog()
    res = run_bass_kernel_spmd(nc, in_maps, core_ids=list(range(NCORES)))

    out = np.empty((8192, DM), dtype=f32)
    for k in range(NCORES):
        out[rowsets[k]] = res.results[k]["o"]
    return out.reshape(1, 8192, DM)


# version 09
# speedup vs baseline: 1.2047x; 1.0654x over previous
import numpy as np
import concourse.bass as bass
import concourse.mybir as mybir
from concourse.bass_utils import run_bass_kernel_spmd

F32 = mybir.dt.float32
F32R = mybir.dt.float32r
AX = mybir.AxisListType
AF = mybir.ActivationFunctionType
ALU = mybir.AluOpType

NCORES = 8
ROWS = 1024
DM = 2048
GROUPS = 32
GMICRO = 4096
NLOC = 131072
KC = 16
MC = 44
SCALE = float(8 ** -0.5)
EPS = 1e-5
VC = 4097.0
YW = 40


def _host_consts():
    c = {}
    c["ident"] = np.eye(128, dtype=np.float32)
    p = np.arange(128)
    c["mask1"] = (p[:, None] // 8 == p[None, :] // 8).astype(np.float32)
    c["mask2"] = (p[:, None] % 16 == p[None, :] % 16).astype(np.float32)

    def shift(step, win):
        m = np.zeros((128, 128), np.float32)
        for q in range(128):
            k = q + step
            if 0 <= k < 128 and win(k) == win(q):
                m[k, q] = 1.0
        return m
    c["cm1"] = shift(-1, lambda x: x // 8)
    c["cp1"] = shift(+1, lambda x: x // 8)
    c["cm2"] = shift(-16, lambda x: x % 16)
    c["cp2"] = shift(+16, lambda x: x % 16)
    return c


def _host_wq_wk(qkv_w):
    wq = np.zeros((2, 128, 40), np.float32)
    wk = np.zeros((2, 128, 40), np.float32)
    for par in range(2):
        for b in range(4):
            r0 = 32 * b + 16 * par
            wq[par, r0:r0 + 16, 0:8] = qkv_w[:, 0:8]
            wq[par, r0:r0 + 16, 32:40] = qkv_w[:, 8:16]
            wk[par, r0:r0 + 16, 0:8] = qkv_w[:, 16:24]
            wk[par, r0:r0 + 16, 32:40] = qkv_w[:, 24:32]
    return wq, wk


def _host_wb_skeleton():
    wb = np.zeros((2, 128, 128), np.float32)
    for par in range(2):
        for b in range(4):
            r0 = 32 * b + 16 * par
            wb[par, r0:r0 + 16, 48:64] = np.eye(16, dtype=np.float32)
    return wb


def build_program():
    nc = bass.Bass("TRN2", target_bir_lowering=False, debug=False)

    def din(name, shape):
        return nc.dram_tensor(name, list(shape), F32, kind="ExternalInput").ap()

    xatt = din("xatt", [ROWS, DM])
    xout = din("xout", [ROWS, DM])
    wgate = din("wgate", [MC, 128, KC, 128])
    wup = din("wup", [MC, 128, KC, 128])
    wdown = din("wdown", [MC, 128, KC, 128])
    gsc = din("gsc", [128, MC])
    usc = din("usc", [128, MC])
    dsc = din("dsc", [128, KC])
    wq_h = din("wq_h", [128, 2, 40])
    wk_h = din("wk_h", [128, 2, 40])
    wb_h = din("wb_h", [128, 2, 128])
    qkvw_h = din("qkvw_h", [16, 48])
    projw1_h = din("projw1_h", [8, 16])
    projw2_h = din("projw2_h", [8, 16])
    projb_h = din("projb_h", [1, 16])
    convw0_h = din("convw0_h", [8, 9])
    convb0_h = din("convb0_h", [8, 1])
    convw1_h = din("convw1_h", [8, 9])
    convb1_h = din("convb1_h", [8, 1])
    lnw_h = din("lnw_h", [128, 16])
    lnb_h = din("lnb_h", [128, 16])
    ones_h = din("ones_h", [1, 128])
    ident_h = din("ident_h", [128, 128])
    mask1_h = din("mask1_h", [128, 128])
    mask2_h = din("mask2_h", [128, 128])
    cm1_h = din("cm1_h", [128, 128])
    cp1_h = din("cp1_h", [128, 128])
    cm2_h = din("cm2_h", [128, 128])
    cp2_h = din("cp2_h", [128, 128])

    o = nc.dram_tensor("o", [ROWS, DM], F32, kind="ExternalOutput").ap()
    s_dram = nc.dram_tensor("s_dram", [16, NLOC], F32, kind="Internal").ap()
    wt_dram = nc.dram_tensor("wt_dram", [16, 128], F32, kind="Internal").ap()

    ctx = []

    def sb(name, shape, dtype=F32):
        g = nc.sbuf_tensor(name, list(shape), dtype)
        t = g.__enter__()
        ctx.append(g)
        return t

    def psalloc(name):
        g = nc.psum_tensor(name, [128, 512], F32)
        t = g.__enter__()
        ctx.append(g)
        return t

    ident = sb("ident", [128, 128])
    mask1 = sb("mask1", [128, 128])
    mask2 = sb("mask2", [128, 128])
    cm1 = sb("cm1", [128, 128])
    cp1 = sb("cp1", [128, 128])
    cm2 = sb("cm2", [128, 128])
    cp2 = sb("cp2", [128, 128])
    wq_s = sb("wq_s", [128, 2, 40])
    wk_s = sb("wk_s", [128, 2, 40])
    wb_s = sb("wb_s", [128, 2, 128])
    qkvw_s = sb("qkvw_s", [16, 48])
    projw1_s = sb("projw1_s", [8, 16])
    projw2_s = sb("projw2_s", [8, 16])
    projb_s = sb("projb_s", [1, 16])
    cw0_s = sb("cw0_s", [8, 9])
    cb0_s = sb("cb0_s", [8, 1])
    cw1_s = sb("cw1_s", [8, 9])
    cb1_s = sb("cb1_s", [8, 1])
    ones1 = sb("ones1", [1, 128])
    beff = sb("beff", [128, 16])
    gamma = sb("gamma", [128, 16])
    beta = sb("beta", [128, 16])
    gsc_s = sb("gsc_s", [128, MC])
    usc_s = sb("usc_s", [128, MC])
    dsc_s = sb("dsc_s", [128, KC])
    wvt = sb("wvt", [8, 16])
    winit = sb("winit", [16, 128])
    smallv = sb("smallv", [128, 64])

    arA = sb("arA", [128, 2048])
    arB = sb("arB", [128, 12288])
    arC = sb("arC", [128, 4096])
    arD = sb("arD", [128, 4096])
    arE = sb("arE", [128, 4096])
    arF = sb("arF", [128, 4096])
    xsr = sb("xsr", [128, 8192], F32R)
    hr = sb("hr", [128, 2048], F32R)
    wr = sb("wr", [128, 8192], F32R)

    xnfm = arA[:, :].rearrange("p (k r) -> p k r", k=KC)
    qs = arB[0:48, 0:4096]
    ks = arB[0:48, 4096:8192]
    vs = arB[:, 8192:12288]
    y2cm = arC[0:16, 0:4096]
    tocm = arC[0:16, 0:4096]
    v1tm = arD[:, 0:528].rearrange("p (t c) -> p t c", c=66)
    v2tm = arD[:, 528:1056].rearrange("p (t c) -> p t c", c=66)
    yslab = arD[:, 1056:2336].rearrange("p (t c) -> p t c", c=YW)
    tokr = arD[:, 2336:2848].rearrange("p (t c) -> p t c", c=16)
    pbuf = [arE[:, 0:512], arE[:, 512:1024]]
    ptm = [arE[:, 1024:1536], arE[:, 1536:2048]]
    zslab = arE[:, 2048:2560].rearrange("p (t c) -> p t c", c=16)
    zsq = arE[:, 2560:3072].rearrange("p (t c) -> p t c", c=16)
    tokout = arE[:, 3072:3584].rearrange("p (t c) -> p t c", c=16)
    zred = arE[:, 3584:3616]
    xa_tm = arF[:, 0:2048]
    sq_tm = arF[:, 2048:4096]

    xs_hi = xsr[:, 0:4096].rearrange("p (k r) -> p k r", k=KC)
    xs_lo = xsr[:, 4096:8192].rearrange("p (k r) -> p k r", k=KC)
    hhi = hr[:, 0:1024].rearrange("p (j r) -> p j r", j=4)
    hlo = hr[:, 1024:2048].rearrange("p (j r) -> p j r", j=4)
    hraw = [arB[:, 0:256], arB[:, 256:512]]
    hv1 = [arB[:, 512:768], arB[:, 768:1024]]
    dslab = arC[:, 0:4096].rearrange("p (k r) -> p k r", k=KC)
    xo_tm = arF[:, 0:2048]
    vhi_tm = arD[:, 0:2048]
    vlo_tm = arD[:, 2048:4096]
    x_tm = arD[:, 0:2048]
    wg_t = [wr[:, 0:2048].rearrange("p (k r) -> p k r", k=KC),
            wr[:, 2048:4096].rearrange("p (k r) -> p k r", k=KC)]
    wu_t = [wr[:, 4096:6144].rearrange("p (k r) -> p k r", k=KC),
            wr[:, 6144:8192].rearrange("p (k r) -> p k r", k=KC)]
    wd_t = wg_t
    g_tm = arF[:, 0:2048]
    s_tm = arF[:, 2048:4096]
    o_tm = s_tm

    y2z = [smallv[:, 0:16], smallv[:, 16:32]]
    rec4 = [smallv[:, 32:36], smallv[:, 36:40]]
    negm2 = [smallv[:, 40:44], smallv[:, 44:48]]
    negm1 = [smallv[:, 48:52], smallv[:, 52:56]]
    mu_c = smallv[:, 56:57]
    rs_c = smallv[:, 57:58]

    pA = psalloc("pA")
    pB = psalloc("pB")
    pC = psalloc("pC")
    pD = psalloc("pD")
    pE = psalloc("pE")
    pF = psalloc("pF")
    pG = psalloc("pG")
    pH = psalloc("pH")

    barg = nc.semaphore("bar")
    barsem = barg.__enter__()
    ctx.append(barg)
    gsems = []
    for gi in range(2):
        gg = nc.semaphore(f"gbar{gi}")
        gsems.append(gg.__enter__())
        ctx.append(gg)
    esems = {}
    for en in ("tensor", "vector", "scalar", "gpsimd"):
        gg = nc.semaphore(f"e_{en}")
        esems[en] = gg.__enter__()
        ctx.append(gg)
    blkg = nc.Block()
    block = blkg.__enter__()

    phases = []
    phase_deps = []

    def phase(dep=None, **fns):
        phases.append(fns)
        phase_deps.append(len(phases) - 2 if dep is None else dep)
        return len(phases) - 1

    def dma_fn(n):
        def wrap(fn):
            fn.dma_incs = n
            return fn
        return wrap

    def sw_dma_fn(n, gi=0):
        def wrap(fn):
            fn.sw_dma_incs = n
            fn.sw_sem = gi
            return fn
        return wrap


    @dma_fn(24)
    def init_loads(e):
        for src, dst in (
            (ident_h, ident[:]), (mask1_h, mask1[:]), (mask2_h, mask2[:]),
            (cm1_h, cm1[:]), (cp1_h, cp1[:]), (cm2_h, cm2[:]), (cp2_h, cp2[:]),
            (wq_h, wq_s[:]), (wk_h, wk_s[:]), (wb_h, wb_s[:]),
            (qkvw_h, qkvw_s[:]), (projw1_h, projw1_s[:]),
            (projw2_h, projw2_s[:]), (projb_h, projb_s[:]),
            (convw0_h, cw0_s[:]), (convb0_h, cb0_s[:]),
            (convw1_h, cw1_s[:]), (convb1_h, cb1_s[:]),
            (lnw_h, gamma[:]), (lnb_h, beta[:]), (ones_h, ones1[:]),
            (gsc, gsc_s[:]), (usc, usc_s[:]), (dsc, dsc_s[:]),
        ):
            e.dma_start(out=dst, in_=src).then_inc(barsem, 16)
    phase(sync=init_loads)

    def init_memset(e):
        e.memset(winit[:], 0.0)
        e.memset(v1tm[:, :, 16:17], 1.0)
        e.memset(v2tm[:, :, 16:17], 1.0)
        e.memset(y2cm[:, :], 0.0)
    phase(gpsimd=init_memset)

    def init_scaleq(e):
        e.tensor_scalar_mul(wq_s[:, :, :], wq_s[:, :, :], SCALE)
    phase(vector=init_scaleq)

    def init_wvT(e):
        e.transpose(pA[0:8, 0:16], qkvw_s[0:16, 32:40], ident[0:16, 0:16])
        e.transpose(pA[0:8, 16:32], qkvw_s[0:16, 40:48], ident[0:16, 0:16])
    phase(tensor=init_wvT)

    def init_wvT_ev(e):
        e.tensor_copy(winit[0:8, 0:32], pA[0:8, 0:32])
    phase(vector=init_wvT_ev)

    jobs = [
        (0,   (0, 16),  None,            projw1_s, True, True),
        (16,  (0, 16),  cw0_s[:, 3:4],   projw1_s, True, True),
        (32,  (0, 16),  cw0_s[:, 5:6],   projw1_s, True, True),
        (64,  (16, 32), None,            projw2_s, True, True),
        (80,  (16, 32), cw1_s[:, 1:2],   projw2_s, True, True),
        (96,  (16, 32), cw1_s[:, 7:8],   projw2_s, True, True),
        (112, (0, 16),  cw0_s[:, 4:5],   projw1_s, True, False),
        (112, (16, 32), cw1_s[:, 4:5],   projw2_s, False, True),
    ]
    for col, (c0, c1), diag, pr, st, sp in jobs:
        def vj(e, c0=c0, c1=c1, diag=diag):
            if diag is None:
                e.tensor_copy(wvt[:], winit[0:8, c0:c1])
            else:
                e.tensor_scalar(wvt[:], winit[0:8, c0:c1], diag, None, ALU.mult)
        phase(vector=vj)

        def tj(e, col=col, pr=pr, st=st, sp=sp):
            e.matmul(pB[0:16, col:col + 16], wvt[:], pr[:],
                     start=st, stop=sp)
        phase(tensor=tj)

    def init_beff_mm(e):
        e.matmul(pC[0:1, 0:16], cb0_s[:], projw1_s[:], start=True, stop=False)
        e.matmul(pC[0:1, 0:16], cb1_s[:], projw2_s[:], start=False, stop=True)
    phase(tensor=init_beff_mm)

    def init_beff_add(e):
        e.tensor_add(winit[0:1, 48:64], pC[0:1, 0:16], projb_s[:])
    phase(vector=init_beff_add)

    def init_beff_bc(e):
        e.matmul(pD[0:128, 0:16], ones1[:], winit[0:1, 48:64],
                 start=True, stop=True)
    phase(tensor=init_beff_bc)

    def init_beff_ev(e):
        e.tensor_copy(beff[:], pD[0:128, 0:16])
    phase(vector=init_beff_ev)

    def init_vt_evac(e):
        e.tensor_copy(winit[:, 0:48], pB[0:16, 0:48])
        e.tensor_copy(winit[:, 64:128], pB[0:16, 64:128])
    phase(vector=init_vt_evac)

    @dma_fn(1)
    def init_vt_out(e):
        e.dma_start(out=wt_dram, in_=winit[:]).then_inc(barsem, 16)
    phase(sync=init_vt_out)

    @dma_fn(16)
    def init_vt_in(e):
        for par in range(2):
            for b in range(4):
                r0 = 32 * b + 16 * par
                e.dma_start(out=wb_s[r0:r0 + 16, par, 0:48],
                            in_=wt_dram[:, 0:48]).then_inc(barsem, 16)
                e.dma_start(out=wb_s[r0:r0 + 16, par, 64:128],
                            in_=wt_dram[:, 64:128]).then_inc(barsem, 16)
    phase(sync=init_vt_in)

    def ln_tile(src_dram, dst_tm, tile):
        @dma_fn(1)
        def load(e, tile=tile):
            e.dma_start(out=dst_tm, in_=src_dram[tile * 128:(tile + 1) * 128, :]
                        ).then_inc(barsem, 16)
        phase(sync=load)

        def mean(e):
            e.reduce_sum(mu_c[:], dst_tm, axis=AX.X, negate=True)
            e.drain()
            e.tensor_scalar_mul(mu_c[:], mu_c[:], 1.0 / DM)
            e.drain()
            e.tensor_scalar(dst_tm, dst_tm, mu_c[:], None, ALU.add)
        phase(vector=mean)

        def sq(e):
            e.activation(sq_tm[:], dst_tm, AF.Square)
        phase(scalar=sq)

        def var(e):
            e.reduce_sum(rs_c[:], sq_tm[:], axis=AX.X)
            e.drain()
            e.tensor_scalar(rs_c[:], rs_c[:], 1.0 / DM, EPS, ALU.mult, ALU.add)
        phase(vector=var)

        def rsq(e):
            e.activation(rs_c[:], rs_c[:], AF.Ln)
            e.drain()
            e.activation(rs_c[:], rs_c[:], AF.Exp, scale=-0.5)
        phase(scalar=rsq)

        def fin(e):
            e.tensor_scalar(dst_tm, dst_tm, rs_c[:], None, ALU.mult)
        phase(vector=fin)

    qps = (pA, pB, pC, pD)
    kps = (pE, pF, pG, pH)
    for m in range(GROUPS):
        if m % 4 == 0:
            tile8 = m // 4
            ln_tile(xatt, xa_tm[:, :], tile8)
            for jh in range(4):
                def tp(e, jh=jh):
                    for j2 in range(4):
                        j = jh * 4 + j2
                        e.transpose(pA[:, j2 * 128:(j2 + 1) * 128],
                                    xa_tm[:, j * 128:(j + 1) * 128], ident[:])
                phase(tensor=tp)

                def tpe(e, jh=jh):
                    for j2 in range(4):
                        j = jh * 4 + j2
                        e.tensor_copy(xnfm[:, j, :],
                                      pA[:, j2 * 128:(j2 + 1) * 128])
                phase(vector=tpe)

        def rhs_ap(blkb, m=m):
            mm4 = m % 4
            return xnfm[32 * blkb:32 * blkb + 32, :, mm4 * 32:mm4 * 32 + 32]

        for par in range(2):
            def qkv_mm(e, par=par, rhs_ap=rhs_ap):
                for blkb in range(4):
                    tp = (32 * blkb, 0)
                    r = rhs_ap(blkb)
                    e.matmul(qps[blkb][0:40, :], wq_s[32 * blkb:32 * blkb + 32, par, :],
                             r, start=True, stop=True, tile_position=tp)
                    e.matmul(kps[blkb][0:40, :], wk_s[32 * blkb:32 * blkb + 32, par, :],
                             r, start=True, stop=True, tile_position=tp)
            phase(tensor=qkv_mm)

            def qkv_ev(e, par=par):
                for blkb in range(4):
                    sp = 2 * blkb + par
                    for slab, psl in ((qs, qps[blkb]), (ks, kps[blkb])):
                        d1 = slab[0:8, :].rearrange(
                            "p (i b s) -> p i b s", i=32, b=16)[
                            :, :, :, sp].transpose([0, 2, 1])
                        e.tensor_copy(d1, psl[0:8, :].rearrange(
                            "p (b i) -> p b i", b=16))
                        d2 = slab[32:40, :].rearrange(
                            "p (b s r g) -> p b s r g", b=16, s=8, r=4)[
                            :, :, sp, :, :].transpose([0, 1, 3, 2])
                        e.tensor_copy(d2, psl[32:40, :].rearrange(
                            "p (b g r) -> p b g r", b=16, g=8))
            phase(vector=qkv_ev)

        for par in range(2):
            def qkvB_mm(e, par=par, rhs_ap=rhs_ap):
                for blkb in range(4):
                    e.matmul(qps[blkb][:, :], wb_s[32 * blkb:32 * blkb + 32, par, :],
                             rhs_ap(blkb), start=True, stop=True,
                             tile_position=(32 * blkb, 0))
            phase(tensor=qkvB_mm)

            def qkvB_ev(e, par=par):
                for blkb in range(4):
                    sp = 2 * blkb + par
                    d1 = vs[0:64, :].rearrange(
                        "p (i b s) -> p i b s", i=32, b=16)[
                        :, :, :, sp].transpose([0, 2, 1])
                    e.tensor_copy(d1, qps[blkb][0:64, :].rearrange(
                        "p (b i) -> p b i", b=16))
                    d2 = vs[64:128, :].rearrange(
                        "p (b s r g) -> p b s r g", b=16, s=8, r=4)[
                        :, :, sp, :, :].transpose([0, 1, 3, 2])
                    e.tensor_copy(d2, qps[blkb][64:128, :].rearrange(
                        "p (b g r) -> p b g r", b=16, g=8))
            phase(vector=qkvB_ev)

        Sb = (pA, pB)
        PTb = (pC, pD)
        VTb = (pE, pF)
        Yb = (pG, pH)

        def run_pass(qrow, krow, vrow0, vrow1, vident, vwork, negmP, is_b1):
            p_mm = {}
            p_ng = {}
            p_exp = {}
            p_ptr = {}
            p_pev = {}
            p_av = {}
            p_tail = {}
            p_rc = {}
            p_dvt = {}
            p_y2ev = {}

            def st_mm(st):
                par = st % 2
                tl = [4 * st + i for i in range(4)]

                def mmS(e, tl=tl, par=par):
                    for i, t2 in enumerate(tl):
                        e.matmul(Sb[par][:, i * 128:(i + 1) * 128],
                                 qs[qrow[0]:qrow[1], t2 * 128:(t2 + 1) * 128],
                                 ks[qrow[0]:qrow[1], t2 * 128:(t2 + 1) * 128],
                                 start=True, stop=True,
                                 tile_position=(qrow[0] - qrow[0] % 32, 0)
                                 if qrow[0] >= 32 else None)
                p_mm[st] = phase(dep=p_exp.get(st - 2), tensor=mmS)

            def st_ng(st):
                par = st % 2

                def negmax(e, par=par):
                    e.reduce_max(negmP[par][:],
                                 Sb[par][:, :].rearrange(
                                     "p (t x) -> p t x", t=4),
                                 axis=AX.X, negate=True)
                p_ng[st] = phase(dep=p_mm[st], vector=negmax)

            def st_exp(st):
                par = st % 2

                def expS(e, par=par):
                    for i in range(4):
                        e.activation(pbuf[par][:, i * 128:(i + 1) * 128],
                                     Sb[par][:, i * 128:(i + 1) * 128], AF.Exp,
                                     bias=negmP[par][:, i:i + 1])
                dep = p_ng[st]
                if st - 2 in p_ptr:
                    dep = max(dep, p_ptr[st - 2])
                p_exp[st] = phase(dep=dep, scalar=expS)

            def st_ptr(st):
                par = st % 2
                tl = [4 * st + i for i in range(4)]

                def ptrans(e, tl=tl, par=par):
                    for i in range(4):
                        e.transpose(PTb[par][:, i * 128:(i + 1) * 128],
                                    pbuf[par][:, i * 128:(i + 1) * 128],
                                    ident[:])
                    for i, t2 in enumerate(tl):
                        e.transpose(VTb[par][:, i * 128:i * 128 + 64],
                                    vs[vrow0:vrow1, t2 * 128:(t2 + 1) * 128],
                                    vident)
                dep = p_exp[st]
                if st - 2 in p_pev:
                    dep = max(dep, p_pev[st - 2])
                if st - 2 in p_y2ev:
                    dep = max(dep, p_y2ev[st - 2])
                p_ptr[st] = phase(dep=dep, tensor=ptrans)

            def st_pev(st):
                par = st % 2
                tl = [4 * st + i for i in range(4)]

                def pt_evac(e, tl=tl, par=par):
                    e.tensor_mul(ptm[par][:, :].rearrange(
                        "p (t x) -> p t x", t=4),
                        PTb[par][:, :].rearrange("p (t x) -> p t x", t=4),
                        mask1[:].unsqueeze(1).broadcast_to([128, 4, 128]))
                    for i, ip in enumerate(tl):
                        if is_b1:
                            e.tensor_add(vwork[:, par * 4 + i, 0:16],
                                         VTb[par][:, i * 128:i * 128 + 16],
                                         beff[:])
                            e.tensor_copy(vwork[:, par * 4 + i, 17:49],
                                          VTb[par][:, i * 128 + 16:i * 128 + 48])
                            e.tensor_copy(tokr[:, ip, :],
                                          VTb[par][:, i * 128 + 48:i * 128 + 64])
                        else:
                            e.tensor_copy(vwork[:, par * 4 + i, 0:16],
                                          VTb[par][:, i * 128:i * 128 + 16])
                            e.tensor_copy(vwork[:, par * 4 + i, 17:65],
                                          VTb[par][:, i * 128 + 16:i * 128 + 64])
                dep = p_ptr[st]
                if st - 2 in p_av:
                    dep = max(dep, p_av[st - 2])
                p_pev[st] = phase(dep=dep, vector=pt_evac)

            def st_av(st):
                par = st % 2
                tl = [4 * st + i for i in range(4)]

                def av(e, tl=tl, par=par):
                    for i, ip in enumerate(tl):
                        lhs = ptm[par][:, i * 128:(i + 1) * 128]
                        e.matmul(Yb[par][:, i * YW:i * YW + 17], lhs,
                                 vwork[:, par * 4 + i, 0:17],
                                 start=True, stop=True)
                        e.matmul(Yb[par][:, i * YW + 20:i * YW + 36], cm1[:],
                                 vwork[:, par * 4 + i, 17:33],
                                 start=True, stop=False)
                        e.matmul(Yb[par][:, i * YW + 20:i * YW + 36], cp1[:],
                                 vwork[:, par * 4 + i, 33:49],
                                 start=False, stop=False)
                        if not is_b1:
                            e.matmul(Yb[par][:, i * YW + 20:i * YW + 36],
                                     ident[:], vwork[:, par * 4 + i, 49:65],
                                     start=False, stop=True)
                        else:
                            e.matmul(Yb[par][:, i * YW + 20:i * YW + 36],
                                     y2cm[:, ip * 128:(ip + 1) * 128],
                                     ident[0:16, 0:16], start=False, stop=True,
                                     is_transpose=True)
                dep = p_pev[st]
                if st - 2 in p_tail:
                    dep = max(dep, p_tail[st - 2])
                p_av[st] = phase(dep=dep, tensor=av)

            def st_tail_b1(st):
                par = st % 2
                tl = [4 * st + i for i in range(4)]

                def y1ev(e, tl=tl, par=par):
                    for i, ip in enumerate(tl):
                        e.copy(yslab[:, ip, 0:17],
                               Yb[par][:, i * YW:i * YW + 17])
                        e.copy(yslab[:, ip, 20:36],
                               Yb[par][:, i * YW + 20:i * YW + 36])
                p_tail[st] = phase(dep=p_av[st], scalar=y1ev)

            def st_rc(st):
                par = st % 2

                def recs2(e, par=par):
                    dv = Yb[par][:, 0:4 * YW].rearrange(
                        "p (t c) -> p t c", c=YW)[:, :, 16]
                    e.activation(rec4[par][:], dv, AF.Ln)
                    e.drain()
                    e.activation(rec4[par][:], rec4[par][:], AF.Exp,
                                 scale=-1.0)
                p_rc[st] = phase(dep=p_av[st], scalar=recs2)

            def st_dvt(st):
                par = st % 2
                prev = p_rc[st]
                for i in range(4):
                    def y2div(e, i=i, par=par):
                        e.tensor_scalar(y2z[par][:],
                                        Yb[par][:, i * YW:i * YW + 16],
                                        rec4[par][:, i:i + 1], None, ALU.mult)
                        e.drain()
                        e.tensor_add(y2z[par][:], y2z[par][:],
                                     Yb[par][:, i * YW + 20:i * YW + 36])
                    dep = prev
                    if i == 0 and st - 2 in p_tail:
                        dep = max(dep, p_tail[st - 2])
                    p_dv = phase(dep=dep, vector=y2div)

                    def y2t(e, i=i, par=par):
                        e.transpose(PTb[par][0:16, i * 128:(i + 1) * 128],
                                    y2z[par][:], ident[:])
                    prev = phase(dep=p_dv, tensor=y2t)
                p_tail[st] = prev

            def st_y2ev(st):
                par = st % 2
                tl = [4 * st + i for i in range(4)]

                def y2ev(e, tl=tl, par=par):
                    for i, t2 in enumerate(tl):
                        b0, h = t2 // 2, t2 % 2
                        dst = y2cm[:, :].rearrange(
                            "p (g r b s) -> p g r b s", g=8, r=4, b=16)[
                            :, :, :, b0, 4 * h:4 * h + 4].transpose(
                            [0, 3, 2, 1])
                        e.tensor_copy(
                            dst, PTb[par][0:16, i * 128:(i + 1) * 128]
                            .rearrange("p (s r g) -> p s r g", s=4, r=4))
                p_y2ev[st] = phase(dep=p_tail[st], vector=y2ev)

            if is_b1:
                stages = [st_mm, st_ng, st_exp, st_ptr, st_pev, st_av,
                          st_tail_b1]
            else:
                stages = [st_mm, st_ng, st_exp, st_ptr, st_pev, st_av,
                          st_rc, st_dvt, st_y2ev]
            ns = len(stages)
            II = 3
            for t in range(II * 7 + ns):
                for gidx in range(ns):
                    if (t - gidx) % II == 0:
                        st = (t - gidx) // II
                        if 0 <= st < 8:
                            stages[gidx](st)

        run_pass((32, 40), None, 64, 128, ident[64:128, 64:128], v2tm,
                 negm2, False)
        run_pass((0, 8), None, 0, 64, ident[0:64, 0:64], v1tm, negm1, True)

        def ln16r(e):
            e.activation(zred[:], yslab[:, :, 16], AF.Ln)
            e.drain()
            e.activation(zred[:], zred[:], AF.Exp, scale=-1.0)
        phase(scalar=ln16r)

        def ln16a(e):
            e.tensor_mul(zslab[:, :, :], yslab[:, :, 0:16],
                         zred[:].unsqueeze(2).broadcast_to(
                             [128, 32, 16]))
            e.drain()
            e.tensor_add(zslab[:, :, :], zslab[:, :, :], yslab[:, :, 20:36])
            e.drain()
            e.reduce_sum(zred[:], zslab[:, :, :], axis=AX.X, negate=True)
            e.drain()
            e.tensor_scalar_mul(zred[:], zred[:], 1.0 / 16.0)
            e.drain()
            e.tensor_add(zslab[:, :, :], zslab[:, :, :],
                         zred[:].unsqueeze(2).broadcast_to(
                             [128, 32, 16]))
        phase(vector=ln16a)

        def ln16sq(e):
            e.activation(zsq[:, :, :], zslab[:, :, :], AF.Square)
        phase(scalar=ln16sq)

        def ln16b(e):
            e.reduce_sum(zred[:], zsq[:, :, :], axis=AX.X)
            e.drain()
            e.tensor_scalar(zred[:], zred[:], 1.0 / 16.0, EPS, ALU.mult, ALU.add)
        phase(vector=ln16b)

        def ln16c(e):
            e.activation(zred[:], zred[:], AF.Ln)
            e.drain()
            e.activation(zred[:], zred[:], AF.Exp, scale=-0.5)
        phase(scalar=ln16c)

        def ln16d(e):
            e.tensor_mul(zslab[:, :, :], zslab[:, :, :],
                         zred[:].unsqueeze(2).broadcast_to(
                             [128, 32, 16]))
            e.drain()
            e.tensor_mul(zslab[:, :, :], zslab[:, :, :],
                         gamma[:].unsqueeze(1).broadcast_to(
                             [128, 32, 16]))
            e.drain()
            e.tensor_add(zslab[:, :, :], zslab[:, :, :],
                         beta[:].unsqueeze(1).broadcast_to(
                             [128, 32, 16]))
            e.drain()
            e.tensor_add(tokout[:, :, :], zslab[:, :, :], tokr[:, :, :])
        phase(vector=ln16d)

        for q in range(4):
            def toT(e, q=q):
                for j in range(4):
                    e.transpose(pA[0:16, j * 128:(j + 1) * 128],
                                tokout[:, q * 8 + j, :], ident[:])
                for j in range(4):
                    e.transpose(pB[0:16, j * 128:(j + 1) * 128],
                                tokout[:, q * 8 + 4 + j, :], ident[:])
            phase(tensor=toT)

            def toEv(e, q=q):
                e.tensor_copy(tocm[:, q * 1024:q * 1024 + 512], pA[0:16, :])
                e.tensor_copy(tocm[:, q * 1024 + 512:(q + 1) * 1024], pB[0:16, :])
            phase(vector=toEv)

        @dma_fn(1)
        def scat(e, m=m):
            e.dma_start(out=s_dram[:, m * GMICRO:(m + 1) * GMICRO],
                        in_=tocm[:, :]).then_inc(barsem, 16)
        phase(sync=scat)


    allp = (pA, pB, pC, pD, pE, pF, pG, pH)
    for tq in range(4):
        for tile in range(2):
            ln_tile(xout, xo_tm, tq * 2 + tile)

            def velt(e):
                e.tensor_scalar_mul(sq_tm[:], xo_tm[:], VC)
                e.drain()
                e.tensor_sub(vlo_tm[:], sq_tm[:], xo_tm[:])
                e.drain()
                e.tensor_sub(vhi_tm[:], sq_tm[:], vlo_tm[:])
                e.drain()
                e.tensor_sub(vlo_tm[:], xo_tm[:], vhi_tm[:])
            phase(vector=velt)

            for jh in range(4):
                def ftp(e, jh=jh):
                    for j2 in range(4):
                        j = jh * 4 + j2
                        e.transpose(pA[:, j2 * 128:(j2 + 1) * 128],
                                    vhi_tm[:, j * 128:(j + 1) * 128], ident[:])
                        e.transpose(pB[:, j2 * 128:(j2 + 1) * 128],
                                    vlo_tm[:, j * 128:(j + 1) * 128], ident[:])
                phase(tensor=ftp)

                def ftpe(e, tile=tile, jh=jh):
                    for j2 in range(4):
                        j = jh * 4 + j2
                        e.tensor_copy(xs_hi[:, j, tile * 128:(tile + 1) * 128],
                                      pA[:, j2 * 128:(j2 + 1) * 128])
                        e.tensor_copy(xs_lo[:, j, tile * 128:(tile + 1) * 128],
                                      pB[:, j2 * 128:(j2 + 1) * 128])
                phase(vector=ftpe)

        gu_banks = ((pA, pB), (pC, pD))
        ph_wload = {}
        ph_gu = {}
        ph_hmake = {-1: None, -2: None}
        ph_dacc_last = None
        ph_dmm_last = None
        for jb in range(11):
            for jj in range(4):
                j = jb * 4 + jj

                @sw_dma_fn(2, j % 2)
                def wload(e, j=j):
                    e.dma_start(out=wg_t[j % 2][:, :, :], in_=wgate[j, :, :, :]
                                ).then_inc(gsems[j % 2], 16)
                    e.dma_start(out=wu_t[j % 2][:, :, :], in_=wup[j, :, :, :]
                                ).then_inc(gsems[j % 2], 16)
                dep = ph_gu.get(j - 2)
                if dep is not None and ph_dmm_last is not None:
                    dep = max(dep, ph_dmm_last)
                ph_wload[j] = phase(dep=dep, gpsimd=wload)

                def gu_mm(e, j=j):
                    bg, bu = gu_banks[j % 2]
                    for k in range(KC):
                        e.matmul(bg[:, 0:256], wg_t[j % 2][:, k, :],
                                 xs_hi[:, k, :], start=(k == 0), stop=False)
                        e.matmul(bg[:, 0:256], wg_t[j % 2][:, k, :],
                                 xs_lo[:, k, :], start=False, stop=(k == KC - 1))
                    for k in range(KC):
                        e.matmul(bu[:, 0:256], wu_t[j % 2][:, k, :],
                                 xs_hi[:, k, :], start=(k == 0), stop=False)
                        e.matmul(bu[:, 0:256], wu_t[j % 2][:, k, :],
                                 xs_lo[:, k, :], start=False, stop=(k == KC - 1))
                gdep = ph_wload[j]
                if ph_hmake.get(j - 2) is not None:
                    gdep = max(gdep, ph_hmake[j - 2])
                if ph_dacc_last is not None:
                    gdep = max(gdep, ph_dacc_last)
                ph_gu[j] = phase(dep=gdep, tensor=gu_mm)

                def gsilu(e, j=j):
                    bg, bu = gu_banks[j % 2]
                    e.activation(hv1[j % 2][:], bg[:, 0:256], AF.Silu,
                                 scale=gsc_s[:, j:j + 1])
                phase(dep=ph_gu[j], scalar=gsilu)

                def hmake(e, j=j, jj=jj):
                    bg, bu = gu_banks[j % 2]
                    hw = hraw[j % 2]
                    e.scalar_tensor_tensor(hw[:], bu[:, 0:256],
                                           usc_s[:, j:j + 1], hv1[j % 2][:],
                                           ALU.mult, ALU.mult)
                    e.drain()
                    e.tensor_scalar_mul(sq_tm[:, 0:256], hw[:], VC)
                    e.drain()
                    e.tensor_sub(sq_tm[:, 256:512], sq_tm[:, 0:256], hw[:])
                    e.drain()
                    e.tensor_sub(hhi[:, jj, :], sq_tm[:, 0:256],
                                 sq_tm[:, 256:512])
                    e.drain()
                    e.tensor_sub(hlo[:, jj, :], hw[:], hhi[:, jj, :])
                ph_hmake[j] = phase(vector=hmake)

            ph_dload = {}
            ph_dmm = {}
            for jj in range(4):
                j = jb * 4 + jj

                @sw_dma_fn(1, j % 2)
                def dload(e, j=j):
                    e.dma_start(out=wd_t[j % 2][:, :, :], in_=wdown[j, :, :, :]
                                ).then_inc(gsems[j % 2], 16)
                ddep = ph_dmm.get(jj - 2)
                ph_dload[jj] = phase(dep=ddep, gpsimd=dload)

                def d_mm(e, jj=jj, j=j):
                    for oc in range(KC):
                        pp = allp[oc % 8]
                        c0 = (oc // 8) * 256
                        e.matmul(pp[:, c0:c0 + 256], wd_t[j % 2][:, oc, :],
                                 hhi[:, jj, :],
                                 start=(jj == 0 and oc < 8), stop=False)
                        e.matmul(pp[:, c0:c0 + 256], wd_t[j % 2][:, oc, :],
                                 hlo[:, jj, :], start=False,
                                 stop=(jj == 3 and oc >= 8))
                ph_dmm[jj] = phase(dep=ph_dload[jj], tensor=d_mm)
                ph_dmm_last = ph_dmm[jj]

            def dacc(e, jb=jb):
                for oc in range(KC):
                    pp = allp[oc % 8]
                    c0 = (oc // 8) * 256
                    if jb == 0:
                        e.tensor_copy(dslab[:, oc, :], pp[:, c0:c0 + 256])
                    else:
                        e.tensor_add(dslab[:, oc, :], dslab[:, oc, :],
                                     pp[:, c0:c0 + 256])
            ph_dacc_last = phase(dep=ph_dmm[3], vector=dacc)

        def gfin(e):
            for oc in range(KC):
                e.activation(dslab[:, oc, :], dslab[:, oc, :], AF.Silu,
                             scale=dsc_s[:, oc:oc + 1])
        phase(scalar=gfin)

        for tb in range(2):
            row0 = tq * 256 + tb * 128

            def gT(e, tb=tb):
                for oc in range(4):
                    e.transpose(pA[:, oc * 128:(oc + 1) * 128],
                                dslab[:, oc, tb * 128:(tb + 1) * 128], ident[:])
                for oc in range(4):
                    e.transpose(pB[:, oc * 128:(oc + 1) * 128],
                                dslab[:, 4 + oc, tb * 128:(tb + 1) * 128],
                                ident[:])
                for oc in range(4):
                    e.transpose(pC[:, oc * 128:(oc + 1) * 128],
                                dslab[:, 8 + oc, tb * 128:(tb + 1) * 128],
                                ident[:])
                for oc in range(4):
                    e.transpose(pD[:, oc * 128:(oc + 1) * 128],
                                dslab[:, 12 + oc, tb * 128:(tb + 1) * 128],
                                ident[:])
            phase(tensor=gT)

            def gEv(e):
                e.tensor_copy(g_tm[:, 0:512], pA[:, :])
                e.tensor_copy(g_tm[:, 512:1024], pB[:, :])
                e.tensor_copy(g_tm[:, 1024:1536], pC[:, :])
                e.tensor_copy(g_tm[:, 1536:2048], pD[:, :])
            phase(vector=gEv)

            @dma_fn(2)
            def fload(e, row0=row0):
                e.dma_start(out=x_tm[:], in_=xout[row0:row0 + 128, :]
                            ).then_inc(barsem, 16)
                ta = row0 // 64
                e.dma_start(
                    out=s_tm[:],
                    in_=s_dram.rearrange("c (r j) -> c r j", j=DM)[
                        ta:ta + 2, 0:64, :]).then_inc(barsem, 16)
            phase(sync=fload)

            def fuse(e):
                e.tensor_mul(o_tm[:], s_tm[:], g_tm[:])
                e.drain()
                e.tensor_add(o_tm[:], o_tm[:], x_tm[:])
            phase(vector=fuse)

            @dma_fn(1)
            def fstore(e, row0=row0):
                e.dma_start(out=o[row0:row0 + 128, :], in_=o_tm[:]
                            ).then_inc(barsem, 16)
            phase(sync=fstore)

    if _PHASE_LIMIT is not None:
        del phases[_PHASE_LIMIT:]
    allsems = ([esems["tensor"], esems["vector"], esems["scalar"],
                esems["gpsimd"], barsem] + gsems)
    eidx = {"tensor": 0, "vector": 1, "scalar": 2, "gpsimd": 3}
    NS = 7
    cum = []
    tot = [0] * NS
    for p in phases:
        for en, fn in p.items():
            d = getattr(fn, "dma_incs", 0)
            g = getattr(fn, "sw_dma_incs", 0)
            if g:
                tot[5 + getattr(fn, "sw_sem", 0)] += 16 * g
            elif d:
                tot[4] += 16 * d
            else:
                tot[eidx[en]] += 1
        cum.append(tuple(tot))
    finals = tuple(tot)

    for ename in ("sync", "gpsimd", "tensor", "vector", "scalar"):
        dec = getattr(block, ename)

        def mk(ename=ename):
            def body(eng):
                lastw = [0] * NS
                for pi, p in enumerate(phases):
                    if ename not in p:
                        continue
                    dep = phase_deps[pi]
                    if dep is not None and dep >= len(phases):
                        dep = len(phases) - 1
                    tgt = cum[dep] if (dep is not None and dep >= 0) \
                        else (0,) * NS
                    for si in range(NS):
                        t2 = tgt[si]
                        if ename in eidx and si == eidx[ename]:
                            continue
                        if t2 > lastw[si]:
                            eng.wait_ge(allsems[si], t2)
                            lastw[si] = t2
                    fn = p[ename]
                    fn(eng)
                    if (getattr(fn, "dma_incs", 0) == 0
                            and getattr(fn, "sw_dma_incs", 0) == 0):
                        eng.drain().then_inc(esems[ename], 1)
                if ename == "sync":
                    for si in range(NS):
                        if finals[si] > 0:
                            eng.wait_ge(allsems[si], finals[si])
            return body
        dec(mk())

    blkg.__exit__(None, None, None)
    for g in reversed(ctx):
        g.__exit__(None, None, None)
    nc._phases_debug = (list(phase_deps),
                        [[(en, fn.__name__) for en, fn in p.items()]
                         for p in phases], cum)
    return nc


_PHASE_LIMIT = None
_PROG = None


def _get_prog():
    global _PROG
    if _PROG is None:
        _PROG = build_program()
    return _PROG


def kernel(L0_attn_x, L0_attn_qkv_w, L0_attn_conv_w0, L0_attn_conv_b0,
           L0_attn_conv_w1, L0_attn_conv_b1, L0_attn_proj_w, L0_attn_proj_b,
           L0_attn_ln_w, L0_attn_ln_b, L0_ffn_gate_w, L0_ffn_up_w,
           L0_ffn_down_w, L0_ffn_gate_s, L0_ffn_up_s, L0_ffn_down_s):
    f32 = np.float32
    X = np.ascontiguousarray(np.asarray(L0_attn_x, dtype=f32).reshape(8192, DM))
    qkv_w = np.asarray(L0_attn_qkv_w, dtype=f32)
    consts = _host_consts()
    wq, wk = _host_wq_wk(qkv_w)
    wb = _host_wb_skeleton()

    gate_w = np.asarray(L0_ffn_gate_w, dtype=f32)
    up_w = np.asarray(L0_ffn_up_w, dtype=f32)
    down_w = np.asarray(L0_ffn_down_w, dtype=f32)
    WG = np.ascontiguousarray(
        gate_w.reshape(MC, 128, KC, 128).transpose(0, 3, 2, 1))
    WU = np.ascontiguousarray(
        up_w.reshape(MC, 128, KC, 128).transpose(0, 3, 2, 1))
    WD = np.ascontiguousarray(
        down_w.reshape(KC, 128, MC, 128).transpose(2, 3, 0, 1))

    gsch = np.ascontiguousarray(
        np.asarray(L0_ffn_gate_s, dtype=f32).reshape(MC, 128).T)
    usch = np.ascontiguousarray(
        np.asarray(L0_ffn_up_s, dtype=f32).reshape(MC, 128).T)
    dsch = np.ascontiguousarray(
        np.asarray(L0_ffn_down_s, dtype=f32).reshape(KC, 128).T)

    shared = {
        "wgate": WG, "wup": WU, "wdown": WD,
        "gsc": gsch, "usc": usch, "dsc": dsch,
        "wq_h": np.ascontiguousarray(wq.transpose(1, 0, 2)),
        "wk_h": np.ascontiguousarray(wk.transpose(1, 0, 2)),
        "wb_h": np.ascontiguousarray(wb.transpose(1, 0, 2)),
        "qkvw_h": qkv_w,
        "projw1_h": np.ascontiguousarray(np.asarray(L0_attn_proj_w, f32)[0:8]),
        "projw2_h": np.ascontiguousarray(np.asarray(L0_attn_proj_w, f32)[8:16]),
        "projb_h": np.asarray(L0_attn_proj_b, f32).reshape(1, 16),
        "convw0_h": np.asarray(L0_attn_conv_w0, f32).reshape(8, 9),
        "convb0_h": np.asarray(L0_attn_conv_b0, f32).reshape(8, 1),
        "convw1_h": np.asarray(L0_attn_conv_w1, f32).reshape(8, 9),
        "convb1_h": np.asarray(L0_attn_conv_b1, f32).reshape(8, 1),
        "lnw_h": np.ascontiguousarray(
            np.broadcast_to(np.asarray(L0_attn_ln_w, f32).reshape(1, 16),
                            (128, 16))),
        "lnb_h": np.ascontiguousarray(
            np.broadcast_to(np.asarray(L0_attn_ln_b, f32).reshape(1, 16),
                            (128, 16))),
        "ones_h": np.ones((1, 128), f32),
        "ident_h": consts["ident"], "mask1_h": consts["mask1"],
        "mask2_h": consts["mask2"], "cm1_h": consts["cm1"],
        "cp1_h": consts["cp1"], "cm2_h": consts["cm2"], "cp2_h": consts["cp2"],
    }

    in_maps = []
    rowsets = []
    for k in range(NCORES):
        rows = np.arange(8192)
        sel = (rows % 512 >= 64 * k) & (rows % 512 < 64 * (k + 1))
        rout = rows[sel]
        rowsets.append(rout)
        m = dict(shared)
        m["xatt"] = np.ascontiguousarray(X[1024 * k:1024 * (k + 1)])
        m["xout"] = np.ascontiguousarray(X[rout])
        in_maps.append(m)

    nc = _get_prog()
    res = run_bass_kernel_spmd(nc, in_maps, core_ids=list(range(NCORES)))

    out = np.empty((8192, DM), dtype=f32)
    for k in range(NCORES):
        out[rowsets[k]] = res.results[k]["o"]
    return out.reshape(1, 8192, DM)


# version 10
# speedup vs baseline: 1.2308x; 1.0217x over previous
import numpy as np
import concourse.bass as bass
import concourse.mybir as mybir
from concourse.bass_utils import run_bass_kernel_spmd

F32 = mybir.dt.float32
F32R = mybir.dt.float32r
AX = mybir.AxisListType
AF = mybir.ActivationFunctionType
ALU = mybir.AluOpType

NCORES = 8
ROWS = 1024
DM = 2048
GROUPS = 32
GMICRO = 4096
NLOC = 131072
KC = 16
MC = 44
SCALE = float(8 ** -0.5)
EPS = 1e-5
VC = 4097.0
YW = 40


def _host_consts():
    c = {}
    c["ident"] = np.eye(128, dtype=np.float32)
    p = np.arange(128)
    c["mask1"] = (p[:, None] // 8 == p[None, :] // 8).astype(np.float32)
    c["mask2"] = (p[:, None] % 16 == p[None, :] % 16).astype(np.float32)

    def shift(step, win):
        m = np.zeros((128, 128), np.float32)
        for q in range(128):
            k = q + step
            if 0 <= k < 128 and win(k) == win(q):
                m[k, q] = 1.0
        return m
    c["cm1"] = shift(-1, lambda x: x // 8)
    c["cp1"] = shift(+1, lambda x: x // 8)
    c["cm2"] = shift(-16, lambda x: x % 16)
    c["cp2"] = shift(+16, lambda x: x % 16)
    return c


def _host_wq_wk(qkv_w):
    wq = np.zeros((2, 128, 40), np.float32)
    wk = np.zeros((2, 128, 40), np.float32)
    for par in range(2):
        for b in range(4):
            r0 = 32 * b + 16 * par
            wq[par, r0:r0 + 16, 0:8] = qkv_w[:, 0:8]
            wq[par, r0:r0 + 16, 32:40] = qkv_w[:, 8:16]
            wk[par, r0:r0 + 16, 0:8] = qkv_w[:, 16:24]
            wk[par, r0:r0 + 16, 32:40] = qkv_w[:, 24:32]
    return wq, wk


def _host_wb_skeleton():
    wb = np.zeros((2, 128, 128), np.float32)
    for par in range(2):
        for b in range(4):
            r0 = 32 * b + 16 * par
            wb[par, r0:r0 + 16, 48:64] = np.eye(16, dtype=np.float32)
    return wb


def build_program():
    nc = bass.Bass("TRN2", target_bir_lowering=False, debug=False)

    def din(name, shape):
        return nc.dram_tensor(name, list(shape), F32, kind="ExternalInput").ap()

    xatt = din("xatt", [ROWS, DM])
    xout = din("xout", [ROWS, DM])
    wgate = din("wgate", [MC, 128, KC, 128])
    wup = din("wup", [MC, 128, KC, 128])
    wdown = din("wdown", [MC, 128, KC, 128])
    gsc = din("gsc", [128, MC])
    usc = din("usc", [128, MC])
    dsc = din("dsc", [128, KC])
    wq_h = din("wq_h", [128, 2, 40])
    wk_h = din("wk_h", [128, 2, 40])
    wb_h = din("wb_h", [128, 2, 128])
    qkvw_h = din("qkvw_h", [16, 48])
    projw1_h = din("projw1_h", [8, 16])
    projw2_h = din("projw2_h", [8, 16])
    projb_h = din("projb_h", [1, 16])
    convw0_h = din("convw0_h", [8, 9])
    convb0_h = din("convb0_h", [8, 1])
    convw1_h = din("convw1_h", [8, 9])
    convb1_h = din("convb1_h", [8, 1])
    lnw_h = din("lnw_h", [128, 16])
    lnb_h = din("lnb_h", [128, 16])
    ones_h = din("ones_h", [1, 128])
    ident_h = din("ident_h", [128, 128])
    mask1_h = din("mask1_h", [128, 128])
    mask2_h = din("mask2_h", [128, 128])
    cm1_h = din("cm1_h", [128, 128])
    cp1_h = din("cp1_h", [128, 128])
    cm2_h = din("cm2_h", [128, 128])
    cp2_h = din("cp2_h", [128, 128])

    o = nc.dram_tensor("o", [ROWS, DM], F32, kind="ExternalOutput").ap()
    s_dram = nc.dram_tensor("s_dram", [16, NLOC], F32, kind="Internal").ap()
    wt_dram = nc.dram_tensor("wt_dram", [16, 128], F32, kind="Internal").ap()

    ctx = []

    def sb(name, shape, dtype=F32):
        g = nc.sbuf_tensor(name, list(shape), dtype)
        t = g.__enter__()
        ctx.append(g)
        return t

    def psalloc(name):
        g = nc.psum_tensor(name, [128, 512], F32)
        t = g.__enter__()
        ctx.append(g)
        return t

    ident = sb("ident", [128, 128])
    mask1 = sb("mask1", [128, 128])
    mask2 = sb("mask2", [128, 128])
    cm1 = sb("cm1", [128, 128])
    cp1 = sb("cp1", [128, 128])
    cm2 = sb("cm2", [128, 128])
    cp2 = sb("cp2", [128, 128])
    wq_s = sb("wq_s", [128, 2, 40])
    wk_s = sb("wk_s", [128, 2, 40])
    wb_s = sb("wb_s", [128, 2, 128])
    qkvw_s = sb("qkvw_s", [16, 48])
    projw1_s = sb("projw1_s", [8, 16])
    projw2_s = sb("projw2_s", [8, 16])
    projb_s = sb("projb_s", [1, 16])
    cw0_s = sb("cw0_s", [8, 9])
    cb0_s = sb("cb0_s", [8, 1])
    cw1_s = sb("cw1_s", [8, 9])
    cb1_s = sb("cb1_s", [8, 1])
    ones1 = sb("ones1", [1, 128])
    beff = sb("beff", [128, 16])
    gamma = sb("gamma", [128, 16])
    beta = sb("beta", [128, 16])
    gsc_s = sb("gsc_s", [128, MC])
    usc_s = sb("usc_s", [128, MC])
    dsc_s = sb("dsc_s", [128, KC])
    wvt = sb("wvt", [8, 16])
    winit = sb("winit", [16, 128])
    smallv = sb("smallv", [128, 64])

    arA = sb("arA", [128, 2048])
    arB = sb("arB", [128, 12288])
    arC = sb("arC", [128, 4096])
    arD = sb("arD", [128, 4096])
    arE = sb("arE", [128, 4096])
    arF = sb("arF", [128, 4096])
    xsr = sb("xsr", [128, 8192], F32R)
    hr = sb("hr", [128, 2048], F32R)
    wr = sb("wr", [128, 8192], F32R)

    xnfm = arA[:, :].rearrange("p (k r) -> p k r", k=KC)
    qs = arB[0:48, 0:4096]
    ks = arB[0:48, 4096:8192]
    vs = arB[:, 8192:12288]
    y2cm = arC[0:16, 0:4096]
    tocm = arC[0:16, 0:4096]
    v1tm = arD[:, 0:528].rearrange("p (t c) -> p t c", c=66)
    v2tm = arD[:, 528:1056].rearrange("p (t c) -> p t c", c=66)
    yslab = arD[:, 1056:2336].rearrange("p (t c) -> p t c", c=YW)
    tokr = arD[:, 2336:2848].rearrange("p (t c) -> p t c", c=16)
    pbuf = [arE[:, 0:512], arE[:, 512:1024]]
    ptm = [arE[:, 1024:1536], arE[:, 1536:2048]]
    zslab = arE[:, 2048:2560].rearrange("p (t c) -> p t c", c=16)
    zsq = arE[:, 2560:3072].rearrange("p (t c) -> p t c", c=16)
    tokout = arE[:, 3072:3584].rearrange("p (t c) -> p t c", c=16)
    zred = arE[:, 3584:3616]
    xa_tm = arF[:, 0:2048]
    sq_tm = arF[:, 2048:4096]

    xs_hi = xsr[:, 0:4096].rearrange("p (k r) -> p k r", k=KC)
    xs_lo = xsr[:, 4096:8192].rearrange("p (k r) -> p k r", k=KC)
    hhi = hr[:, 0:1024].rearrange("p (j r) -> p j r", j=4)
    hlo = hr[:, 1024:2048].rearrange("p (j r) -> p j r", j=4)
    hraw = [arB[:, 0:256], arB[:, 256:512]]
    hv1 = [arB[:, 512:768], arB[:, 768:1024]]
    dslab = arC[:, 0:4096].rearrange("p (k r) -> p k r", k=KC)
    xo_tm = arF[:, 0:2048]
    vhi_tm = arD[:, 0:2048]
    vlo_tm = arD[:, 2048:4096]
    x_tm = arD[:, 0:2048]
    wg_t = [wr[:, 0:2048].rearrange("p (k r) -> p k r", k=KC),
            wr[:, 2048:4096].rearrange("p (k r) -> p k r", k=KC)]
    wu_t = [wr[:, 4096:6144].rearrange("p (k r) -> p k r", k=KC),
            wr[:, 6144:8192].rearrange("p (k r) -> p k r", k=KC)]
    wd_t = wg_t
    g_tm = arF[:, 0:2048]
    s_tm = arF[:, 2048:4096]
    o_tm = s_tm

    y2z = [smallv[:, 0:16], smallv[:, 16:32]]
    rec4 = [smallv[:, 32:36], smallv[:, 36:40]]
    negm2 = [smallv[:, 40:44], smallv[:, 44:48]]
    negm1 = [smallv[:, 48:52], smallv[:, 52:56]]
    mu_c = smallv[:, 56:57]
    rs_c = smallv[:, 57:58]

    pA = psalloc("pA")
    pB = psalloc("pB")
    pC = psalloc("pC")
    pD = psalloc("pD")
    pE = psalloc("pE")
    pF = psalloc("pF")
    pG = psalloc("pG")
    pH = psalloc("pH")

    barg = nc.semaphore("bar")
    barsem = barg.__enter__()
    ctx.append(barg)
    gsems = []
    for gi in range(2):
        gg = nc.semaphore(f"gbar{gi}")
        gsems.append(gg.__enter__())
        ctx.append(gg)
    esems = {}
    for en in ("tensor", "vector", "scalar", "gpsimd"):
        gg = nc.semaphore(f"e_{en}")
        esems[en] = gg.__enter__()
        ctx.append(gg)
    blkg = nc.Block()
    block = blkg.__enter__()

    phases = []
    phase_deps = []

    def phase(dep=None, **fns):
        phases.append(fns)
        phase_deps.append(len(phases) - 2 if dep is None else dep)
        return len(phases) - 1

    def dma_fn(n):
        def wrap(fn):
            fn.dma_incs = n
            return fn
        return wrap

    def sw_dma_fn(n, gi=0):
        def wrap(fn):
            fn.sw_dma_incs = n
            fn.sw_sem = gi
            return fn
        return wrap


    @dma_fn(24)
    def init_loads(e):
        for src, dst in (
            (ident_h, ident[:]), (mask1_h, mask1[:]), (mask2_h, mask2[:]),
            (cm1_h, cm1[:]), (cp1_h, cp1[:]), (cm2_h, cm2[:]), (cp2_h, cp2[:]),
            (wq_h, wq_s[:]), (wk_h, wk_s[:]), (wb_h, wb_s[:]),
            (qkvw_h, qkvw_s[:]), (projw1_h, projw1_s[:]),
            (projw2_h, projw2_s[:]), (projb_h, projb_s[:]),
            (convw0_h, cw0_s[:]), (convb0_h, cb0_s[:]),
            (convw1_h, cw1_s[:]), (convb1_h, cb1_s[:]),
            (lnw_h, gamma[:]), (lnb_h, beta[:]), (ones_h, ones1[:]),
            (gsc, gsc_s[:]), (usc, usc_s[:]), (dsc, dsc_s[:]),
        ):
            e.dma_start(out=dst, in_=src).then_inc(barsem, 16)
    phase(sync=init_loads)

    def init_memset(e):
        e.memset(winit[:], 0.0)
        e.memset(v1tm[:, :, 16:17], 1.0)
        e.memset(v2tm[:, :, 16:17], 1.0)
        e.memset(y2cm[:, :], 0.0)
    phase(gpsimd=init_memset)

    def init_scaleq(e):
        e.tensor_scalar_mul(wq_s[:, :, :], wq_s[:, :, :], SCALE)
    phase(vector=init_scaleq)

    def init_wvT(e):
        e.transpose(pA[0:8, 0:16], qkvw_s[0:16, 32:40], ident[0:16, 0:16])
        e.transpose(pA[0:8, 16:32], qkvw_s[0:16, 40:48], ident[0:16, 0:16])
    phase(tensor=init_wvT)

    def init_wvT_ev(e):
        e.tensor_copy(winit[0:8, 0:32], pA[0:8, 0:32])
    phase(vector=init_wvT_ev)

    jobs = [
        (0,   (0, 16),  None,            projw1_s, True, True),
        (16,  (0, 16),  cw0_s[:, 3:4],   projw1_s, True, True),
        (32,  (0, 16),  cw0_s[:, 5:6],   projw1_s, True, True),
        (64,  (16, 32), None,            projw2_s, True, True),
        (80,  (16, 32), cw1_s[:, 1:2],   projw2_s, True, True),
        (96,  (16, 32), cw1_s[:, 7:8],   projw2_s, True, True),
        (112, (0, 16),  cw0_s[:, 4:5],   projw1_s, True, False),
        (112, (16, 32), cw1_s[:, 4:5],   projw2_s, False, True),
    ]
    for col, (c0, c1), diag, pr, st, sp in jobs:
        def vj(e, c0=c0, c1=c1, diag=diag):
            if diag is None:
                e.tensor_copy(wvt[:], winit[0:8, c0:c1])
            else:
                e.tensor_scalar(wvt[:], winit[0:8, c0:c1], diag, None, ALU.mult)
        phase(vector=vj)

        def tj(e, col=col, pr=pr, st=st, sp=sp):
            e.matmul(pB[0:16, col:col + 16], wvt[:], pr[:],
                     start=st, stop=sp)
        phase(tensor=tj)

    def init_beff_mm(e):
        e.matmul(pC[0:1, 0:16], cb0_s[:], projw1_s[:], start=True, stop=False)
        e.matmul(pC[0:1, 0:16], cb1_s[:], projw2_s[:], start=False, stop=True)
    phase(tensor=init_beff_mm)

    def init_beff_add(e):
        e.tensor_add(winit[0:1, 48:64], pC[0:1, 0:16], projb_s[:])
    phase(vector=init_beff_add)

    def init_beff_bc(e):
        e.matmul(pD[0:128, 0:16], ones1[:], winit[0:1, 48:64],
                 start=True, stop=True)
    phase(tensor=init_beff_bc)

    def init_beff_ev(e):
        e.tensor_copy(beff[:], pD[0:128, 0:16])
    phase(vector=init_beff_ev)

    def init_vt_evac(e):
        e.tensor_copy(winit[:, 0:48], pB[0:16, 0:48])
        e.tensor_copy(winit[:, 64:128], pB[0:16, 64:128])
    phase(vector=init_vt_evac)

    @dma_fn(1)
    def init_vt_out(e):
        e.dma_start(out=wt_dram, in_=winit[:]).then_inc(barsem, 16)
    phase(sync=init_vt_out)

    @dma_fn(16)
    def init_vt_in(e):
        for par in range(2):
            for b in range(4):
                r0 = 32 * b + 16 * par
                e.dma_start(out=wb_s[r0:r0 + 16, par, 0:48],
                            in_=wt_dram[:, 0:48]).then_inc(barsem, 16)
                e.dma_start(out=wb_s[r0:r0 + 16, par, 64:128],
                            in_=wt_dram[:, 64:128]).then_inc(barsem, 16)
    phase(sync=init_vt_in)

    def ln_tile(src_dram, dst_tm, tile):
        @dma_fn(1)
        def load(e, tile=tile):
            e.dma_start(out=dst_tm, in_=src_dram[tile * 128:(tile + 1) * 128, :]
                        ).then_inc(barsem, 16)
        phase(sync=load)

        def mean(e):
            e.reduce_sum(mu_c[:], dst_tm, axis=AX.X, negate=True)
            e.drain()
            e.tensor_scalar_mul(mu_c[:], mu_c[:], 1.0 / DM)
            e.drain()
            return e.tensor_scalar(dst_tm, dst_tm, mu_c[:], None, ALU.add)
        phase(vector=mean)

        def sq(e):
            return e.activation(sq_tm[:], dst_tm, AF.Square)
        phase(scalar=sq)

        def var(e):
            e.reduce_sum(rs_c[:], sq_tm[:], axis=AX.X)
            e.drain()
            return e.tensor_scalar(rs_c[:], rs_c[:], 1.0 / DM, EPS,
                                   ALU.mult, ALU.add)
        phase(vector=var)

        def rsq(e):
            e.activation(rs_c[:], rs_c[:], AF.Ln)
            e.drain()
            return e.activation(rs_c[:], rs_c[:], AF.Exp, scale=-0.5)
        phase(scalar=rsq)

        def fin(e):
            return e.tensor_scalar(dst_tm, dst_tm, rs_c[:], None, ALU.mult)
        phase(vector=fin)

    qps = (pA, pB, pC, pD)
    kps = (pE, pF, pG, pH)
    for m in range(GROUPS):
        if m % 4 == 0:
            tile8 = m // 4
            ln_tile(xatt, xa_tm[:, :], tile8)
            for jh in range(4):
                def tp(e, jh=jh):
                    h = None
                    for j2 in range(4):
                        j = jh * 4 + j2
                        h = e.transpose(pA[:, j2 * 128:(j2 + 1) * 128],
                                        xa_tm[:, j * 128:(j + 1) * 128],
                                        ident[:])
                    return h
                phase(tensor=tp)

                def tpe(e, jh=jh):
                    h = None
                    for j2 in range(4):
                        j = jh * 4 + j2
                        h = e.tensor_copy(xnfm[:, j, :],
                                          pA[:, j2 * 128:(j2 + 1) * 128])
                    return h
                phase(vector=tpe)

        def rhs_ap(blkb, m=m):
            mm4 = m % 4
            return xnfm[32 * blkb:32 * blkb + 32, :, mm4 * 32:mm4 * 32 + 32]

        for par in range(2):
            def qkv_mm(e, par=par, rhs_ap=rhs_ap):
                h = None
                for blkb in range(4):
                    tp = (32 * blkb, 0)
                    r = rhs_ap(blkb)
                    e.matmul(qps[blkb][0:40, :],
                             wq_s[32 * blkb:32 * blkb + 32, par, :],
                             r, start=True, stop=True, tile_position=tp)
                    h = e.matmul(kps[blkb][0:40, :],
                                 wk_s[32 * blkb:32 * blkb + 32, par, :],
                                 r, start=True, stop=True, tile_position=tp)
                return h
            phase(tensor=qkv_mm)

            def qkv_ev(e, par=par):
                h = None
                for blkb in range(4):
                    sp = 2 * blkb + par
                    for slab, psl in ((qs, qps[blkb]), (ks, kps[blkb])):
                        d1 = slab[0:8, :].rearrange(
                            "p (i b s) -> p i b s", i=32, b=16)[
                            :, :, :, sp].transpose([0, 2, 1])
                        e.tensor_copy(d1, psl[0:8, :].rearrange(
                            "p (b i) -> p b i", b=16))
                        d2 = slab[32:40, :].rearrange(
                            "p (b s r g) -> p b s r g", b=16, s=8, r=4)[
                            :, :, sp, :, :].transpose([0, 1, 3, 2])
                        h = e.tensor_copy(d2, psl[32:40, :].rearrange(
                            "p (b g r) -> p b g r", b=16, g=8))
                return h
            phase(vector=qkv_ev)

        for par in range(2):
            def qkvB_mm(e, par=par, rhs_ap=rhs_ap):
                h = None
                for blkb in range(4):
                    h = e.matmul(qps[blkb][:, :],
                                 wb_s[32 * blkb:32 * blkb + 32, par, :],
                                 rhs_ap(blkb), start=True, stop=True,
                                 tile_position=(32 * blkb, 0))
                return h
            phase(tensor=qkvB_mm)

            def qkvB_ev(e, par=par):
                h = None
                for blkb in range(4):
                    sp = 2 * blkb + par
                    d1 = vs[0:64, :].rearrange(
                        "p (i b s) -> p i b s", i=32, b=16)[
                        :, :, :, sp].transpose([0, 2, 1])
                    e.tensor_copy(d1, qps[blkb][0:64, :].rearrange(
                        "p (b i) -> p b i", b=16))
                    d2 = vs[64:128, :].rearrange(
                        "p (b s r g) -> p b s r g", b=16, s=8, r=4)[
                        :, :, sp, :, :].transpose([0, 1, 3, 2])
                    h = e.tensor_copy(d2, qps[blkb][64:128, :].rearrange(
                        "p (b g r) -> p b g r", b=16, g=8))
                return h
            phase(vector=qkvB_ev)

        Sb = (pA, pB)
        PTb = (pC, pD)
        VTb = (pE, pF)
        Yb = (pG, pH)

        def run_pass(qrow, krow, vrow0, vrow1, vident, vwork, negmP, is_b1):
            p_mm = {}
            p_ng = {}
            p_exp = {}
            p_ptr = {}
            p_pev = {}
            p_av = {}
            p_tail = {}
            p_rc = {}
            p_dvt = {}
            p_y2ev = {}

            def st_mm(st):
                par = st % 2
                tl = [4 * st + i for i in range(4)]

                def mmS(e, tl=tl, par=par):
                    h = None
                    for i, t2 in enumerate(tl):
                        h = e.matmul(Sb[par][:, i * 128:(i + 1) * 128],
                                     qs[qrow[0]:qrow[1],
                                        t2 * 128:(t2 + 1) * 128],
                                     ks[qrow[0]:qrow[1],
                                        t2 * 128:(t2 + 1) * 128],
                                     start=True, stop=True,
                                     tile_position=(qrow[0] - qrow[0] % 32, 0)
                                     if qrow[0] >= 32 else None)
                    return h
                p_mm[st] = phase(dep=p_exp.get(st - 2), tensor=mmS)

            def st_ng(st):
                par = st % 2

                def negmax(e, par=par):
                    return e.reduce_max(negmP[par][:],
                                        Sb[par][:, :].rearrange(
                                            "p (t x) -> p t x", t=4),
                                        axis=AX.X, negate=True)
                p_ng[st] = phase(dep=p_mm[st], vector=negmax)

            def st_exp(st):
                par = st % 2

                def expS(e, par=par):
                    h = None
                    for i in range(4):
                        h = e.activation(pbuf[par][:, i * 128:(i + 1) * 128],
                                         Sb[par][:, i * 128:(i + 1) * 128],
                                         AF.Exp, bias=negmP[par][:, i:i + 1])
                    return h
                dep = p_ng[st]
                if st - 2 in p_ptr:
                    dep = max(dep, p_ptr[st - 2])
                p_exp[st] = phase(dep=dep, scalar=expS)

            def st_ptr(st):
                par = st % 2
                tl = [4 * st + i for i in range(4)]

                def ptrans(e, tl=tl, par=par):
                    for i in range(4):
                        e.transpose(PTb[par][:, i * 128:(i + 1) * 128],
                                    pbuf[par][:, i * 128:(i + 1) * 128],
                                    ident[:])
                    h = None
                    for i, t2 in enumerate(tl):
                        h = e.transpose(VTb[par][:, i * 128:i * 128 + 64],
                                        vs[vrow0:vrow1,
                                           t2 * 128:(t2 + 1) * 128], vident)
                    return h
                dep = p_exp[st]
                if st - 2 in p_pev:
                    dep = max(dep, p_pev[st - 2])
                if st - 2 in p_y2ev:
                    dep = max(dep, p_y2ev[st - 2])
                p_ptr[st] = phase(dep=dep, tensor=ptrans)

            def st_pev(st):
                par = st % 2
                tl = [4 * st + i for i in range(4)]

                def pt_evac(e, tl=tl, par=par):
                    e.tensor_mul(ptm[par][:, :].rearrange(
                        "p (t x) -> p t x", t=4),
                        PTb[par][:, :].rearrange("p (t x) -> p t x", t=4),
                        mask1[:].unsqueeze(1).broadcast_to([128, 4, 128]))
                    for i, ip in enumerate(tl):
                        if is_b1:
                            e.tensor_add(vwork[:, par * 4 + i, 0:16],
                                         VTb[par][:, i * 128:i * 128 + 16],
                                         beff[:])
                            e.tensor_copy(vwork[:, par * 4 + i, 17:49],
                                          VTb[par][:, i * 128 + 16:i * 128 + 48])
                            h = e.tensor_copy(
                                tokr[:, ip, :],
                                VTb[par][:, i * 128 + 48:i * 128 + 64])
                        else:
                            e.tensor_copy(vwork[:, par * 4 + i, 0:16],
                                          VTb[par][:, i * 128:i * 128 + 16])
                            h = e.tensor_copy(
                                vwork[:, par * 4 + i, 17:65],
                                VTb[par][:, i * 128 + 16:i * 128 + 64])
                    return h
                dep = p_ptr[st]
                if st - 2 in p_av:
                    dep = max(dep, p_av[st - 2])
                p_pev[st] = phase(dep=dep, vector=pt_evac)

            def st_av(st):
                par = st % 2
                tl = [4 * st + i for i in range(4)]

                def av(e, tl=tl, par=par):
                    h = None
                    for i, ip in enumerate(tl):
                        lhs = ptm[par][:, i * 128:(i + 1) * 128]
                        e.matmul(Yb[par][:, i * YW:i * YW + 17], lhs,
                                 vwork[:, par * 4 + i, 0:17],
                                 start=True, stop=True)
                        e.matmul(Yb[par][:, i * YW + 20:i * YW + 36], cm1[:],
                                 vwork[:, par * 4 + i, 17:33],
                                 start=True, stop=False)
                        e.matmul(Yb[par][:, i * YW + 20:i * YW + 36], cp1[:],
                                 vwork[:, par * 4 + i, 33:49],
                                 start=False, stop=False)
                        if not is_b1:
                            h = e.matmul(Yb[par][:, i * YW + 20:i * YW + 36],
                                         ident[:],
                                         vwork[:, par * 4 + i, 49:65],
                                         start=False, stop=True)
                        else:
                            h = e.matmul(Yb[par][:, i * YW + 20:i * YW + 36],
                                         y2cm[:, ip * 128:(ip + 1) * 128],
                                         ident[0:16, 0:16], start=False,
                                         stop=True, is_transpose=True)
                    return h
                dep = p_pev[st]
                if st - 2 in p_tail:
                    dep = max(dep, p_tail[st - 2])
                p_av[st] = phase(dep=dep, tensor=av)

            def st_tail_b1(st):
                par = st % 2
                tl = [4 * st + i for i in range(4)]

                def y1ev(e, tl=tl, par=par):
                    h = None
                    for i, ip in enumerate(tl):
                        e.copy(yslab[:, ip, 0:17],
                               Yb[par][:, i * YW:i * YW + 17])
                        h = e.copy(yslab[:, ip, 20:36],
                                   Yb[par][:, i * YW + 20:i * YW + 36])
                    return h
                p_tail[st] = phase(dep=p_av[st], scalar=y1ev)

            def st_rc(st):
                par = st % 2

                def recs2(e, par=par):
                    dv = Yb[par][:, 0:4 * YW].rearrange(
                        "p (t c) -> p t c", c=YW)[:, :, 16]
                    e.activation(rec4[par][:], dv, AF.Ln)
                    e.drain()
                    return e.activation(rec4[par][:], rec4[par][:], AF.Exp,
                                        scale=-1.0)
                p_rc[st] = phase(dep=p_av[st], scalar=recs2)

            def st_dvt(st):
                par = st % 2
                prev = p_rc[st]
                for i in range(4):
                    def y2div(e, i=i, par=par):
                        e.tensor_scalar(y2z[par][:],
                                        Yb[par][:, i * YW:i * YW + 16],
                                        rec4[par][:, i:i + 1], None, ALU.mult)
                        e.drain()
                        return e.tensor_add(y2z[par][:], y2z[par][:],
                                            Yb[par][:, i * YW + 20:i * YW + 36])
                    dep = prev
                    if i == 0 and st - 2 in p_tail:
                        dep = max(dep, p_tail[st - 2])
                    p_dv = phase(dep=dep, vector=y2div)

                    def y2t(e, i=i, par=par):
                        return e.transpose(
                            PTb[par][0:16, i * 128:(i + 1) * 128],
                            y2z[par][:], ident[:])
                    prev = phase(dep=p_dv, tensor=y2t)
                p_tail[st] = prev

            def st_y2ev(st):
                par = st % 2
                tl = [4 * st + i for i in range(4)]

                def y2ev(e, tl=tl, par=par):
                    hh = None
                    for i, t2 in enumerate(tl):
                        b0, h = t2 // 2, t2 % 2
                        dst = y2cm[:, :].rearrange(
                            "p (g r b s) -> p g r b s", g=8, r=4, b=16)[
                            :, :, :, b0, 4 * h:4 * h + 4].transpose(
                            [0, 3, 2, 1])
                        hh = e.tensor_copy(
                            dst, PTb[par][0:16, i * 128:(i + 1) * 128]
                            .rearrange("p (s r g) -> p s r g", s=4, r=4))
                    return hh
                p_y2ev[st] = phase(dep=p_tail[st], vector=y2ev)

            if is_b1:
                stages = [st_mm, st_ng, st_exp, st_ptr, st_pev, st_av,
                          st_tail_b1]
            else:
                stages = [st_mm, st_ng, st_exp, st_ptr, st_pev, st_av,
                          st_rc, st_dvt, st_y2ev]
            ns = len(stages)
            II = 3
            for t in range(II * 7 + ns):
                for gidx in range(ns):
                    if (t - gidx) % II == 0:
                        st = (t - gidx) // II
                        if 0 <= st < 8:
                            stages[gidx](st)

        run_pass((32, 40), None, 64, 128, ident[64:128, 64:128], v2tm,
                 negm2, False)
        run_pass((0, 8), None, 0, 64, ident[0:64, 0:64], v1tm, negm1, True)

        def ln16r(e):
            e.activation(zred[:], yslab[:, :, 16], AF.Ln)
            e.drain()
            return e.activation(zred[:], zred[:], AF.Exp, scale=-1.0)
        phase(scalar=ln16r)

        def ln16a(e):
            e.tensor_mul(zslab[:, :, :], yslab[:, :, 0:16],
                         zred[:].unsqueeze(2).broadcast_to(
                             [128, 32, 16]))
            e.drain()
            e.tensor_add(zslab[:, :, :], zslab[:, :, :], yslab[:, :, 20:36])
            e.drain()
            e.reduce_sum(zred[:], zslab[:, :, :], axis=AX.X, negate=True)
            e.drain()
            e.tensor_scalar_mul(zred[:], zred[:], 1.0 / 16.0)
            e.drain()
            return e.tensor_add(zslab[:, :, :], zslab[:, :, :],
                                zred[:].unsqueeze(2).broadcast_to(
                                    [128, 32, 16]))
        phase(vector=ln16a)

        def ln16sq(e):
            return e.activation(zsq[:, :, :], zslab[:, :, :], AF.Square)
        phase(scalar=ln16sq)

        def ln16b(e):
            e.reduce_sum(zred[:], zsq[:, :, :], axis=AX.X)
            e.drain()
            return e.tensor_scalar(zred[:], zred[:], 1.0 / 16.0, EPS,
                                   ALU.mult, ALU.add)
        phase(vector=ln16b)

        def ln16c(e):
            e.activation(zred[:], zred[:], AF.Ln)
            e.drain()
            return e.activation(zred[:], zred[:], AF.Exp, scale=-0.5)
        phase(scalar=ln16c)

        def ln16d(e):
            e.tensor_mul(zslab[:, :, :], zslab[:, :, :],
                         zred[:].unsqueeze(2).broadcast_to(
                             [128, 32, 16]))
            e.drain()
            e.tensor_mul(zslab[:, :, :], zslab[:, :, :],
                         gamma[:].unsqueeze(1).broadcast_to(
                             [128, 32, 16]))
            e.drain()
            e.tensor_add(zslab[:, :, :], zslab[:, :, :],
                         beta[:].unsqueeze(1).broadcast_to(
                             [128, 32, 16]))
            e.drain()
            return e.tensor_add(tokout[:, :, :], zslab[:, :, :],
                                tokr[:, :, :])
        phase(vector=ln16d)

        for q in range(4):
            def toT(e, q=q):
                h = None
                for j in range(4):
                    e.transpose(pA[0:16, j * 128:(j + 1) * 128],
                                tokout[:, q * 8 + j, :], ident[:])
                for j in range(4):
                    h = e.transpose(pB[0:16, j * 128:(j + 1) * 128],
                                    tokout[:, q * 8 + 4 + j, :], ident[:])
                return h
            phase(tensor=toT)

            def toEv(e, q=q):
                e.tensor_copy(tocm[:, q * 1024:q * 1024 + 512], pA[0:16, :])
                return e.tensor_copy(
                    tocm[:, q * 1024 + 512:(q + 1) * 1024], pB[0:16, :])
            phase(vector=toEv)

        @dma_fn(1)
        def scat(e, m=m):
            e.dma_start(out=s_dram[:, m * GMICRO:(m + 1) * GMICRO],
                        in_=tocm[:, :]).then_inc(barsem, 16)
        phase(sync=scat)


    allp = (pA, pB, pC, pD, pE, pF, pG, pH)
    for tq in range(4):
        for tile in range(2):
            ln_tile(xout, xo_tm, tq * 2 + tile)

            def velt(e):
                e.tensor_scalar_mul(sq_tm[:], xo_tm[:], VC)
                e.drain()
                e.tensor_sub(vlo_tm[:], sq_tm[:], xo_tm[:])
                e.drain()
                e.tensor_sub(vhi_tm[:], sq_tm[:], vlo_tm[:])
                e.drain()
                return e.tensor_sub(vlo_tm[:], xo_tm[:], vhi_tm[:])
            phase(vector=velt)

            for jh in range(4):
                def ftp(e, jh=jh):
                    h = None
                    for j2 in range(4):
                        j = jh * 4 + j2
                        e.transpose(pA[:, j2 * 128:(j2 + 1) * 128],
                                    vhi_tm[:, j * 128:(j + 1) * 128], ident[:])
                        h = e.transpose(pB[:, j2 * 128:(j2 + 1) * 128],
                                        vlo_tm[:, j * 128:(j + 1) * 128],
                                        ident[:])
                    return h
                phase(tensor=ftp)

                def ftpe(e, tile=tile, jh=jh):
                    h = None
                    for j2 in range(4):
                        j = jh * 4 + j2
                        e.tensor_copy(xs_hi[:, j, tile * 128:(tile + 1) * 128],
                                      pA[:, j2 * 128:(j2 + 1) * 128])
                        h = e.tensor_copy(
                            xs_lo[:, j, tile * 128:(tile + 1) * 128],
                            pB[:, j2 * 128:(j2 + 1) * 128])
                    return h
                phase(vector=ftpe)

        gu_banks = ((pA, pB), (pC, pD))
        ph_wload = {}
        ph_gu = {}
        ph_hmake = {-1: None, -2: None}
        ph_dacc_last = None
        ph_dmm_last = None
        for jb in range(11):
            for jj in range(4):
                j = jb * 4 + jj

                @sw_dma_fn(2, j % 2)
                def wload(e, j=j):
                    e.dma_start(out=wg_t[j % 2][:, :, :], in_=wgate[j, :, :, :]
                                ).then_inc(gsems[j % 2], 16)
                    e.dma_start(out=wu_t[j % 2][:, :, :], in_=wup[j, :, :, :]
                                ).then_inc(gsems[j % 2], 16)
                dep = ph_gu.get(j - 2)
                if dep is not None and ph_dmm_last is not None:
                    dep = max(dep, ph_dmm_last)
                ph_wload[j] = phase(dep=dep, gpsimd=wload)

                def gu_mm(e, j=j):
                    bg, bu = gu_banks[j % 2]
                    h = None
                    for k in range(KC):
                        e.matmul(bg[:, 0:256], wg_t[j % 2][:, k, :],
                                 xs_hi[:, k, :], start=(k == 0), stop=False)
                        e.matmul(bg[:, 0:256], wg_t[j % 2][:, k, :],
                                 xs_lo[:, k, :], start=False, stop=(k == KC - 1))
                    for k in range(KC):
                        e.matmul(bu[:, 0:256], wu_t[j % 2][:, k, :],
                                 xs_hi[:, k, :], start=(k == 0), stop=False)
                        h = e.matmul(bu[:, 0:256], wu_t[j % 2][:, k, :],
                                     xs_lo[:, k, :], start=False,
                                     stop=(k == KC - 1))
                    return h
                gdep = ph_wload[j]
                if ph_hmake.get(j - 2) is not None:
                    gdep = max(gdep, ph_hmake[j - 2])
                if ph_dacc_last is not None:
                    gdep = max(gdep, ph_dacc_last)
                ph_gu[j] = phase(dep=gdep, tensor=gu_mm)

                def gsilu(e, j=j):
                    bg, bu = gu_banks[j % 2]
                    return e.activation(hv1[j % 2][:], bg[:, 0:256], AF.Silu,
                                        scale=gsc_s[:, j:j + 1])
                phase(dep=ph_gu[j], scalar=gsilu)

                def hmake(e, j=j, jj=jj):
                    bg, bu = gu_banks[j % 2]
                    hw = hraw[j % 2]
                    e.scalar_tensor_tensor(hw[:], bu[:, 0:256],
                                           usc_s[:, j:j + 1], hv1[j % 2][:],
                                           ALU.mult, ALU.mult)
                    e.drain()
                    e.tensor_scalar_mul(sq_tm[:, 0:256], hw[:], VC)
                    e.drain()
                    e.tensor_sub(sq_tm[:, 256:512], sq_tm[:, 0:256], hw[:])
                    e.drain()
                    e.tensor_sub(hhi[:, jj, :], sq_tm[:, 0:256],
                                 sq_tm[:, 256:512])
                    e.drain()
                    return e.tensor_sub(hlo[:, jj, :], hw[:], hhi[:, jj, :])
                ph_hmake[j] = phase(vector=hmake)

            ph_dload = {}
            ph_dmm = {}
            for jj in range(4):
                j = jb * 4 + jj

                @sw_dma_fn(1, j % 2)
                def dload(e, j=j):
                    e.dma_start(out=wd_t[j % 2][:, :, :], in_=wdown[j, :, :, :]
                                ).then_inc(gsems[j % 2], 16)
                ddep = ph_dmm.get(jj - 2)
                ph_dload[jj] = phase(dep=ddep, gpsimd=dload)

                def d_mm(e, jj=jj, j=j):
                    h = None
                    for oc in range(KC):
                        pp = allp[oc % 8]
                        c0 = (oc // 8) * 256
                        e.matmul(pp[:, c0:c0 + 256], wd_t[j % 2][:, oc, :],
                                 hhi[:, jj, :],
                                 start=(jj == 0 and oc < 8), stop=False)
                        h = e.matmul(pp[:, c0:c0 + 256], wd_t[j % 2][:, oc, :],
                                     hlo[:, jj, :], start=False,
                                     stop=(jj == 3 and oc >= 8))
                    return h
                ph_dmm[jj] = phase(dep=ph_dload[jj], tensor=d_mm)
                ph_dmm_last = ph_dmm[jj]

            def dacc(e, jb=jb):
                h = None
                for oc in range(KC):
                    pp = allp[oc % 8]
                    c0 = (oc // 8) * 256
                    if jb == 0:
                        h = e.tensor_copy(dslab[:, oc, :], pp[:, c0:c0 + 256])
                    else:
                        h = e.tensor_add(dslab[:, oc, :], dslab[:, oc, :],
                                         pp[:, c0:c0 + 256])
                return h
            ph_dacc_last = phase(dep=ph_dmm[3], vector=dacc)

        def gfin(e):
            h = None
            for oc in range(KC):
                h = e.activation(dslab[:, oc, :], dslab[:, oc, :], AF.Silu,
                                 scale=dsc_s[:, oc:oc + 1])
            return h
        phase(scalar=gfin)

        for tb in range(2):
            row0 = tq * 256 + tb * 128

            def gT(e, tb=tb):
                for oc in range(4):
                    e.transpose(pA[:, oc * 128:(oc + 1) * 128],
                                dslab[:, oc, tb * 128:(tb + 1) * 128], ident[:])
                for oc in range(4):
                    e.transpose(pB[:, oc * 128:(oc + 1) * 128],
                                dslab[:, 4 + oc, tb * 128:(tb + 1) * 128],
                                ident[:])
                for oc in range(4):
                    e.transpose(pC[:, oc * 128:(oc + 1) * 128],
                                dslab[:, 8 + oc, tb * 128:(tb + 1) * 128],
                                ident[:])
                h = None
                for oc in range(4):
                    h = e.transpose(pD[:, oc * 128:(oc + 1) * 128],
                                    dslab[:, 12 + oc, tb * 128:(tb + 1) * 128],
                                    ident[:])
                return h
            phase(tensor=gT)

            def gEv(e):
                e.tensor_copy(g_tm[:, 0:512], pA[:, :])
                e.tensor_copy(g_tm[:, 512:1024], pB[:, :])
                e.tensor_copy(g_tm[:, 1024:1536], pC[:, :])
                return e.tensor_copy(g_tm[:, 1536:2048], pD[:, :])
            phase(vector=gEv)

            @dma_fn(2)
            def fload(e, row0=row0):
                e.dma_start(out=x_tm[:], in_=xout[row0:row0 + 128, :]
                            ).then_inc(barsem, 16)
                ta = row0 // 64
                e.dma_start(
                    out=s_tm[:],
                    in_=s_dram.rearrange("c (r j) -> c r j", j=DM)[
                        ta:ta + 2, 0:64, :]).then_inc(barsem, 16)
            phase(sync=fload)

            def fuse(e):
                e.tensor_mul(o_tm[:], s_tm[:], g_tm[:])
                e.drain()
                return e.tensor_add(o_tm[:], o_tm[:], x_tm[:])
            phase(vector=fuse)

            @dma_fn(1)
            def fstore(e, row0=row0):
                e.dma_start(out=o[row0:row0 + 128, :], in_=o_tm[:]
                            ).then_inc(barsem, 16)
            phase(sync=fstore)

    if _PHASE_LIMIT is not None:
        del phases[_PHASE_LIMIT:]
    allsems = ([esems["tensor"], esems["vector"], esems["scalar"],
                esems["gpsimd"], barsem] + gsems)
    eidx = {"tensor": 0, "vector": 1, "scalar": 2, "gpsimd": 3}
    NS = 7
    cum = []
    tot = [0] * NS
    for p in phases:
        for en, fn in p.items():
            d = getattr(fn, "dma_incs", 0)
            g = getattr(fn, "sw_dma_incs", 0)
            if g:
                tot[5 + getattr(fn, "sw_sem", 0)] += 16 * g
            elif d:
                tot[4] += 16 * d
            else:
                tot[eidx[en]] += 1
        cum.append(tuple(tot))
    finals = tuple(tot)

    for ename in ("sync", "gpsimd", "tensor", "vector", "scalar"):
        dec = getattr(block, ename)

        def mk(ename=ename):
            def body(eng):
                lastw = [0] * NS
                for pi, p in enumerate(phases):
                    if ename not in p:
                        continue
                    dep = phase_deps[pi]
                    if dep is not None and dep >= len(phases):
                        dep = len(phases) - 1
                    tgt = cum[dep] if (dep is not None and dep >= 0) \
                        else (0,) * NS
                    for si in range(NS):
                        t2 = tgt[si]
                        if t2 > lastw[si]:
                            eng.wait_ge(allsems[si], t2)
                            lastw[si] = t2
                    fn = p[ename]
                    h = fn(eng)
                    if (getattr(fn, "dma_incs", 0) == 0
                            and getattr(fn, "sw_dma_incs", 0) == 0):
                        if h is None:
                            h = eng.drain()
                        h.then_inc(esems[ename], 1)
                if ename == "sync":
                    for si in range(NS):
                        if finals[si] > 0:
                            eng.wait_ge(allsems[si], finals[si])
            return body
        dec(mk())

    blkg.__exit__(None, None, None)
    for g in reversed(ctx):
        g.__exit__(None, None, None)
    nc._phases_debug = (list(phase_deps),
                        [[(en, fn.__name__) for en, fn in p.items()]
                         for p in phases], cum)
    return nc


_PHASE_LIMIT = None
_PROG = None


def _get_prog():
    global _PROG
    if _PROG is None:
        _PROG = build_program()
    return _PROG


def kernel(L0_attn_x, L0_attn_qkv_w, L0_attn_conv_w0, L0_attn_conv_b0,
           L0_attn_conv_w1, L0_attn_conv_b1, L0_attn_proj_w, L0_attn_proj_b,
           L0_attn_ln_w, L0_attn_ln_b, L0_ffn_gate_w, L0_ffn_up_w,
           L0_ffn_down_w, L0_ffn_gate_s, L0_ffn_up_s, L0_ffn_down_s):
    f32 = np.float32
    X = np.ascontiguousarray(np.asarray(L0_attn_x, dtype=f32).reshape(8192, DM))
    qkv_w = np.asarray(L0_attn_qkv_w, dtype=f32)
    consts = _host_consts()
    wq, wk = _host_wq_wk(qkv_w)
    wb = _host_wb_skeleton()

    gate_w = np.asarray(L0_ffn_gate_w, dtype=f32)
    up_w = np.asarray(L0_ffn_up_w, dtype=f32)
    down_w = np.asarray(L0_ffn_down_w, dtype=f32)
    WG = np.ascontiguousarray(
        gate_w.reshape(MC, 128, KC, 128).transpose(0, 3, 2, 1))
    WU = np.ascontiguousarray(
        up_w.reshape(MC, 128, KC, 128).transpose(0, 3, 2, 1))
    WD = np.ascontiguousarray(
        down_w.reshape(KC, 128, MC, 128).transpose(2, 3, 0, 1))

    gsch = np.ascontiguousarray(
        np.asarray(L0_ffn_gate_s, dtype=f32).reshape(MC, 128).T)
    usch = np.ascontiguousarray(
        np.asarray(L0_ffn_up_s, dtype=f32).reshape(MC, 128).T)
    dsch = np.ascontiguousarray(
        np.asarray(L0_ffn_down_s, dtype=f32).reshape(KC, 128).T)

    shared = {
        "wgate": WG, "wup": WU, "wdown": WD,
        "gsc": gsch, "usc": usch, "dsc": dsch,
        "wq_h": np.ascontiguousarray(wq.transpose(1, 0, 2)),
        "wk_h": np.ascontiguousarray(wk.transpose(1, 0, 2)),
        "wb_h": np.ascontiguousarray(wb.transpose(1, 0, 2)),
        "qkvw_h": qkv_w,
        "projw1_h": np.ascontiguousarray(np.asarray(L0_attn_proj_w, f32)[0:8]),
        "projw2_h": np.ascontiguousarray(np.asarray(L0_attn_proj_w, f32)[8:16]),
        "projb_h": np.asarray(L0_attn_proj_b, f32).reshape(1, 16),
        "convw0_h": np.asarray(L0_attn_conv_w0, f32).reshape(8, 9),
        "convb0_h": np.asarray(L0_attn_conv_b0, f32).reshape(8, 1),
        "convw1_h": np.asarray(L0_attn_conv_w1, f32).reshape(8, 9),
        "convb1_h": np.asarray(L0_attn_conv_b1, f32).reshape(8, 1),
        "lnw_h": np.ascontiguousarray(
            np.broadcast_to(np.asarray(L0_attn_ln_w, f32).reshape(1, 16),
                            (128, 16))),
        "lnb_h": np.ascontiguousarray(
            np.broadcast_to(np.asarray(L0_attn_ln_b, f32).reshape(1, 16),
                            (128, 16))),
        "ones_h": np.ones((1, 128), f32),
        "ident_h": consts["ident"], "mask1_h": consts["mask1"],
        "mask2_h": consts["mask2"], "cm1_h": consts["cm1"],
        "cp1_h": consts["cp1"], "cm2_h": consts["cm2"], "cp2_h": consts["cp2"],
    }

    in_maps = []
    rowsets = []
    for k in range(NCORES):
        rows = np.arange(8192)
        sel = (rows % 512 >= 64 * k) & (rows % 512 < 64 * (k + 1))
        rout = rows[sel]
        rowsets.append(rout)
        m = dict(shared)
        m["xatt"] = np.ascontiguousarray(X[1024 * k:1024 * (k + 1)])
        m["xout"] = np.ascontiguousarray(X[rout])
        in_maps.append(m)

    nc = _get_prog()
    res = run_bass_kernel_spmd(nc, in_maps, core_ids=list(range(NCORES)))

    out = np.empty((8192, DM), dtype=f32)
    for k in range(NCORES):
        out[rowsets[k]] = res.results[k]["o"]
    return out.reshape(1, 8192, DM)


# version 12
# speedup vs baseline: 1.4144x; 1.1491x over previous
import numpy as np
import concourse.bass as bass
import concourse.mybir as mybir
from concourse.bass_utils import run_bass_kernel_spmd

F32 = mybir.dt.float32
F32R = mybir.dt.float32r
AX = mybir.AxisListType
AF = mybir.ActivationFunctionType
ALU = mybir.AluOpType

NCORES = 8
ROWS = 1024
DM = 2048
GROUPS = 32
GMICRO = 4096
NLOC = 131072
KC = 16
MC = 44
SCALE = float(8 ** -0.5)
EPS = 1e-5
VC = 4097.0
YW = 40


def _host_consts():
    c = {}
    c["ident"] = np.eye(128, dtype=np.float32)
    p = np.arange(128)
    c["mask1"] = (p[:, None] // 8 == p[None, :] // 8).astype(np.float32)
    c["mask2"] = (p[:, None] % 16 == p[None, :] % 16).astype(np.float32)

    def shift(step, win):
        m = np.zeros((128, 128), np.float32)
        for q in range(128):
            k = q + step
            if 0 <= k < 128 and win(k) == win(q):
                m[k, q] = 1.0
        return m
    c["cm1"] = shift(-1, lambda x: x // 8)
    c["cp1"] = shift(+1, lambda x: x // 8)
    c["cm2"] = shift(-16, lambda x: x % 16)
    c["cp2"] = shift(+16, lambda x: x % 16)
    return c


def _host_wq_wk(qkv_w):
    wq = np.zeros((2, 128, 40), np.float32)
    wk = np.zeros((2, 128, 40), np.float32)
    for par in range(2):
        for b in range(4):
            r0 = 32 * b + 16 * par
            wq[par, r0:r0 + 16, 0:8] = qkv_w[:, 0:8]
            wq[par, r0:r0 + 16, 32:40] = qkv_w[:, 8:16]
            wk[par, r0:r0 + 16, 0:8] = qkv_w[:, 16:24]
            wk[par, r0:r0 + 16, 32:40] = qkv_w[:, 24:32]
    return wq, wk


def _host_wb_skeleton():
    wb = np.zeros((2, 128, 128), np.float32)
    for par in range(2):
        for b in range(4):
            r0 = 32 * b + 16 * par
            wb[par, r0:r0 + 16, 48:64] = np.eye(16, dtype=np.float32)
    return wb


def build_program():
    nc = bass.Bass("TRN2", target_bir_lowering=False, debug=False)

    def din(name, shape):
        return nc.dram_tensor(name, list(shape), F32, kind="ExternalInput").ap()

    xatt = din("xatt", [ROWS, DM])
    xout = din("xout", [ROWS, DM])
    wgate = din("wgate", [MC, 128, KC, 128])
    wup = din("wup", [MC, 128, KC, 128])
    wdown = din("wdown", [MC, 128, KC, 128])
    gsc = din("gsc", [128, MC])
    usc = din("usc", [128, MC])
    dsc = din("dsc", [128, KC])
    wq_h = din("wq_h", [128, 2, 40])
    wk_h = din("wk_h", [128, 2, 40])
    wb_h = din("wb_h", [128, 2, 128])
    qkvw_h = din("qkvw_h", [16, 48])
    projw1_h = din("projw1_h", [8, 16])
    projw2_h = din("projw2_h", [8, 16])
    projb_h = din("projb_h", [1, 16])
    convw0_h = din("convw0_h", [8, 9])
    convb0_h = din("convb0_h", [8, 1])
    convw1_h = din("convw1_h", [8, 9])
    convb1_h = din("convb1_h", [8, 1])
    lnw_h = din("lnw_h", [128, 16])
    lnb_h = din("lnb_h", [128, 16])
    ones_h = din("ones_h", [1, 128])
    ident_h = din("ident_h", [128, 128])
    mask1_h = din("mask1_h", [128, 128])
    mask2_h = din("mask2_h", [128, 128])
    cm1_h = din("cm1_h", [128, 128])
    cp1_h = din("cp1_h", [128, 128])
    cm2_h = din("cm2_h", [128, 128])
    cp2_h = din("cp2_h", [128, 128])

    o = nc.dram_tensor("o", [ROWS, DM], F32, kind="ExternalOutput").ap()
    s_dram = nc.dram_tensor("s_dram", [16, NLOC], F32, kind="Internal").ap()
    wt_dram = nc.dram_tensor("wt_dram", [16, 128], F32, kind="Internal").ap()

    ctx = []

    def sb(name, shape, dtype=F32):
        g = nc.sbuf_tensor(name, list(shape), dtype)
        t = g.__enter__()
        ctx.append(g)
        return t

    def psalloc(name):
        g = nc.psum_tensor(name, [128, 512], F32)
        t = g.__enter__()
        ctx.append(g)
        return t

    ident = sb("ident", [128, 128])
    mask1 = sb("mask1", [128, 128])
    mask2 = sb("mask2", [128, 128])
    cm1 = sb("cm1", [128, 128])
    cp1 = sb("cp1", [128, 128])
    cm2 = sb("cm2", [128, 128])
    cp2 = sb("cp2", [128, 128])
    wq_s = sb("wq_s", [128, 2, 40])
    wk_s = sb("wk_s", [128, 2, 40])
    wb_s = sb("wb_s", [128, 2, 128])
    qkvw_s = sb("qkvw_s", [16, 48])
    projw1_s = sb("projw1_s", [8, 16])
    projw2_s = sb("projw2_s", [8, 16])
    projb_s = sb("projb_s", [1, 16])
    cw0_s = sb("cw0_s", [8, 9])
    cb0_s = sb("cb0_s", [8, 1])
    cw1_s = sb("cw1_s", [8, 9])
    cb1_s = sb("cb1_s", [8, 1])
    ones1 = sb("ones1", [1, 128])
    beff = sb("beff", [128, 16])
    gamma = sb("gamma", [128, 16])
    beta = sb("beta", [128, 16])
    gsc_s = sb("gsc_s", [128, MC])
    usc_s = sb("usc_s", [128, MC])
    dsc_s = sb("dsc_s", [128, KC])
    wvt = sb("wvt", [8, 16])
    winit = sb("winit", [16, 128])
    smallv = sb("smallv", [128, 64])

    arA = sb("arA", [128, 2048])
    arB = sb("arB", [128, 12288])
    arC = sb("arC", [128, 4096])
    arD = sb("arD", [128, 4096])
    arE = sb("arE", [128, 4096])
    arF = sb("arF", [128, 4096])
    xsr = sb("xsr", [128, 8192], F32R)
    hr = sb("hr", [128, 2048], F32R)
    wr = sb("wr", [128, 8192], F32R)

    xnfm = arA[:, :].rearrange("p (k r) -> p k r", k=KC)
    qs = arB[0:48, 0:4096]
    ks = arB[0:48, 4096:8192]
    vs = arB[:, 8192:12288]
    y2cm = arC[0:16, 0:4096]
    tocm = arC[0:16, 0:4096]
    v1tm = arD[:, 0:528].rearrange("p (t c) -> p t c", c=66)
    v2tm = arD[:, 528:1056].rearrange("p (t c) -> p t c", c=66)
    yslab = arD[:, 1056:2336].rearrange("p (t c) -> p t c", c=YW)
    tokr = arD[:, 2336:2848].rearrange("p (t c) -> p t c", c=16)
    pbuf = [arE[:, 0:512], arE[:, 512:1024]]
    ptm = [arE[:, 1024:1536], arE[:, 1536:2048]]
    zslab = arE[:, 2048:2560].rearrange("p (t c) -> p t c", c=16)
    zsq = arE[:, 2560:3072].rearrange("p (t c) -> p t c", c=16)
    tokout = arE[:, 3072:3584].rearrange("p (t c) -> p t c", c=16)
    zred = arE[:, 3584:3616]
    y2zq = [arE[:, 3616:3744], arE[:, 3744:3872]]
    xa_tm = arF[:, 0:2048]
    sq_tm = arF[:, 2048:4096]

    xsv = xsr[:, 0:8192].rearrange("p (k t) -> p k t", k=KC)
    hjv = hr[:, 0:2048].rearrange("p (j t) -> p j t", j=4)
    hraw = [arB[:, 0:256], arB[:, 256:512]]
    hv1 = [arB[:, 512:768], arB[:, 768:1024]]
    gusum = [arB[:, 1024:1536], arB[:, 1536:2048]]
    dslab = arC[:, 0:4096].rearrange("p (k r) -> p k r", k=KC)
    xo_tm = arF[:, 0:2048]
    vhi_tm = arD[:, 0:2048]
    vlo_tm = arD[:, 2048:4096]
    x_tm = arD[:, 0:2048]
    wg_t = [wr[:, 0:2048].rearrange("p (k r) -> p k r", k=KC),
            wr[:, 2048:4096].rearrange("p (k r) -> p k r", k=KC)]
    wu_t = [wr[:, 4096:6144].rearrange("p (k r) -> p k r", k=KC),
            wr[:, 6144:8192].rearrange("p (k r) -> p k r", k=KC)]
    wd_t = wg_t
    g_tm = arF[:, 0:2048]
    s_tm = arF[:, 2048:4096]
    o_tm = s_tm

    y2z = [smallv[:, 0:16], smallv[:, 16:32]]
    rec4 = [smallv[:, 32:36], smallv[:, 36:40]]
    negm2 = [smallv[:, 40:44], smallv[:, 44:48]]
    negm1 = [smallv[:, 48:52], smallv[:, 52:56]]
    mu_c = smallv[:, 56:57]
    rs_c = smallv[:, 57:58]

    pA = psalloc("pA")
    pB = psalloc("pB")
    pC = psalloc("pC")
    pD = psalloc("pD")
    pE = psalloc("pE")
    pF = psalloc("pF")
    pG = psalloc("pG")
    pH = psalloc("pH")

    barg = nc.semaphore("bar")
    barsem = barg.__enter__()
    ctx.append(barg)
    gsems = []
    for gi in range(2):
        gg = nc.semaphore(f"gbar{gi}")
        gsems.append(gg.__enter__())
        ctx.append(gg)
    esems = {}
    for en in ("tensor", "vector", "scalar", "gpsimd"):
        gg = nc.semaphore(f"e_{en}")
        esems[en] = gg.__enter__()
        ctx.append(gg)
    blkg = nc.Block()
    block = blkg.__enter__()

    phases = []
    phase_deps = []

    def phase(dep=None, **fns):
        phases.append(fns)
        phase_deps.append(len(phases) - 2 if dep is None else dep)
        return len(phases) - 1

    def dma_fn(n):
        def wrap(fn):
            fn.dma_incs = n
            return fn
        return wrap

    def sw_dma_fn(n, gi=0):
        def wrap(fn):
            fn.sw_dma_incs = n
            fn.sw_sem = gi
            return fn
        return wrap


    @dma_fn(24)
    def init_loads(e):
        for src, dst in (
            (ident_h, ident[:]), (mask1_h, mask1[:]), (mask2_h, mask2[:]),
            (cm1_h, cm1[:]), (cp1_h, cp1[:]), (cm2_h, cm2[:]), (cp2_h, cp2[:]),
            (wq_h, wq_s[:]), (wk_h, wk_s[:]), (wb_h, wb_s[:]),
            (qkvw_h, qkvw_s[:]), (projw1_h, projw1_s[:]),
            (projw2_h, projw2_s[:]), (projb_h, projb_s[:]),
            (convw0_h, cw0_s[:]), (convb0_h, cb0_s[:]),
            (convw1_h, cw1_s[:]), (convb1_h, cb1_s[:]),
            (lnw_h, gamma[:]), (lnb_h, beta[:]), (ones_h, ones1[:]),
            (gsc, gsc_s[:]), (usc, usc_s[:]), (dsc, dsc_s[:]),
        ):
            e.dma_start(out=dst, in_=src).then_inc(barsem, 16)
    phase(sync=init_loads)

    def init_memset(e):
        e.memset(winit[:], 0.0)
        e.memset(v1tm[:, :, 16:17], 1.0)
        e.memset(v2tm[:, :, 16:17], 1.0)
        e.memset(y2cm[:, :], 0.0)
    phase(gpsimd=init_memset)

    def init_scaleq(e):
        e.tensor_scalar_mul(wq_s[:, :, :], wq_s[:, :, :], SCALE)
    phase(vector=init_scaleq)

    def init_wvT(e):
        e.transpose(pA[0:8, 0:16], qkvw_s[0:16, 32:40], ident[0:16, 0:16])
        e.transpose(pA[0:8, 16:32], qkvw_s[0:16, 40:48], ident[0:16, 0:16])
    phase(tensor=init_wvT)

    def init_wvT_ev(e):
        e.tensor_copy(winit[0:8, 0:32], pA[0:8, 0:32])
    phase(vector=init_wvT_ev)

    jobs = [
        (0,   (0, 16),  None,            projw1_s, True, True),
        (16,  (0, 16),  cw0_s[:, 3:4],   projw1_s, True, True),
        (32,  (0, 16),  cw0_s[:, 5:6],   projw1_s, True, True),
        (64,  (16, 32), None,            projw2_s, True, True),
        (80,  (16, 32), cw1_s[:, 1:2],   projw2_s, True, True),
        (96,  (16, 32), cw1_s[:, 7:8],   projw2_s, True, True),
        (112, (0, 16),  cw0_s[:, 4:5],   projw1_s, True, False),
        (112, (16, 32), cw1_s[:, 4:5],   projw2_s, False, True),
    ]
    for col, (c0, c1), diag, pr, st, sp in jobs:
        def vj(e, c0=c0, c1=c1, diag=diag):
            if diag is None:
                e.tensor_copy(wvt[:], winit[0:8, c0:c1])
            else:
                e.tensor_scalar(wvt[:], winit[0:8, c0:c1], diag, None, ALU.mult)
        phase(vector=vj)

        def tj(e, col=col, pr=pr, st=st, sp=sp):
            e.matmul(pB[0:16, col:col + 16], wvt[:], pr[:],
                     start=st, stop=sp)
        phase(tensor=tj)

    def init_beff_mm(e):
        e.matmul(pC[0:1, 0:16], cb0_s[:], projw1_s[:], start=True, stop=False)
        e.matmul(pC[0:1, 0:16], cb1_s[:], projw2_s[:], start=False, stop=True)
    phase(tensor=init_beff_mm)

    def init_beff_add(e):
        e.tensor_add(winit[0:1, 48:64], pC[0:1, 0:16], projb_s[:])
    phase(vector=init_beff_add)

    def init_beff_bc(e):
        e.matmul(pD[0:128, 0:16], ones1[:], winit[0:1, 48:64],
                 start=True, stop=True)
    phase(tensor=init_beff_bc)

    def init_beff_ev(e):
        e.tensor_copy(beff[:], pD[0:128, 0:16])
    phase(vector=init_beff_ev)

    def init_vt_evac(e):
        e.tensor_copy(winit[:, 0:48], pB[0:16, 0:48])
        e.tensor_copy(winit[:, 64:128], pB[0:16, 64:128])
    phase(vector=init_vt_evac)

    @dma_fn(1)
    def init_vt_out(e):
        e.dma_start(out=wt_dram, in_=winit[:]).then_inc(barsem, 16)
    phase(sync=init_vt_out)

    @dma_fn(16)
    def init_vt_in(e):
        for par in range(2):
            for b in range(4):
                r0 = 32 * b + 16 * par
                e.dma_start(out=wb_s[r0:r0 + 16, par, 0:48],
                            in_=wt_dram[:, 0:48]).then_inc(barsem, 16)
                e.dma_start(out=wb_s[r0:r0 + 16, par, 64:128],
                            in_=wt_dram[:, 64:128]).then_inc(barsem, 16)
    phase(sync=init_vt_in)

    def ln_tile(src_dram, dst_tm, tile):
        @dma_fn(1)
        def load(e, tile=tile):
            e.dma_start(out=dst_tm, in_=src_dram[tile * 128:(tile + 1) * 128, :]
                        ).then_inc(barsem, 16)
        phase(sync=load)

        def mean(e):
            e.reduce_sum(mu_c[:], dst_tm, axis=AX.X, negate=True)
            e.drain()
            e.tensor_scalar_mul(mu_c[:], mu_c[:], 1.0 / DM)
            e.drain()
            return e.tensor_scalar(dst_tm, dst_tm, mu_c[:], None, ALU.add)
        phase(vector=mean)

        def sq(e):
            return e.activation(sq_tm[:], dst_tm, AF.Square)
        phase(scalar=sq)

        def var(e):
            e.reduce_sum(rs_c[:], sq_tm[:], axis=AX.X)
            e.drain()
            return e.tensor_scalar(rs_c[:], rs_c[:], 1.0 / DM, EPS,
                                   ALU.mult, ALU.add)
        phase(vector=var)

        def rsq(e):
            e.activation(rs_c[:], rs_c[:], AF.Ln)
            e.drain()
            return e.activation(rs_c[:], rs_c[:], AF.Exp, scale=-0.5)
        phase(scalar=rsq)

        def fin(e):
            return e.tensor_scalar(dst_tm, dst_tm, rs_c[:], None, ALU.mult)
        phase(vector=fin)

    qps = (pA, pB, pC, pD)
    kps = (pE, pF, pG, pH)
    for m in range(GROUPS):
        if m % 4 == 0:
            tile8 = m // 4
            ln_tile(xatt, xa_tm[:, :], tile8)
            for jh in range(4):
                def tp(e, jh=jh):
                    h = None
                    for j2 in range(4):
                        j = jh * 4 + j2
                        h = e.transpose(pA[:, j2 * 128:(j2 + 1) * 128],
                                        xa_tm[:, j * 128:(j + 1) * 128],
                                        ident[:])
                    return h
                phase(tensor=tp)

                def tpe(e, jh=jh):
                    h = None
                    for j2 in range(4):
                        j = jh * 4 + j2
                        h = e.copy(xnfm[:, j, :],
                                   pA[:, j2 * 128:(j2 + 1) * 128])
                    return h
                phase(scalar=tpe)

        def rhs_ap(blkb, m=m):
            mm4 = m % 4
            return xnfm[32 * blkb:32 * blkb + 32, :, mm4 * 32:mm4 * 32 + 32]

        for par in range(2):
            def qkv_mm(e, par=par, rhs_ap=rhs_ap):
                h = None
                for blkb in range(4):
                    tp = (32 * blkb, 0)
                    r = rhs_ap(blkb)
                    e.matmul(qps[blkb][0:40, :],
                             wq_s[32 * blkb:32 * blkb + 32, par, :],
                             r, start=True, stop=True, tile_position=tp)
                    h = e.matmul(kps[blkb][0:40, :],
                                 wk_s[32 * blkb:32 * blkb + 32, par, :],
                                 r, start=True, stop=True, tile_position=tp)
                return h
            phase(tensor=qkv_mm)

            def qkv_ev(e, par=par):
                h = None
                for blkb in range(4):
                    sp = 2 * blkb + par
                    for slab, psl in ((qs, qps[blkb]), (ks, kps[blkb])):
                        d1 = slab[0:8, :].rearrange(
                            "p (i b s) -> p i b s", i=32, b=16)[
                            :, :, :, sp].transpose([0, 2, 1])
                        e.copy(d1, psl[0:8, :].rearrange(
                            "p (b i) -> p b i", b=16))
                        d2 = slab[32:40, :].rearrange(
                            "p (b s r g) -> p b s r g", b=16, s=8, r=4)[
                            :, :, sp, :, :].transpose([0, 1, 3, 2])
                        h = e.copy(d2, psl[32:40, :].rearrange(
                            "p (b g r) -> p b g r", b=16, g=8))
                return h
            phase(scalar=qkv_ev)

        for par in range(2):
            def qkvB_mm(e, par=par, rhs_ap=rhs_ap):
                h = None
                for blkb in range(4):
                    h = e.matmul(qps[blkb][:, :],
                                 wb_s[32 * blkb:32 * blkb + 32, par, :],
                                 rhs_ap(blkb), start=True, stop=True,
                                 tile_position=(32 * blkb, 0))
                return h
            phase(tensor=qkvB_mm)

            def qkvB_ev(e, par=par):
                h = None
                for blkb in range(4):
                    sp = 2 * blkb + par
                    d1 = vs[0:64, :].rearrange(
                        "p (i b s) -> p i b s", i=32, b=16)[
                        :, :, :, sp].transpose([0, 2, 1])
                    e.tensor_copy(d1, qps[blkb][0:64, :].rearrange(
                        "p (b i) -> p b i", b=16))
                    d2 = vs[64:128, :].rearrange(
                        "p (b s r g) -> p b s r g", b=16, s=8, r=4)[
                        :, :, sp, :, :].transpose([0, 1, 3, 2])
                    h = e.tensor_copy(d2, qps[blkb][64:128, :].rearrange(
                        "p (b g r) -> p b g r", b=16, g=8))
                return h
            phase(vector=qkvB_ev)

        Sb = (pA, pB)
        PTb = (pC, pD)
        VTb = (pE, pF)
        Yb = (pG, pH)

        def run_pass(qrow, krow, vrow0, vrow1, vident, vwork, negmP, is_b1):
            p_mm = {}
            p_ng = {}
            p_exp = {}
            p_ptr = {}
            p_pev = {}
            p_av = {}
            p_tail = {}
            p_rc = {}
            p_dvt = {}
            p_y2ev = {}

            def st_mm(st):
                par = st % 2
                tl = [4 * st + i for i in range(4)]

                def mmS(e, tl=tl, par=par):
                    h = None
                    for i, t2 in enumerate(tl):
                        h = e.matmul(Sb[par][:, i * 128:(i + 1) * 128],
                                     qs[qrow[0]:qrow[1],
                                        t2 * 128:(t2 + 1) * 128],
                                     ks[qrow[0]:qrow[1],
                                        t2 * 128:(t2 + 1) * 128],
                                     start=True, stop=True,
                                     tile_position=(qrow[0] - qrow[0] % 32, 0)
                                     if qrow[0] >= 32 else None)
                    return h
                p_mm[st] = phase(dep=p_exp.get(st - 2), tensor=mmS)

            def st_ng(st):
                par = st % 2

                def negmax(e, par=par):
                    return e.reduce_max(negmP[par][:],
                                        Sb[par][:, :].rearrange(
                                            "p (t x) -> p t x", t=4),
                                        axis=AX.X, negate=True)
                p_ng[st] = phase(dep=p_mm[st], vector=negmax)

            def st_exp(st):
                par = st % 2

                def expS(e, par=par):
                    h = None
                    for i in range(4):
                        h = e.activation(pbuf[par][:, i * 128:(i + 1) * 128],
                                         Sb[par][:, i * 128:(i + 1) * 128],
                                         AF.Exp, bias=negmP[par][:, i:i + 1])
                    return h
                dep = p_ng[st]
                if st - 2 in p_ptr:
                    dep = max(dep, p_ptr[st - 2])
                p_exp[st] = phase(dep=dep, scalar=expS)

            def st_ptr(st):
                par = st % 2
                tl = [4 * st + i for i in range(4)]

                def ptrans(e, tl=tl, par=par):
                    for i in range(4):
                        e.transpose(PTb[par][:, i * 128:(i + 1) * 128],
                                    pbuf[par][:, i * 128:(i + 1) * 128],
                                    ident[:])
                    h = None
                    for i, t2 in enumerate(tl):
                        h = e.transpose(VTb[par][:, i * 128:i * 128 + 64],
                                        vs[vrow0:vrow1,
                                           t2 * 128:(t2 + 1) * 128], vident)
                    return h
                dep = p_exp[st]
                if st - 2 in p_pev:
                    dep = max(dep, p_pev[st - 2])
                if st - 2 in p_y2ev:
                    dep = max(dep, p_y2ev[st - 2])
                p_ptr[st] = phase(dep=dep, tensor=ptrans)

            def st_pev(st):
                par = st % 2
                tl = [4 * st + i for i in range(4)]

                def pt_evac(e, tl=tl, par=par):
                    h = e.tensor_mul(ptm[par][:, :].rearrange(
                        "p (t x) -> p t x", t=4),
                        PTb[par][:, :].rearrange("p (t x) -> p t x", t=4),
                        mask1[:].unsqueeze(1).broadcast_to([128, 4, 128]))
                    if is_b1:
                        for i, ip in enumerate(tl):
                            h = e.tensor_add(vwork[:, par * 4 + i, 0:16],
                                             VTb[par][:, i * 128:i * 128 + 16],
                                             beff[:])
                    return h
                dep = p_ptr[st]
                if st - 2 in p_av:
                    dep = max(dep, p_av[st - 2])
                pv = phase(dep=dep, vector=pt_evac)

                def pt_evac_s(e, tl=tl, par=par):
                    h = None
                    for i, ip in enumerate(tl):
                        if is_b1:
                            e.copy(vwork[:, par * 4 + i, 17:49],
                                   VTb[par][:, i * 128 + 16:i * 128 + 48])
                            h = e.copy(tokr[:, ip, :],
                                       VTb[par][:, i * 128 + 48:i * 128 + 64])
                        else:
                            e.copy(vwork[:, par * 4 + i, 0:16],
                                   VTb[par][:, i * 128:i * 128 + 16])
                            h = e.copy(vwork[:, par * 4 + i, 17:65],
                                       VTb[par][:, i * 128 + 16:i * 128 + 64])
                    return h
                p_pev[st] = phase(dep=max(dep, pv), scalar=pt_evac_s)

            def st_av(st):
                par = st % 2
                tl = [4 * st + i for i in range(4)]

                def av(e, tl=tl, par=par):
                    for i, ip in enumerate(tl):
                        lhs = ptm[par][:, i * 128:(i + 1) * 128]
                        e.matmul(Yb[par][:, i * 17:i * 17 + 17], lhs,
                                 vwork[:, par * 4 + i, 0:17],
                                 start=True, stop=True)
                    lep = Yb[par][:, 72:136]
                    e.matmul(lep, cm1[:], vwork[:, par * 4:par * 4 + 4, 17:33],
                             start=True, stop=False)
                    h = e.matmul(lep, cp1[:],
                                 vwork[:, par * 4:par * 4 + 4, 33:49],
                                 start=False, stop=False)
                    if not is_b1:
                        h = e.matmul(lep, ident[:],
                                     vwork[:, par * 4:par * 4 + 4, 49:65],
                                     start=False, stop=True)
                    else:
                        for i, ip in enumerate(tl):
                            h = e.matmul(Yb[par][:, 72 + i * 16:88 + i * 16],
                                         y2cm[:, ip * 128:(ip + 1) * 128],
                                         ident[0:16, 0:16], start=False,
                                         stop=(i == 3), is_transpose=True)
                    return h
                dep = p_pev[st]
                if st - 2 in p_tail:
                    dep = max(dep, p_tail[st - 2])
                p_av[st] = phase(dep=dep, tensor=av)

            def st_tail_b1(st):
                par = st % 2
                tl = [4 * st + i for i in range(4)]

                def y1ev(e, st=st, par=par):
                    e.copy(yslab[:, 4 * st:4 * st + 4, 0:17],
                           Yb[par][:, 0:68].rearrange(
                               "p (i c) -> p i c", c=17))
                    return e.copy(yslab[:, 4 * st:4 * st + 4, 20:36],
                                  Yb[par][:, 72:136].rearrange(
                                      "p (i c) -> p i c", c=16))
                p_tail[st] = phase(dep=p_av[st], scalar=y1ev)

            def st_rc(st):
                par = st % 2

                def recs2(e, par=par):
                    dv = Yb[par][:, 0:68].rearrange(
                        "p (t c) -> p t c", c=17)[:, :, 16]
                    e.activation(rec4[par][:], dv, AF.Ln)
                    e.drain()
                    return e.activation(rec4[par][:], rec4[par][:], AF.Exp,
                                        scale=-1.0)
                p_rc[st] = phase(dep=p_av[st], scalar=recs2)

            def st_dvt(st):
                par = st % 2

                def y2div(e, par=par):
                    yav = Yb[par][:, 0:68].rearrange(
                        "p (i c) -> p i c", c=17)[:, :, 0:16]
                    lep = Yb[par][:, 72:136].rearrange(
                        "p (i c) -> p i c", c=16)
                    zq = y2zq[par].rearrange(
                        "p (i c) -> p i c", c=32)[:, :, 0:16]
                    e.tensor_mul(zq, yav,
                                 rec4[par][:].unsqueeze(2).broadcast_to(
                                     [128, 4, 16]))
                    e.drain()
                    return e.tensor_add(zq, zq, lep)
                dep = p_rc[st]
                if st - 2 in p_tail:
                    dep = max(dep, p_tail[st - 2])
                p_dv = phase(dep=dep, vector=y2div)

                def y2t(e, par=par):
                    return e.transpose(PTb[par][0:128, 0:128],
                                       y2zq[par][:], ident[:])
                p_tail[st] = phase(dep=p_dv, tensor=y2t)

            def st_y2ev(st):
                par = st % 2
                tl = [4 * st + i for i in range(4)]

                def y2ev(e, tl=tl, par=par):
                    hh = None
                    for i, t2 in enumerate(tl):
                        b0, h = t2 // 2, t2 % 2
                        dst = y2cm[:, :].rearrange(
                            "p (g r b s) -> p g r b s", g=8, r=4, b=16)[
                            :, :, :, b0, 4 * h:4 * h + 4].transpose(
                            [0, 3, 2, 1])
                        hh = e.tensor_copy(
                            dst, PTb[par][32 * i:32 * i + 16, 0:128]
                            .rearrange("p (s r g) -> p s r g", s=4, r=4))
                    return hh
                p_y2ev[st] = phase(dep=p_tail[st], vector=y2ev)

            if is_b1:
                stages = [st_mm, st_ng, st_exp, st_ptr, st_pev, st_av,
                          st_tail_b1]
            else:
                stages = [st_mm, st_ng, st_exp, st_ptr, st_pev, st_av,
                          st_rc, st_dvt, st_y2ev]
            ns = len(stages)
            II = 3
            for t in range(II * 7 + ns):
                for gidx in range(ns):
                    if (t - gidx) % II == 0:
                        st = (t - gidx) // II
                        if 0 <= st < 8:
                            stages[gidx](st)

        run_pass((32, 40), None, 64, 128, ident[64:128, 64:128], v2tm,
                 negm2, False)
        run_pass((0, 8), None, 0, 64, ident[0:64, 0:64], v1tm, negm1, True)

        def ln16r(e):
            e.activation(zred[:], yslab[:, :, 16], AF.Ln)
            e.drain()
            return e.activation(zred[:], zred[:], AF.Exp, scale=-1.0)
        phase(scalar=ln16r)

        def ln16a(e):
            e.tensor_mul(zslab[:, :, :], yslab[:, :, 0:16],
                         zred[:].unsqueeze(2).broadcast_to(
                             [128, 32, 16]))
            e.drain()
            e.tensor_add(zslab[:, :, :], zslab[:, :, :], yslab[:, :, 20:36])
            e.drain()
            e.reduce_sum(zred[:], zslab[:, :, :], axis=AX.X, negate=True)
            e.drain()
            e.tensor_scalar_mul(zred[:], zred[:], 1.0 / 16.0)
            e.drain()
            return e.tensor_add(zslab[:, :, :], zslab[:, :, :],
                                zred[:].unsqueeze(2).broadcast_to(
                                    [128, 32, 16]))
        phase(vector=ln16a)

        def ln16sq(e):
            return e.activation(zsq[:, :, :], zslab[:, :, :], AF.Square)
        phase(scalar=ln16sq)

        def ln16b(e):
            e.reduce_sum(zred[:], zsq[:, :, :], axis=AX.X)
            e.drain()
            return e.tensor_scalar(zred[:], zred[:], 1.0 / 16.0, EPS,
                                   ALU.mult, ALU.add)
        phase(vector=ln16b)

        def ln16c(e):
            e.activation(zred[:], zred[:], AF.Ln)
            e.drain()
            return e.activation(zred[:], zred[:], AF.Exp, scale=-0.5)
        phase(scalar=ln16c)

        def ln16d(e):
            e.tensor_mul(zslab[:, :, :], zslab[:, :, :],
                         zred[:].unsqueeze(2).broadcast_to(
                             [128, 32, 16]))
            e.drain()
            e.tensor_mul(zslab[:, :, :], zslab[:, :, :],
                         gamma[:].unsqueeze(1).broadcast_to(
                             [128, 32, 16]))
            e.drain()
            e.tensor_add(zslab[:, :, :], zslab[:, :, :],
                         beta[:].unsqueeze(1).broadcast_to(
                             [128, 32, 16]))
            e.drain()
            return e.tensor_add(tokout[:, :, :], zslab[:, :, :],
                                tokr[:, :, :])
        phase(vector=ln16d)

        for q in range(4):
            def toT(e, q=q):
                h = None
                for j in range(4):
                    e.transpose(pA[0:16, j * 128:(j + 1) * 128],
                                tokout[:, q * 8 + j, :], ident[:])
                for j in range(4):
                    h = e.transpose(pB[0:16, j * 128:(j + 1) * 128],
                                    tokout[:, q * 8 + 4 + j, :], ident[:])
                return h
            phase(tensor=toT)

            def toEv(e, q=q):
                e.copy(tocm[:, q * 1024:q * 1024 + 512], pA[0:16, :])
                return e.copy(
                    tocm[:, q * 1024 + 512:(q + 1) * 1024], pB[0:16, :])
            phase(scalar=toEv)

        @dma_fn(1)
        def scat(e, m=m):
            e.dma_start(out=s_dram[:, m * GMICRO:(m + 1) * GMICRO],
                        in_=tocm[:, :]).then_inc(barsem, 16)
        phase(sync=scat)


    allp = (pA, pB, pC, pD, pE, pF, pG, pH)
    for tq in range(4):
        for tile in range(2):
            ln_tile(xout, xo_tm, tq * 2 + tile)

            def velt(e):
                e.tensor_scalar_mul(sq_tm[:], xo_tm[:], VC)
                e.drain()
                e.tensor_sub(vlo_tm[:], sq_tm[:], xo_tm[:])
                e.drain()
                e.tensor_sub(vhi_tm[:], sq_tm[:], vlo_tm[:])
                e.drain()
                return e.tensor_sub(vlo_tm[:], xo_tm[:], vhi_tm[:])
            phase(vector=velt)

            for jh in range(4):
                def ftp(e, jh=jh):
                    h = None
                    for j2 in range(4):
                        j = jh * 4 + j2
                        e.transpose(pA[:, j2 * 128:(j2 + 1) * 128],
                                    vhi_tm[:, j * 128:(j + 1) * 128], ident[:])
                        h = e.transpose(pB[:, j2 * 128:(j2 + 1) * 128],
                                        vlo_tm[:, j * 128:(j + 1) * 128],
                                        ident[:])
                    return h
                phase(tensor=ftp)

                def ftpe(e, tile=tile, jh=jh):
                    h = None
                    for j2 in range(4):
                        j = jh * 4 + j2
                        e.tensor_copy(xsv[:, j, tile * 128:(tile + 1) * 128],
                                      pA[:, j2 * 128:(j2 + 1) * 128])
                        h = e.tensor_copy(
                            xsv[:, j, 256 + tile * 128:256 + (tile + 1) * 128],
                            pB[:, j2 * 128:(j2 + 1) * 128])
                    return h
                phase(vector=ftpe)

        gu_banks = ((pA, pB), (pC, pD))
        ph_wload = {}
        ph_gu = {}
        ph_gadd = {}
        ph_hmake = {-1: None, -2: None}
        ph_dacc_last = None
        ph_dmm_last = None
        for jb in range(11):
            for jj in range(4):
                j = jb * 4 + jj

                @sw_dma_fn(2, j % 2)
                def wload(e, j=j):
                    e.dma_start(out=wg_t[j % 2][:, :, :], in_=wgate[j, :, :, :]
                                ).then_inc(gsems[j % 2], 16)
                    e.dma_start(out=wu_t[j % 2][:, :, :], in_=wup[j, :, :, :]
                                ).then_inc(gsems[j % 2], 16)
                dep = ph_gu.get(j - 2)
                if dep is not None and ph_dmm_last is not None:
                    dep = max(dep, ph_dmm_last)
                ph_wload[j] = phase(dep=dep, gpsimd=wload)

                def gu_mm(e, j=j):
                    bg, bu = gu_banks[j % 2]
                    h = None
                    for k in range(KC):
                        e.matmul(bg[:, 0:512], wg_t[j % 2][:, k, :],
                                 xsv[:, k, :], start=(k == 0),
                                 stop=(k == KC - 1))
                    for k in range(KC):
                        h = e.matmul(bu[:, 0:512], wu_t[j % 2][:, k, :],
                                     xsv[:, k, :], start=(k == 0),
                                     stop=(k == KC - 1))
                    return h
                gdep = ph_wload[j]
                if ph_gadd.get(j - 2) is not None:
                    gdep = max(gdep, ph_gadd[j - 2])
                if ph_dacc_last is not None:
                    gdep = max(gdep, ph_dacc_last)
                ph_gu[j] = phase(dep=gdep, tensor=gu_mm)

                def gadd(e, j=j):
                    bg, bu = gu_banks[j % 2]
                    e.reduce_sum(gusum[j % 2][:, 0:256],
                                 bg[:, 0:512].rearrange(
                                     "p (h t) -> p h t", h=2).transpose(
                                     [0, 2, 1]), axis=AX.X)
                    return e.reduce_sum(gusum[j % 2][:, 256:512],
                                        bu[:, 0:512].rearrange(
                                            "p (h t) -> p h t", h=2).transpose(
                                            [0, 2, 1]), axis=AX.X)
                ph_gadd[j] = phase(dep=ph_gu[j], vector=gadd)

                def gsilu(e, j=j):
                    return e.activation(hv1[j % 2][:], gusum[j % 2][:, 0:256],
                                        AF.Silu, scale=gsc_s[:, j:j + 1])
                phase(dep=ph_gadd[j], scalar=gsilu)

                def hmake(e, j=j, jj=jj):
                    hw = hraw[j % 2]
                    e.scalar_tensor_tensor(hw[:], gusum[j % 2][:, 256:512],
                                           usc_s[:, j:j + 1], hv1[j % 2][:],
                                           ALU.mult, ALU.mult)
                    e.drain()
                    e.tensor_scalar_mul(sq_tm[:, 0:256], hw[:], VC)
                    e.drain()
                    e.tensor_sub(sq_tm[:, 256:512], sq_tm[:, 0:256], hw[:])
                    e.drain()
                    e.tensor_sub(hjv[:, jj, 0:256], sq_tm[:, 0:256],
                                 sq_tm[:, 256:512])
                    e.drain()
                    return e.tensor_sub(hjv[:, jj, 256:512], hw[:],
                                        hjv[:, jj, 0:256])
                ph_hmake[j] = phase(vector=hmake)

            ph_dload = {}
            ph_dmm = {}
            for jj in range(4):
                j = jb * 4 + jj

                @sw_dma_fn(1, j % 2)
                def dload(e, j=j):
                    e.dma_start(out=wd_t[j % 2][:, :, :], in_=wdown[j, :, :, :]
                                ).then_inc(gsems[j % 2], 16)
                ddep = ph_dmm.get(jj - 2)
                ph_dload[jj] = phase(dep=ddep, gpsimd=dload)

                def d_mm(e, jj=jj, j=j):
                    h = None
                    for oc in range(KC):
                        pp = allp[oc % 8]
                        c0 = (oc // 8) * 256
                        e.matmul(pp[:, c0:c0 + 256], wd_t[j % 2][:, oc, :],
                                 hjv[:, jj, 0:256],
                                 start=(jj == 0 and oc < 8), stop=False)
                        h = e.matmul(pp[:, c0:c0 + 256], wd_t[j % 2][:, oc, :],
                                     hjv[:, jj, 256:512], start=False,
                                     stop=(jj == 3 and oc >= 8))
                    return h
                ph_dmm[jj] = phase(dep=ph_dload[jj], tensor=d_mm)
                ph_dmm_last = ph_dmm[jj]

            def dacc(e, jb=jb):
                h = None
                for oc in range(KC):
                    pp = allp[oc % 8]
                    c0 = (oc // 8) * 256
                    if jb == 0:
                        h = e.tensor_copy(dslab[:, oc, :], pp[:, c0:c0 + 256])
                    else:
                        h = e.tensor_add(dslab[:, oc, :], dslab[:, oc, :],
                                         pp[:, c0:c0 + 256])
                return h
            ph_dacc_last = phase(dep=ph_dmm[3], vector=dacc)

        def gfin(e):
            h = None
            for oc in range(KC):
                h = e.activation(dslab[:, oc, :], dslab[:, oc, :], AF.Silu,
                                 scale=dsc_s[:, oc:oc + 1])
            return h
        phase(scalar=gfin)

        for tb in range(2):
            row0 = tq * 256 + tb * 128

            def gT(e, tb=tb):
                for oc in range(4):
                    e.transpose(pA[:, oc * 128:(oc + 1) * 128],
                                dslab[:, oc, tb * 128:(tb + 1) * 128], ident[:])
                for oc in range(4):
                    e.transpose(pB[:, oc * 128:(oc + 1) * 128],
                                dslab[:, 4 + oc, tb * 128:(tb + 1) * 128],
                                ident[:])
                for oc in range(4):
                    e.transpose(pC[:, oc * 128:(oc + 1) * 128],
                                dslab[:, 8 + oc, tb * 128:(tb + 1) * 128],
                                ident[:])
                h = None
                for oc in range(4):
                    h = e.transpose(pD[:, oc * 128:(oc + 1) * 128],
                                    dslab[:, 12 + oc, tb * 128:(tb + 1) * 128],
                                    ident[:])
                return h
            phase(tensor=gT)

            def gEv(e):
                e.copy(g_tm[:, 0:512], pA[:, :])
                e.copy(g_tm[:, 512:1024], pB[:, :])
                e.copy(g_tm[:, 1024:1536], pC[:, :])
                return e.copy(g_tm[:, 1536:2048], pD[:, :])
            phase(scalar=gEv)

            @dma_fn(2)
            def fload(e, row0=row0):
                e.dma_start(out=x_tm[:], in_=xout[row0:row0 + 128, :]
                            ).then_inc(barsem, 16)
                ta = row0 // 64
                e.dma_start(
                    out=s_tm[:],
                    in_=s_dram.rearrange("c (r j) -> c r j", j=DM)[
                        ta:ta + 2, 0:64, :]).then_inc(barsem, 16)
            phase(sync=fload)

            def fuse(e):
                e.tensor_mul(o_tm[:], s_tm[:], g_tm[:])
                e.drain()
                return e.tensor_add(o_tm[:], o_tm[:], x_tm[:])
            phase(vector=fuse)

            @dma_fn(1)
            def fstore(e, row0=row0):
                e.dma_start(out=o[row0:row0 + 128, :], in_=o_tm[:]
                            ).then_inc(barsem, 16)
            phase(sync=fstore)

    if _PHASE_LIMIT is not None:
        del phases[_PHASE_LIMIT:]
    allsems = ([esems["tensor"], esems["vector"], esems["scalar"],
                esems["gpsimd"], barsem] + gsems)
    eidx = {"tensor": 0, "vector": 1, "scalar": 2, "gpsimd": 3}
    NS = 7
    cum = []
    tot = [0] * NS
    for p in phases:
        for en, fn in p.items():
            d = getattr(fn, "dma_incs", 0)
            g = getattr(fn, "sw_dma_incs", 0)
            if g:
                tot[5 + getattr(fn, "sw_sem", 0)] += 16 * g
            elif d:
                tot[4] += 16 * d
            else:
                tot[eidx[en]] += 1
        cum.append(tuple(tot))
    finals = tuple(tot)

    for ename in ("sync", "gpsimd", "tensor", "vector", "scalar"):
        dec = getattr(block, ename)

        def mk(ename=ename):
            def body(eng):
                lastw = [0] * NS
                for pi, p in enumerate(phases):
                    if ename not in p:
                        continue
                    dep = phase_deps[pi]
                    if dep is not None and dep >= len(phases):
                        dep = len(phases) - 1
                    tgt = cum[dep] if (dep is not None and dep >= 0) \
                        else (0,) * NS
                    for si in range(NS):
                        t2 = tgt[si]
                        if t2 > lastw[si]:
                            eng.wait_ge(allsems[si], t2)
                            lastw[si] = t2
                    fn = p[ename]
                    h = fn(eng)
                    if (getattr(fn, "dma_incs", 0) == 0
                            and getattr(fn, "sw_dma_incs", 0) == 0):
                        if h is None:
                            h = eng.drain()
                        h.then_inc(esems[ename], 1)
                if ename == "sync":
                    for si in range(NS):
                        if finals[si] > 0:
                            eng.wait_ge(allsems[si], finals[si])
            return body
        dec(mk())

    blkg.__exit__(None, None, None)
    for g in reversed(ctx):
        g.__exit__(None, None, None)
    nc._phases_debug = (list(phase_deps),
                        [[(en, fn.__name__) for en, fn in p.items()]
                         for p in phases], cum)
    return nc


_PHASE_LIMIT = None
_PROG = None


def _get_prog():
    global _PROG
    if _PROG is None:
        _PROG = build_program()
    return _PROG


def kernel(L0_attn_x, L0_attn_qkv_w, L0_attn_conv_w0, L0_attn_conv_b0,
           L0_attn_conv_w1, L0_attn_conv_b1, L0_attn_proj_w, L0_attn_proj_b,
           L0_attn_ln_w, L0_attn_ln_b, L0_ffn_gate_w, L0_ffn_up_w,
           L0_ffn_down_w, L0_ffn_gate_s, L0_ffn_up_s, L0_ffn_down_s):
    f32 = np.float32
    X = np.ascontiguousarray(np.asarray(L0_attn_x, dtype=f32).reshape(8192, DM))
    qkv_w = np.asarray(L0_attn_qkv_w, dtype=f32)
    consts = _host_consts()
    wq, wk = _host_wq_wk(qkv_w)
    wb = _host_wb_skeleton()

    gate_w = np.asarray(L0_ffn_gate_w, dtype=f32)
    up_w = np.asarray(L0_ffn_up_w, dtype=f32)
    down_w = np.asarray(L0_ffn_down_w, dtype=f32)
    WG = np.ascontiguousarray(
        gate_w.reshape(MC, 128, KC, 128).transpose(0, 3, 2, 1))
    WU = np.ascontiguousarray(
        up_w.reshape(MC, 128, KC, 128).transpose(0, 3, 2, 1))
    WD = np.ascontiguousarray(
        down_w.reshape(KC, 128, MC, 128).transpose(2, 3, 0, 1))

    gsch = np.ascontiguousarray(
        np.asarray(L0_ffn_gate_s, dtype=f32).reshape(MC, 128).T)
    usch = np.ascontiguousarray(
        np.asarray(L0_ffn_up_s, dtype=f32).reshape(MC, 128).T)
    dsch = np.ascontiguousarray(
        np.asarray(L0_ffn_down_s, dtype=f32).reshape(KC, 128).T)

    shared = {
        "wgate": WG, "wup": WU, "wdown": WD,
        "gsc": gsch, "usc": usch, "dsc": dsch,
        "wq_h": np.ascontiguousarray(wq.transpose(1, 0, 2)),
        "wk_h": np.ascontiguousarray(wk.transpose(1, 0, 2)),
        "wb_h": np.ascontiguousarray(wb.transpose(1, 0, 2)),
        "qkvw_h": qkv_w,
        "projw1_h": np.ascontiguousarray(np.asarray(L0_attn_proj_w, f32)[0:8]),
        "projw2_h": np.ascontiguousarray(np.asarray(L0_attn_proj_w, f32)[8:16]),
        "projb_h": np.asarray(L0_attn_proj_b, f32).reshape(1, 16),
        "convw0_h": np.asarray(L0_attn_conv_w0, f32).reshape(8, 9),
        "convb0_h": np.asarray(L0_attn_conv_b0, f32).reshape(8, 1),
        "convw1_h": np.asarray(L0_attn_conv_w1, f32).reshape(8, 9),
        "convb1_h": np.asarray(L0_attn_conv_b1, f32).reshape(8, 1),
        "lnw_h": np.ascontiguousarray(
            np.broadcast_to(np.asarray(L0_attn_ln_w, f32).reshape(1, 16),
                            (128, 16))),
        "lnb_h": np.ascontiguousarray(
            np.broadcast_to(np.asarray(L0_attn_ln_b, f32).reshape(1, 16),
                            (128, 16))),
        "ones_h": np.ones((1, 128), f32),
        "ident_h": consts["ident"], "mask1_h": consts["mask1"],
        "mask2_h": consts["mask2"], "cm1_h": consts["cm1"],
        "cp1_h": consts["cp1"], "cm2_h": consts["cm2"], "cp2_h": consts["cp2"],
    }

    in_maps = []
    rowsets = []
    for k in range(NCORES):
        rows = np.arange(8192)
        sel = (rows % 512 >= 64 * k) & (rows % 512 < 64 * (k + 1))
        rout = rows[sel]
        rowsets.append(rout)
        m = dict(shared)
        m["xatt"] = np.ascontiguousarray(X[1024 * k:1024 * (k + 1)])
        m["xout"] = np.ascontiguousarray(X[rout])
        in_maps.append(m)

    nc = _get_prog()
    res = run_bass_kernel_spmd(nc, in_maps, core_ids=list(range(NCORES)))

    out = np.empty((8192, DM), dtype=f32)
    for k in range(NCORES):
        out[rowsets[k]] = res.results[k]["o"]
    return out.reshape(1, 8192, DM)


# version 14 (final)
# speedup vs baseline: 1.6992x; 1.2014x over previous
import numpy as np
import concourse.bass as bass
import concourse.mybir as mybir
from concourse.bass_utils import run_bass_kernel_spmd

F32 = mybir.dt.float32
F32R = mybir.dt.float32r
AX = mybir.AxisListType
AF = mybir.ActivationFunctionType
ALU = mybir.AluOpType

NCORES = 8
ROWS = 1024
DM = 2048
GROUPS = 32
GMICRO = 4096
NLOC = 131072
KC = 16
MC = 44
SCALE = float(8 ** -0.5)
EPS = 1e-5
VC = 4097.0
YW = 40


def _host_consts():
    c = {}
    c["ident"] = np.eye(128, dtype=np.float32)
    p = np.arange(128)
    c["mask1"] = (p[:, None] // 8 == p[None, :] // 8).astype(np.float32)
    c["mask2"] = (p[:, None] % 16 == p[None, :] % 16).astype(np.float32)

    def shift(step, win):
        m = np.zeros((128, 128), np.float32)
        for q in range(128):
            k = q + step
            if 0 <= k < 128 and win(k) == win(q):
                m[k, q] = 1.0
        return m
    c["cm1"] = shift(-1, lambda x: x // 8)
    c["cp1"] = shift(+1, lambda x: x // 8)
    c["cm2"] = shift(-16, lambda x: x % 16)
    c["cp2"] = shift(+16, lambda x: x % 16)
    return c


def _host_wq_wk(qkv_w):
    wq = np.zeros((2, 128, 40), np.float32)
    wk = np.zeros((2, 128, 40), np.float32)
    for par in range(2):
        for b in range(4):
            r0 = 32 * b + 16 * par
            wq[par, r0:r0 + 16, 0:8] = qkv_w[:, 0:8]
            wq[par, r0:r0 + 16, 32:40] = qkv_w[:, 8:16]
            wk[par, r0:r0 + 16, 0:8] = qkv_w[:, 16:24]
            wk[par, r0:r0 + 16, 32:40] = qkv_w[:, 24:32]
    return wq, wk


def _host_wb_skeleton():
    wb = np.zeros((2, 128, 128), np.float32)
    for par in range(2):
        for b in range(4):
            r0 = 32 * b + 16 * par
            wb[par, r0:r0 + 16, 48:64] = np.eye(16, dtype=np.float32)
    return wb


def build_program():
    nc = bass.Bass("TRN2", target_bir_lowering=False, debug=False)

    def din(name, shape):
        return nc.dram_tensor(name, list(shape), F32, kind="ExternalInput").ap()

    xatt = din("xatt", [ROWS, DM])
    xout = din("xout", [ROWS, DM])
    wgate = din("wgate", [MC, 128, KC, 128])
    wup = din("wup", [MC, 128, KC, 128])
    wdown = din("wdown", [MC, 128, KC, 128])
    gsc = din("gsc", [128, MC])
    usc = din("usc", [128, MC])
    dsc = din("dsc", [128, KC])
    wq_h = din("wq_h", [128, 2, 40])
    wk_h = din("wk_h", [128, 2, 40])
    wb_h = din("wb_h", [128, 2, 128])
    qkvw_h = din("qkvw_h", [16, 48])
    projw1_h = din("projw1_h", [8, 16])
    projw2_h = din("projw2_h", [8, 16])
    projb_h = din("projb_h", [1, 16])
    convw0_h = din("convw0_h", [8, 9])
    convb0_h = din("convb0_h", [8, 1])
    convw1_h = din("convw1_h", [8, 9])
    convb1_h = din("convb1_h", [8, 1])
    lnw_h = din("lnw_h", [128, 16])
    lnb_h = din("lnb_h", [128, 16])
    ones_h = din("ones_h", [1, 128])
    ident_h = din("ident_h", [128, 128])
    mask1_h = din("mask1_h", [128, 128])
    mask2_h = din("mask2_h", [128, 128])
    cm1_h = din("cm1_h", [128, 128])
    cp1_h = din("cp1_h", [128, 128])
    cm2_h = din("cm2_h", [128, 128])
    cp2_h = din("cp2_h", [128, 128])

    o = nc.dram_tensor("o", [ROWS, DM], F32, kind="ExternalOutput").ap()
    s_dram = nc.dram_tensor("s_dram", [16, NLOC], F32, kind="Internal").ap()
    wt_dram = nc.dram_tensor("wt_dram", [16, 128], F32, kind="Internal").ap()

    ctx = []

    def sb(name, shape, dtype=F32):
        g = nc.sbuf_tensor(name, list(shape), dtype)
        t = g.__enter__()
        ctx.append(g)
        return t

    def psalloc(name):
        g = nc.psum_tensor(name, [128, 512], F32)
        t = g.__enter__()
        ctx.append(g)
        return t

    ident = sb("ident", [128, 128])
    mask1 = sb("mask1", [128, 128])
    mask2 = sb("mask2", [128, 128])
    cm1 = sb("cm1", [128, 128])
    cp1 = sb("cp1", [128, 128])
    cm2 = sb("cm2", [128, 128])
    cp2 = sb("cp2", [128, 128])
    wq_s = sb("wq_s", [128, 2, 40])
    wk_s = sb("wk_s", [128, 2, 40])
    wb_s = sb("wb_s", [128, 2, 128])
    qkvw_s = sb("qkvw_s", [16, 48])
    projw1_s = sb("projw1_s", [8, 16])
    projw2_s = sb("projw2_s", [8, 16])
    projb_s = sb("projb_s", [1, 16])
    cw0_s = sb("cw0_s", [8, 9])
    cb0_s = sb("cb0_s", [8, 1])
    cw1_s = sb("cw1_s", [8, 9])
    cb1_s = sb("cb1_s", [8, 1])
    ones1 = sb("ones1", [1, 128])
    beff = sb("beff", [128, 16])
    gamma = sb("gamma", [128, 16])
    beta = sb("beta", [128, 16])
    gsc_s = sb("gsc_s", [128, MC])
    usc_s = sb("usc_s", [128, MC])
    dsc_s = sb("dsc_s", [128, KC])
    wvt = sb("wvt", [8, 16])
    winit = sb("winit", [16, 128])
    smallv = sb("smallv", [128, 64])

    arA = sb("arA", [128, 2048])
    arB = sb("arB", [128, 12288])
    arC = sb("arC", [128, 4096])
    arD = sb("arD", [128, 4096])
    arE = sb("arE", [128, 4096])
    arF = sb("arF", [128, 4096])
    xsr = sb("xsr", [128, 8192], F32R)
    hr = sb("hr", [128, 2048], F32R)
    wr = sb("wr", [128, 8192], F32R)

    xnfm = arA[:, :].rearrange("p (k r) -> p k r", k=KC)
    qs = arB[0:48, 0:4096]
    ks = arB[0:48, 4096:8192]
    vs = arB[:, 8192:12288]
    y2cm = arC[0:16, 0:4096]
    tocm = arC[0:16, 0:4096]
    v1tm = arD[:, 0:528].rearrange("p (t c) -> p t c", c=66)
    v2tm = arD[:, 528:1056].rearrange("p (t c) -> p t c", c=66)
    yslab = arD[:, 1056:2336].rearrange("p (t c) -> p t c", c=YW)
    tokr = arD[:, 2336:2848].rearrange("p (t c) -> p t c", c=16)
    pbuf = [arE[:, 0:512], arE[:, 512:1024]]
    ptm = [arE[:, 1024:1536], arE[:, 1536:2048]]
    zslab = arE[:, 2048:2560].rearrange("p (t c) -> p t c", c=16)
    zsq = arE[:, 2560:3072].rearrange("p (t c) -> p t c", c=16)
    tokout = arE[:, 3072:3584].rearrange("p (t c) -> p t c", c=16)
    zred = arE[:, 3584:3616]
    y2zq = [arE[:, 3616:3744], arE[:, 3744:3872]]
    xa_tm = arF[:, 0:2048]
    sq_tm = arF[:, 2048:4096]

    xsv = xsr[:, 0:8192].rearrange("p (k t) -> p k t", k=KC)
    hjv = hr[:, 0:2048].rearrange("p (j t) -> p j t", j=4)
    hraw = [arB[:, 0:256], arB[:, 256:512]]
    hv1 = [arB[:, 512:768], arB[:, 768:1024]]
    gusum = [arB[:, 1024:1536], arB[:, 1536:2048]]
    dslab = arC[:, 0:4096].rearrange("p (k r) -> p k r", k=KC)
    xo_tm = arF[:, 0:2048]
    vhi_tm = arD[:, 0:2048]
    vlo_tm = arD[:, 2048:4096]
    x_tm = arD[:, 0:2048]
    wg_t = [wr[:, 0:2048].rearrange("p (k r) -> p k r", k=KC),
            wr[:, 2048:4096].rearrange("p (k r) -> p k r", k=KC)]
    wu_t = [wr[:, 4096:6144].rearrange("p (k r) -> p k r", k=KC),
            wr[:, 6144:8192].rearrange("p (k r) -> p k r", k=KC)]
    wd_t = wg_t
    g_tm = arF[:, 0:2048]
    s_tm = arF[:, 2048:4096]
    o_tm = s_tm

    y2z = [smallv[:, 0:16], smallv[:, 16:32]]
    rec4 = [smallv[:, 32:36], smallv[:, 36:40]]
    negm2 = [smallv[:, 40:44], smallv[:, 44:48]]
    negm1 = [smallv[:, 48:52], smallv[:, 52:56]]
    mu_c = smallv[:, 56:57]
    rs_c = smallv[:, 57:58]

    pA = psalloc("pA")
    pB = psalloc("pB")
    pC = psalloc("pC")
    pD = psalloc("pD")
    pE = psalloc("pE")
    pF = psalloc("pF")
    pG = psalloc("pG")
    pH = psalloc("pH")

    barg = nc.semaphore("bar")
    barsem = barg.__enter__()
    ctx.append(barg)
    gsems = []
    for gi in range(2):
        gg = nc.semaphore(f"gbar{gi}")
        gsems.append(gg.__enter__())
        ctx.append(gg)
    esems = {}
    for en in ("tensor", "vector", "scalar", "gpsimd"):
        gg = nc.semaphore(f"e_{en}")
        esems[en] = gg.__enter__()
        ctx.append(gg)
    blkg = nc.Block()
    block = blkg.__enter__()

    phases = []
    phase_deps = []

    def phase(dep=None, **fns):
        phases.append(fns)
        if dep is None:
            phase_deps.append(None)
        else:
            if not isinstance(dep, (list, tuple)):
                dep = [dep]
            phase_deps.append(sorted({d for d in dep if d is not None}))
        return len(phases) - 1

    def dma_fn(n):
        def wrap(fn):
            fn.dma_incs = n
            return fn
        return wrap

    def sw_dma_fn(n, gi=0):
        def wrap(fn):
            fn.sw_dma_incs = n
            fn.sw_sem = gi
            return fn
        return wrap


    @dma_fn(24)
    def init_loads(e):
        for src, dst in (
            (ident_h, ident[:]), (mask1_h, mask1[:]), (mask2_h, mask2[:]),
            (cm1_h, cm1[:]), (cp1_h, cp1[:]), (cm2_h, cm2[:]), (cp2_h, cp2[:]),
            (wq_h, wq_s[:]), (wk_h, wk_s[:]), (wb_h, wb_s[:]),
            (qkvw_h, qkvw_s[:]), (projw1_h, projw1_s[:]),
            (projw2_h, projw2_s[:]), (projb_h, projb_s[:]),
            (convw0_h, cw0_s[:]), (convb0_h, cb0_s[:]),
            (convw1_h, cw1_s[:]), (convb1_h, cb1_s[:]),
            (lnw_h, gamma[:]), (lnb_h, beta[:]), (ones_h, ones1[:]),
            (gsc, gsc_s[:]), (usc, usc_s[:]), (dsc, dsc_s[:]),
        ):
            e.dma_start(out=dst, in_=src).then_inc(barsem, 16)
    phase(sync=init_loads)

    def init_memset(e):
        e.memset(winit[:], 0.0)
        e.memset(v1tm[:, :, 16:17], 1.0)
        e.memset(v2tm[:, :, 16:17], 1.0)
        e.memset(y2cm[:, :], 0.0)
    phase(gpsimd=init_memset)

    def init_scaleq(e):
        e.tensor_scalar_mul(wq_s[:, :, :], wq_s[:, :, :], SCALE)
    phase(vector=init_scaleq)

    def init_wvT(e):
        e.transpose(pA[0:8, 0:16], qkvw_s[0:16, 32:40], ident[0:16, 0:16])
        e.transpose(pA[0:8, 16:32], qkvw_s[0:16, 40:48], ident[0:16, 0:16])
    phase(tensor=init_wvT)

    def init_wvT_ev(e):
        e.tensor_copy(winit[0:8, 0:32], pA[0:8, 0:32])
    phase(vector=init_wvT_ev)

    jobs = [
        (0,   (0, 16),  None,            projw1_s, True, True),
        (16,  (0, 16),  cw0_s[:, 3:4],   projw1_s, True, True),
        (32,  (0, 16),  cw0_s[:, 5:6],   projw1_s, True, True),
        (64,  (16, 32), None,            projw2_s, True, True),
        (80,  (16, 32), cw1_s[:, 1:2],   projw2_s, True, True),
        (96,  (16, 32), cw1_s[:, 7:8],   projw2_s, True, True),
        (112, (0, 16),  cw0_s[:, 4:5],   projw1_s, True, False),
        (112, (16, 32), cw1_s[:, 4:5],   projw2_s, False, True),
    ]
    for col, (c0, c1), diag, pr, st, sp in jobs:
        def vj(e, c0=c0, c1=c1, diag=diag):
            if diag is None:
                e.tensor_copy(wvt[:], winit[0:8, c0:c1])
            else:
                e.tensor_scalar(wvt[:], winit[0:8, c0:c1], diag, None, ALU.mult)
        phase(vector=vj)

        def tj(e, col=col, pr=pr, st=st, sp=sp):
            e.matmul(pB[0:16, col:col + 16], wvt[:], pr[:],
                     start=st, stop=sp)
        phase(tensor=tj)

    def init_beff_mm(e):
        e.matmul(pC[0:1, 0:16], cb0_s[:], projw1_s[:], start=True, stop=False)
        e.matmul(pC[0:1, 0:16], cb1_s[:], projw2_s[:], start=False, stop=True)
    phase(tensor=init_beff_mm)

    def init_beff_add(e):
        e.tensor_add(winit[0:1, 48:64], pC[0:1, 0:16], projb_s[:])
    phase(vector=init_beff_add)

    def init_beff_bc(e):
        e.matmul(pD[0:128, 0:16], ones1[:], winit[0:1, 48:64],
                 start=True, stop=True)
    phase(tensor=init_beff_bc)

    def init_beff_ev(e):
        e.tensor_copy(beff[:], pD[0:128, 0:16])
    phase(vector=init_beff_ev)

    def init_vt_evac(e):
        e.tensor_copy(winit[:, 0:48], pB[0:16, 0:48])
        e.tensor_copy(winit[:, 64:128], pB[0:16, 64:128])
    phase(vector=init_vt_evac)

    @dma_fn(1)
    def init_vt_out(e):
        e.dma_start(out=wt_dram, in_=winit[:]).then_inc(barsem, 16)
    phase(sync=init_vt_out)

    @dma_fn(16)
    def init_vt_in(e):
        for par in range(2):
            for b in range(4):
                r0 = 32 * b + 16 * par
                e.dma_start(out=wb_s[r0:r0 + 16, par, 0:48],
                            in_=wt_dram[:, 0:48]).then_inc(barsem, 16)
                e.dma_start(out=wb_s[r0:r0 + 16, par, 64:128],
                            in_=wt_dram[:, 64:128]).then_inc(barsem, 16)
    phase(sync=init_vt_in)

    def ln_tile(src_dram, dst_tm, tile, ldep=None):
        @dma_fn(1)
        def load(e, tile=tile):
            e.dma_start(out=dst_tm, in_=src_dram[tile * 128:(tile + 1) * 128, :]
                        ).then_inc(barsem, 16)
        p_load = phase(dep=ldep, sync=load)

        def mean(e):
            e.reduce_sum(mu_c[:], dst_tm, axis=AX.X, negate=True)
            e.drain()
            e.tensor_scalar_mul(mu_c[:], mu_c[:], 1.0 / DM)
            e.drain()
            return e.tensor_scalar(dst_tm, dst_tm, mu_c[:], None, ALU.add)
        p_mean = phase(dep=[p_load], vector=mean)

        def sq(e):
            return e.activation(sq_tm[:], dst_tm, AF.Square)
        p_sq = phase(dep=[p_mean], scalar=sq)

        def var(e):
            e.reduce_sum(rs_c[:], sq_tm[:], axis=AX.X)
            e.drain()
            return e.tensor_scalar(rs_c[:], rs_c[:], 1.0 / DM, EPS,
                                   ALU.mult, ALU.add)
        p_var = phase(dep=[p_sq], vector=var)

        def rsq(e):
            e.activation(rs_c[:], rs_c[:], AF.Ln)
            e.drain()
            return e.activation(rs_c[:], rs_c[:], AF.Exp, scale=-0.5)
        p_rsq = phase(dep=[p_var], scalar=rsq)

        def fin(e):
            return e.tensor_scalar(dst_tm, dst_tm, rs_c[:], None, ALU.mult)
        return phase(dep=[p_rsq], vector=fin)

    qps = (pA, pB, pC, pD)
    kps = (pE, pF, pG, pH)
    prev_toEv3 = None
    prev_pev1 = None
    prev_scat = None
    for m in range(GROUPS):
        if m % 4 == 0:
            tile8 = m // 4
            p_fin = ln_tile(xatt, xa_tm[:, :], tile8, None)
            p_tpe = None
            for jh in range(4):
                def tp(e, jh=jh):
                    h = None
                    for j2 in range(4):
                        j = jh * 4 + j2
                        h = e.transpose(pA[:, j2 * 128:(j2 + 1) * 128],
                                        xa_tm[:, j * 128:(j + 1) * 128],
                                        ident[:])
                    return h
                p_tp = phase(dep=[p_fin] + ([p_tpe] if p_tpe is not None
                                            else []), tensor=tp)

                def tpe(e, jh=jh):
                    h = None
                    for j2 in range(4):
                        j = jh * 4 + j2
                        h = e.copy(xnfm[:, j, :],
                                   pA[:, j2 * 128:(j2 + 1) * 128])
                    return h
                p_tpe = phase(dep=[p_tp], scalar=tpe)

        def rhs_ap(blkb, m=m):
            mm4 = m % 4
            return xnfm[32 * blkb:32 * blkb + 32, :, mm4 * 32:mm4 * 32 + 32]

        if m % 4 == 0:
            qdep = [p_tpe]
        else:
            qdep = [prev_toEv3, prev_pev1]
        p_qev = {}
        for par in range(2):
            def qkv_mm(e, par=par, rhs_ap=rhs_ap):
                h = None
                for blkb in range(4):
                    tp = (32 * blkb, 0)
                    r = rhs_ap(blkb)
                    e.matmul(qps[blkb][0:40, :],
                             wq_s[32 * blkb:32 * blkb + 32, par, :],
                             r, start=True, stop=True, tile_position=tp)
                    h = e.matmul(kps[blkb][0:40, :],
                                 wk_s[32 * blkb:32 * blkb + 32, par, :],
                                 r, start=True, stop=True, tile_position=tp)
                return h
            p_qmm = phase(dep=qdep if par == 0 else [p_qev[0]],
                          tensor=qkv_mm)

            def qkv_ev(e, par=par):
                h = None
                for blkb in range(4):
                    sp = 2 * blkb + par
                    for slab, psl in ((qs, qps[blkb]), (ks, kps[blkb])):
                        d1 = slab[0:8, :].rearrange(
                            "p (i b s) -> p i b s", i=32, b=16)[
                            :, :, :, sp].transpose([0, 2, 1])
                        e.copy(d1, psl[0:8, :].rearrange(
                            "p (b i) -> p b i", b=16))
                        d2 = slab[32:40, :].rearrange(
                            "p (b s r g) -> p b s r g", b=16, s=8, r=4)[
                            :, :, sp, :, :].transpose([0, 1, 3, 2])
                        h = e.copy(d2, psl[32:40, :].rearrange(
                            "p (b g r) -> p b g r", b=16, g=8))
                return h
            p_qev[par] = phase(dep=[p_qmm], scalar=qkv_ev)

        p_bev = {}
        for par in range(2):
            def qkvB_mm(e, par=par, rhs_ap=rhs_ap):
                h = None
                for blkb in range(4):
                    h = e.matmul(qps[blkb][:, :],
                                 wb_s[32 * blkb:32 * blkb + 32, par, :],
                                 rhs_ap(blkb), start=True, stop=True,
                                 tile_position=(32 * blkb, 0))
                return h
            p_bmm = phase(dep=[p_qev[1]] if par == 0 else [p_bev[0]],
                          tensor=qkvB_mm)

            def qkvB_ev(e, par=par):
                h = None
                for blkb in range(4):
                    sp = 2 * blkb + par
                    d1 = vs[0:64, :].rearrange(
                        "p (i b s) -> p i b s", i=32, b=16)[
                        :, :, :, sp].transpose([0, 2, 1])
                    e.tensor_copy(d1, qps[blkb][0:64, :].rearrange(
                        "p (b i) -> p b i", b=16))
                    d2 = vs[64:128, :].rearrange(
                        "p (b s r g) -> p b s r g", b=16, s=8, r=4)[
                        :, :, sp, :, :].transpose([0, 1, 3, 2])
                    h = e.tensor_copy(d2, qps[blkb][64:128, :].rearrange(
                        "p (b g r) -> p b g r", b=16, g=8))
                return h
            p_bev[par] = phase(dep=[p_bmm], vector=qkvB_ev)
        entry = p_bev[1]

        Sb = (pA, pB)
        PTb = (pC, pD)
        VTb = (pE, pF)
        Yb = (pG, pH)

        def run_pass(qrow, krow, vrow0, vrow1, vident, vwork, negmP, is_b1,
                     pentry, scat_prev):
            p_mm = {}
            p_ng = {}
            p_exp = {}
            p_ptr = {}
            p_pev = {}
            p_pevs = {}
            p_av = {}
            p_tail = {}
            p_rc = {}
            p_dvt = {}
            p_y2ev = {}

            def st_mm(st):
                par = st % 2
                tl = [4 * st + i for i in range(4)]

                def mmS(e, tl=tl, par=par):
                    h = None
                    for i, t2 in enumerate(tl):
                        h = e.matmul(Sb[par][:, i * 128:(i + 1) * 128],
                                     qs[qrow[0]:qrow[1],
                                        t2 * 128:(t2 + 1) * 128],
                                     ks[qrow[0]:qrow[1],
                                        t2 * 128:(t2 + 1) * 128],
                                     start=True, stop=True,
                                     tile_position=(qrow[0] - qrow[0] % 32, 0)
                                     if qrow[0] >= 32 else None)
                    return h
                mdep = [p_exp[st - 2]] if st >= 2 else list(pentry)
                p_mm[st] = phase(dep=mdep, tensor=mmS)

            def st_ng(st):
                par = st % 2

                def negmax(e, par=par):
                    return e.reduce_max(negmP[par][:],
                                        Sb[par][:, :].rearrange(
                                            "p (t x) -> p t x", t=4),
                                        axis=AX.X, negate=True)
                p_ng[st] = phase(dep=[p_mm[st]], vector=negmax)

            def st_exp(st):
                par = st % 2

                def expS(e, par=par):
                    h = None
                    for i in range(4):
                        h = e.activation(pbuf[par][:, i * 128:(i + 1) * 128],
                                         Sb[par][:, i * 128:(i + 1) * 128],
                                         AF.Exp, bias=negmP[par][:, i:i + 1])
                    return h
                dep = [p_ng[st]]
                if st - 2 in p_ptr:
                    dep.append(p_ptr[st - 2])
                p_exp[st] = phase(dep=dep, scalar=expS)

            def st_ptr(st):
                par = st % 2
                tl = [4 * st + i for i in range(4)]

                def ptrans(e, tl=tl, par=par):
                    for i in range(4):
                        e.transpose(PTb[par][:, i * 128:(i + 1) * 128],
                                    pbuf[par][:, i * 128:(i + 1) * 128],
                                    ident[:])
                    h = None
                    for i, t2 in enumerate(tl):
                        h = e.transpose(VTb[par][:, i * 128:i * 128 + 64],
                                        vs[vrow0:vrow1,
                                           t2 * 128:(t2 + 1) * 128], vident)
                    return h
                dep = [p_exp[st]]
                if st - 2 in p_pev:
                    dep.append(p_pev[st - 2])
                if st - 2 in p_pevs:
                    dep.append(p_pevs[st - 2])
                if st - 2 in p_y2ev:
                    dep.append(p_y2ev[st - 2])
                p_ptr[st] = phase(dep=dep, tensor=ptrans)

            def st_pev(st):
                par = st % 2
                tl = [4 * st + i for i in range(4)]

                def pt_evac(e, tl=tl, par=par):
                    h = e.tensor_mul(ptm[par][:, :].rearrange(
                        "p (t x) -> p t x", t=4),
                        PTb[par][:, :].rearrange("p (t x) -> p t x", t=4),
                        mask1[:].unsqueeze(1).broadcast_to([128, 4, 128]))
                    if is_b1:
                        for i, ip in enumerate(tl):
                            h = e.tensor_add(vwork[:, par * 4 + i, 0:16],
                                             VTb[par][:, i * 128:i * 128 + 16],
                                             beff[:])
                    return h
                dep = [p_ptr[st]]
                if st - 2 in p_av:
                    dep.append(p_av[st - 2])
                p_pev[st] = phase(dep=dep, vector=pt_evac)

                def pt_evac_s(e, tl=tl, par=par):
                    h = None
                    for i, ip in enumerate(tl):
                        if is_b1:
                            e.copy(vwork[:, par * 4 + i, 17:49],
                                   VTb[par][:, i * 128 + 16:i * 128 + 48])
                            h = e.copy(tokr[:, ip, :],
                                       VTb[par][:, i * 128 + 48:i * 128 + 64])
                        else:
                            e.copy(vwork[:, par * 4 + i, 0:16],
                                   VTb[par][:, i * 128:i * 128 + 16])
                            h = e.copy(vwork[:, par * 4 + i, 17:65],
                                       VTb[par][:, i * 128 + 16:i * 128 + 64])
                    return h
                sdep = dep + [p_pev[st]] if is_b1 else dep
                p_pevs[st] = phase(dep=sdep, scalar=pt_evac_s)

            def st_av(st):
                par = st % 2
                tl = [4 * st + i for i in range(4)]

                def av(e, tl=tl, par=par):
                    for i, ip in enumerate(tl):
                        lhs = ptm[par][:, i * 128:(i + 1) * 128]
                        e.matmul(Yb[par][:, i * 17:i * 17 + 17], lhs,
                                 vwork[:, par * 4 + i, 0:17],
                                 start=True, stop=True)
                    lep = Yb[par][:, 72:136]
                    e.matmul(lep, cm1[:], vwork[:, par * 4:par * 4 + 4, 17:33],
                             start=True, stop=False)
                    h = e.matmul(lep, cp1[:],
                                 vwork[:, par * 4:par * 4 + 4, 33:49],
                                 start=False, stop=False)
                    if not is_b1:
                        h = e.matmul(lep, ident[:],
                                     vwork[:, par * 4:par * 4 + 4, 49:65],
                                     start=False, stop=True)
                    else:
                        for i, ip in enumerate(tl):
                            h = e.matmul(Yb[par][:, 72 + i * 16:88 + i * 16],
                                         y2cm[:, ip * 128:(ip + 1) * 128],
                                         ident[0:16, 0:16], start=False,
                                         stop=(i == 3), is_transpose=True)
                    return h
                dep = [p_pev[st], p_pevs[st]]
                if st - 2 in p_tail:
                    dep.append(p_tail[st - 2])
                p_av[st] = phase(dep=dep, tensor=av)

            def st_tail_b1(st):
                par = st % 2
                tl = [4 * st + i for i in range(4)]

                def y1ev(e, st=st, par=par):
                    e.copy(yslab[:, 4 * st:4 * st + 4, 0:17],
                           Yb[par][:, 0:68].rearrange(
                               "p (i c) -> p i c", c=17))
                    return e.copy(yslab[:, 4 * st:4 * st + 4, 20:36],
                                  Yb[par][:, 72:136].rearrange(
                                      "p (i c) -> p i c", c=16))
                p_tail[st] = phase(dep=[p_av[st]], scalar=y1ev)

            def st_rc(st):
                par = st % 2

                def recs2(e, par=par):
                    dv = Yb[par][:, 0:68].rearrange(
                        "p (t c) -> p t c", c=17)[:, :, 16]
                    e.activation(rec4[par][:], dv, AF.Ln)
                    e.drain()
                    return e.activation(rec4[par][:], rec4[par][:], AF.Exp,
                                        scale=-1.0)
                p_rc[st] = phase(dep=[p_av[st]], scalar=recs2)

            def st_dvt(st):
                par = st % 2

                def y2div(e, par=par):
                    yav = Yb[par][:, 0:68].rearrange(
                        "p (i c) -> p i c", c=17)[:, :, 0:16]
                    lep = Yb[par][:, 72:136].rearrange(
                        "p (i c) -> p i c", c=16)
                    zq = y2zq[par].rearrange(
                        "p (i c) -> p i c", c=32)[:, :, 0:16]
                    e.tensor_mul(zq, yav,
                                 rec4[par][:].unsqueeze(2).broadcast_to(
                                     [128, 4, 16]))
                    e.drain()
                    return e.tensor_add(zq, zq, lep)
                dep = [p_rc[st]]
                if st - 2 in p_tail:
                    dep.append(p_tail[st - 2])
                p_dv = phase(dep=dep, vector=y2div)

                def y2t(e, par=par):
                    return e.transpose(PTb[par][0:128, 0:128],
                                       y2zq[par][:], ident[:])
                p_tail[st] = phase(dep=[p_dv], tensor=y2t)

            def st_y2ev(st):
                par = st % 2
                tl = [4 * st + i for i in range(4)]

                def y2ev(e, tl=tl, par=par):
                    hh = None
                    for i, t2 in enumerate(tl):
                        b0, h = t2 // 2, t2 % 2
                        dst = y2cm[:, :].rearrange(
                            "p (g r b s) -> p g r b s", g=8, r=4, b=16)[
                            :, :, :, b0, 4 * h:4 * h + 4].transpose(
                            [0, 3, 2, 1])
                        hh = e.tensor_copy(
                            dst, PTb[par][32 * i:32 * i + 16, 0:128]
                            .rearrange("p (s r g) -> p s r g", s=4, r=4))
                    return hh
                dep = [p_tail[st]]
                if scat_prev is not None:
                    dep.append(scat_prev)
                p_y2ev[st] = phase(dep=dep, vector=y2ev)

            if is_b1:
                stages = [st_mm, st_ng, st_exp, st_ptr, st_pev, st_av,
                          st_tail_b1]
            else:
                stages = [st_mm, st_ng, st_exp, st_ptr, st_pev, st_av,
                          st_rc, st_dvt, st_y2ev]
            ns = len(stages)
            II = 3
            for t in range(II * 7 + ns):
                for gidx in range(ns):
                    if (t - gidx) % II == 0:
                        st = (t - gidx) // II
                        if 0 <= st < 8:
                            stages[gidx](st)
            return (p_y2ev.get(7), p_pev[7], p_tail[7])

        b2_y2ev7, _, _ = run_pass(
            (32, 40), None, 64, 128, ident[64:128, 64:128], v2tm,
            negm2, False, [entry], prev_scat)
        _, prev_pev1, p_y1ev7 = run_pass(
            (0, 8), None, 0, 64, ident[0:64, 0:64], v1tm, negm1, True,
            [b2_y2ev7], None)

        def ln16r(e):
            e.activation(zred[:], yslab[:, :, 16], AF.Ln)
            e.drain()
            return e.activation(zred[:], zred[:], AF.Exp, scale=-1.0)
        p_l1 = phase(dep=[p_y1ev7], scalar=ln16r)

        def ln16a(e):
            e.tensor_mul(zslab[:, :, :], yslab[:, :, 0:16],
                         zred[:].unsqueeze(2).broadcast_to(
                             [128, 32, 16]))
            e.drain()
            e.tensor_add(zslab[:, :, :], zslab[:, :, :], yslab[:, :, 20:36])
            e.drain()
            e.reduce_sum(zred[:], zslab[:, :, :], axis=AX.X, negate=True)
            e.drain()
            e.tensor_scalar_mul(zred[:], zred[:], 1.0 / 16.0)
            e.drain()
            return e.tensor_add(zslab[:, :, :], zslab[:, :, :],
                                zred[:].unsqueeze(2).broadcast_to(
                                    [128, 32, 16]))
        p_l2 = phase(dep=[p_l1], vector=ln16a)

        def ln16sq(e):
            return e.activation(zsq[:, :, :], zslab[:, :, :], AF.Square)
        p_l3 = phase(dep=[p_l2], scalar=ln16sq)

        def ln16b(e):
            e.reduce_sum(zred[:], zsq[:, :, :], axis=AX.X)
            e.drain()
            return e.tensor_scalar(zred[:], zred[:], 1.0 / 16.0, EPS,
                                   ALU.mult, ALU.add)
        p_l4 = phase(dep=[p_l3], vector=ln16b)

        def ln16c(e):
            e.activation(zred[:], zred[:], AF.Ln)
            e.drain()
            return e.activation(zred[:], zred[:], AF.Exp, scale=-0.5)
        p_l5 = phase(dep=[p_l4], scalar=ln16c)

        def ln16d(e):
            e.tensor_mul(zslab[:, :, :], zslab[:, :, :],
                         zred[:].unsqueeze(2).broadcast_to(
                             [128, 32, 16]))
            e.drain()
            e.tensor_mul(zslab[:, :, :], zslab[:, :, :],
                         gamma[:].unsqueeze(1).broadcast_to(
                             [128, 32, 16]))
            e.drain()
            e.tensor_add(zslab[:, :, :], zslab[:, :, :],
                         beta[:].unsqueeze(1).broadcast_to(
                             [128, 32, 16]))
            e.drain()
            return e.tensor_add(tokout[:, :, :], zslab[:, :, :],
                                tokr[:, :, :])
        p_l6 = phase(dep=[p_l5], vector=ln16d)

        for q in range(4):
            def toT(e, q=q):
                h = None
                for j in range(4):
                    e.transpose(pA[0:16, j * 128:(j + 1) * 128],
                                tokout[:, q * 8 + j, :], ident[:])
                for j in range(4):
                    h = e.transpose(pB[0:16, j * 128:(j + 1) * 128],
                                    tokout[:, q * 8 + 4 + j, :], ident[:])
                return h
            p_toT = phase(dep=[p_l6] + ([p_toEv] if q > 0 else []),
                          tensor=toT)

            def toEv(e, q=q):
                e.copy(tocm[:, q * 1024:q * 1024 + 512], pA[0:16, :])
                return e.copy(
                    tocm[:, q * 1024 + 512:(q + 1) * 1024], pB[0:16, :])
            p_toEv = phase(dep=[p_toT], scalar=toEv)

        @dma_fn(1)
        def scat(e, m=m):
            e.dma_start(out=s_dram[:, m * GMICRO:(m + 1) * GMICRO],
                        in_=tocm[:, :]).then_inc(barsem, 16)
        prev_scat = phase(dep=[p_toEv], sync=scat)
        prev_toEv3 = p_toEv


    allp = (pA, pB, pC, pD, pE, pF, pG, pH)
    ph_dmm_prev_q = None
    ph_gev_last = None
    ph_fuse_last = None
    ph_fstore_last = None
    for tq in range(4):
        ph_ftpe_last = None
        for tile in range(2):
            p_fin = ln_tile(xout, xo_tm, tq * 2 + tile, None)

            def velt(e):
                e.tensor_scalar_mul(sq_tm[:], xo_tm[:], VC)
                e.drain()
                e.tensor_sub(vlo_tm[:], sq_tm[:], xo_tm[:])
                e.drain()
                e.tensor_sub(vhi_tm[:], sq_tm[:], vlo_tm[:])
                e.drain()
                return e.tensor_sub(vlo_tm[:], xo_tm[:], vhi_tm[:])
            p_velt = phase(dep=[p_fin], vector=velt)

            for jh in range(4):
                def ftp(e, jh=jh):
                    h = None
                    for j2 in range(4):
                        j = jh * 4 + j2
                        e.transpose(pA[:, j2 * 128:(j2 + 1) * 128],
                                    vhi_tm[:, j * 128:(j + 1) * 128], ident[:])
                        h = e.transpose(pB[:, j2 * 128:(j2 + 1) * 128],
                                        vlo_tm[:, j * 128:(j + 1) * 128],
                                        ident[:])
                    return h
                p_ftp = phase(dep=[p_velt] + ([ph_ftpe_last]
                                              if ph_ftpe_last is not None
                                              else []), tensor=ftp)

                def ftpe(e, tile=tile, jh=jh):
                    h = None
                    for j2 in range(4):
                        j = jh * 4 + j2
                        e.tensor_copy(xsv[:, j, tile * 128:(tile + 1) * 128],
                                      pA[:, j2 * 128:(j2 + 1) * 128])
                        h = e.tensor_copy(
                            xsv[:, j, 256 + tile * 128:256 + (tile + 1) * 128],
                            pB[:, j2 * 128:(j2 + 1) * 128])
                    return h
                ph_ftpe_last = phase(dep=[p_ftp], vector=ftpe)

        gu_banks = ((pA, pB), (pC, pD))
        ph_wload = {}
        ph_gu = {}
        ph_gadd = {}
        ph_gsilu = {}
        ph_hmake = {-1: None, -2: None}
        ph_dacc_last = None
        ph_dmm_last = None
        for jb in range(11):
            for jj in range(4):
                j = jb * 4 + jj

                @sw_dma_fn(2, j % 2)
                def wload(e, j=j):
                    e.dma_start(out=wg_t[j % 2][:, :, :], in_=wgate[j, :, :, :]
                                ).then_inc(gsems[j % 2], 16)
                    e.dma_start(out=wu_t[j % 2][:, :, :], in_=wup[j, :, :, :]
                                ).then_inc(gsems[j % 2], 16)
                dep = []
                if j - 2 in ph_gu:
                    dep.append(ph_gu[j - 2])
                if ph_dmm_last is not None:
                    dep.append(ph_dmm_last)
                elif ph_dmm_prev_q is not None and j < 2:
                    dep.append(ph_dmm_prev_q)
                ph_wload[j] = phase(dep=dep, gpsimd=wload)

                def gu_mm(e, j=j):
                    bg, bu = gu_banks[j % 2]
                    h = None
                    for k in range(KC):
                        e.matmul(bg[:, 0:512], wg_t[j % 2][:, k, :],
                                 xsv[:, k, :], start=(k == 0),
                                 stop=(k == KC - 1))
                    for k in range(KC):
                        h = e.matmul(bu[:, 0:512], wu_t[j % 2][:, k, :],
                                     xsv[:, k, :], start=(k == 0),
                                     stop=(k == KC - 1))
                    return h
                gdep = [ph_wload[j]]
                if ph_gadd.get(j - 2) is not None:
                    gdep.append(ph_gadd[j - 2])
                if j < 2:
                    gdep.append(ph_ftpe_last)
                    if ph_gev_last is not None:
                        gdep.append(ph_gev_last)
                if ph_dacc_last is not None:
                    gdep.append(ph_dacc_last)
                ph_gu[j] = phase(dep=gdep, tensor=gu_mm)

                def gadd(e, j=j):
                    bg, bu = gu_banks[j % 2]
                    e.reduce_sum(gusum[j % 2][:, 0:256],
                                 bg[:, 0:512].rearrange(
                                     "p (h t) -> p h t", h=2).transpose(
                                     [0, 2, 1]), axis=AX.X)
                    return e.reduce_sum(gusum[j % 2][:, 256:512],
                                        bu[:, 0:512].rearrange(
                                            "p (h t) -> p h t", h=2).transpose(
                                            [0, 2, 1]), axis=AX.X)
                gadep = [ph_gu[j]]
                if j - 2 in ph_gsilu:
                    gadep.append(ph_gsilu[j - 2])
                ph_gadd[j] = phase(dep=gadep, vector=gadd)

                def gsilu(e, j=j):
                    return e.activation(hv1[j % 2][:], gusum[j % 2][:, 0:256],
                                        AF.Silu, scale=gsc_s[:, j:j + 1])
                ph_gsilu[j] = phase(dep=[ph_gadd[j]], scalar=gsilu)

                def hmake(e, j=j, jj=jj):
                    hw = hraw[j % 2]
                    e.scalar_tensor_tensor(hw[:], gusum[j % 2][:, 256:512],
                                           usc_s[:, j:j + 1], hv1[j % 2][:],
                                           ALU.mult, ALU.mult)
                    e.drain()
                    e.tensor_scalar_mul(sq_tm[:, 0:256], hw[:], VC)
                    e.drain()
                    e.tensor_sub(sq_tm[:, 256:512], sq_tm[:, 0:256], hw[:])
                    e.drain()
                    e.tensor_sub(hjv[:, jj, 0:256], sq_tm[:, 0:256],
                                 sq_tm[:, 256:512])
                    e.drain()
                    return e.tensor_sub(hjv[:, jj, 256:512], hw[:],
                                        hjv[:, jj, 0:256])
                ph_hmake[j] = phase(dep=[ph_gsilu[j]], vector=hmake)

            ph_dload = {}
            ph_dmm = {}
            for jj in range(4):
                j = jb * 4 + jj

                @sw_dma_fn(1, j % 2)
                def dload(e, j=j):
                    e.dma_start(out=wd_t[j % 2][:, :, :], in_=wdown[j, :, :, :]
                                ).then_inc(gsems[j % 2], 16)
                ddep = [ph_gu[jb * 4 + 2 + (jj % 2)]]
                if jj - 2 in ph_dmm:
                    ddep.append(ph_dmm[jj - 2])
                ph_dload[jj] = phase(dep=ddep, gpsimd=dload)

                def d_mm(e, jj=jj, j=j):
                    h = None
                    for oc in range(KC):
                        pp = allp[oc % 8]
                        c0 = (oc // 8) * 256
                        e.matmul(pp[:, c0:c0 + 256], wd_t[j % 2][:, oc, :],
                                 hjv[:, jj, 0:256],
                                 start=(jj == 0 and oc < 8), stop=False)
                        h = e.matmul(pp[:, c0:c0 + 256], wd_t[j % 2][:, oc, :],
                                     hjv[:, jj, 256:512], start=False,
                                     stop=(jj == 3 and oc >= 8))
                    return h
                dmdep = [ph_dload[jj], ph_hmake[jb * 4 + jj]]
                if jj == 0:
                    dmdep.append(ph_gadd[jb * 4 + 3])
                ph_dmm[jj] = phase(dep=dmdep, tensor=d_mm)
                ph_dmm_last = ph_dmm[jj]

            def dacc(e, jb=jb):
                h = None
                for oc in range(KC):
                    pp = allp[oc % 8]
                    c0 = (oc // 8) * 256
                    if jb == 0:
                        h = e.tensor_copy(dslab[:, oc, :], pp[:, c0:c0 + 256])
                    else:
                        h = e.tensor_add(dslab[:, oc, :], dslab[:, oc, :],
                                         pp[:, c0:c0 + 256])
                return h
            ph_dacc_last = phase(dep=[ph_dmm[3]], vector=dacc)

        def gfin(e):
            h = None
            for oc in range(KC):
                h = e.activation(dslab[:, oc, :], dslab[:, oc, :], AF.Silu,
                                 scale=dsc_s[:, oc:oc + 1])
            return h
        p_gfin = phase(dep=[ph_dacc_last], scalar=gfin)
        ph_dmm_prev_q = ph_dmm_last
        ph_dmm_last = None

        for tb in range(2):
            row0 = tq * 256 + tb * 128

            def gT(e, tb=tb):
                for oc in range(4):
                    e.transpose(pA[:, oc * 128:(oc + 1) * 128],
                                dslab[:, oc, tb * 128:(tb + 1) * 128], ident[:])
                for oc in range(4):
                    e.transpose(pB[:, oc * 128:(oc + 1) * 128],
                                dslab[:, 4 + oc, tb * 128:(tb + 1) * 128],
                                ident[:])
                for oc in range(4):
                    e.transpose(pC[:, oc * 128:(oc + 1) * 128],
                                dslab[:, 8 + oc, tb * 128:(tb + 1) * 128],
                                ident[:])
                h = None
                for oc in range(4):
                    h = e.transpose(pD[:, oc * 128:(oc + 1) * 128],
                                    dslab[:, 12 + oc, tb * 128:(tb + 1) * 128],
                                    ident[:])
                return h
            p_gT = phase(dep=[p_gfin] + ([ph_gev_last]
                                          if ph_gev_last is not None
                                          else []), tensor=gT)

            def gEv(e):
                e.copy(g_tm[:, 0:512], pA[:, :])
                e.copy(g_tm[:, 512:1024], pB[:, :])
                e.copy(g_tm[:, 1024:1536], pC[:, :])
                return e.copy(g_tm[:, 1536:2048], pD[:, :])
            ph_gev_last = phase(
                dep=[p_gT] + ([ph_fuse_last] if ph_fuse_last is not None
                              else []), scalar=gEv)

            @dma_fn(2)
            def fload(e, row0=row0):
                e.dma_start(out=x_tm[:], in_=xout[row0:row0 + 128, :]
                            ).then_inc(barsem, 16)
                ta = row0 // 64
                e.dma_start(
                    out=s_tm[:],
                    in_=s_dram.rearrange("c (r j) -> c r j", j=DM)[
                        ta:ta + 2, 0:64, :]).then_inc(barsem, 16)
            p_fload = phase(
                dep=([ph_fuse_last, ph_fstore_last]
                     if ph_fuse_last is not None else [])
                + [ph_ftpe_last, ph_dacc_last], sync=fload)

            def fuse(e):
                e.tensor_mul(o_tm[:], s_tm[:], g_tm[:])
                e.drain()
                return e.tensor_add(o_tm[:], o_tm[:], x_tm[:])
            ph_fuse_last = phase(
                dep=[ph_gev_last, p_fload]
                + ([ph_fstore_last] if ph_fstore_last is not None else []),
                vector=fuse)

            @dma_fn(1)
            def fstore(e, row0=row0):
                e.dma_start(out=o[row0:row0 + 128, :], in_=o_tm[:]
                            ).then_inc(barsem, 16)
            ph_fstore_last = phase(dep=[ph_fuse_last], sync=fstore)

    if _PHASE_LIMIT is not None:
        del phases[_PHASE_LIMIT:]
    allsems = ([esems["tensor"], esems["vector"], esems["scalar"],
                esems["gpsimd"], barsem] + gsems)
    eidx = {"tensor": 0, "vector": 1, "scalar": 2, "gpsimd": 3}
    NS = 7
    cum = []
    tot = [0] * NS
    for p in phases:
        for en, fn in p.items():
            d = getattr(fn, "dma_incs", 0)
            g = getattr(fn, "sw_dma_incs", 0)
            if g:
                tot[5 + getattr(fn, "sw_sem", 0)] += 16 * g
            elif d:
                tot[4] += 16 * d
            else:
                tot[eidx[en]] += 1
        cum.append(tuple(tot))
    finals = tuple(tot)

    for ename in ("sync", "gpsimd", "tensor", "vector", "scalar"):
        dec = getattr(block, ename)

        def mk(ename=ename):
            def body(eng):
                lastw = [0] * NS
                for pi, p in enumerate(phases):
                    if ename not in p:
                        continue
                    dep = phase_deps[pi]
                    if dep is None:
                        tgt = list(cum[pi - 1]) if pi > 0 else [0] * NS
                    else:
                        tgt = [0] * NS
                        for di in dep:
                            if di >= len(phases):
                                di = len(phases) - 1
                            for en2, fn2 in phases[di].items():
                                g2 = getattr(fn2, "sw_dma_incs", 0)
                                d2 = getattr(fn2, "dma_incs", 0)
                                if g2:
                                    si = 5 + getattr(fn2, "sw_sem", 0)
                                elif d2:
                                    si = 4
                                else:
                                    si = eidx[en2]
                                if cum[di][si] > tgt[si]:
                                    tgt[si] = cum[di][si]
                    for si in range(NS):
                        t2 = tgt[si]
                        if t2 > lastw[si]:
                            eng.wait_ge(allsems[si], t2)
                            lastw[si] = t2
                    fn = p[ename]
                    h = fn(eng)
                    if (getattr(fn, "dma_incs", 0) == 0
                            and getattr(fn, "sw_dma_incs", 0) == 0):
                        if h is None:
                            h = eng.drain()
                        h.then_inc(esems[ename], 1)
                if ename == "sync":
                    for si in range(NS):
                        if finals[si] > 0:
                            eng.wait_ge(allsems[si], finals[si])
            return body
        dec(mk())

    blkg.__exit__(None, None, None)
    for g in reversed(ctx):
        g.__exit__(None, None, None)
    nc._phases_debug = (list(phase_deps),
                        [[(en, fn.__name__) for en, fn in p.items()]
                         for p in phases], cum)
    return nc


_PHASE_LIMIT = None
_PROG = None


def _get_prog():
    global _PROG
    if _PROG is None:
        _PROG = build_program()
    return _PROG


def kernel(L0_attn_x, L0_attn_qkv_w, L0_attn_conv_w0, L0_attn_conv_b0,
           L0_attn_conv_w1, L0_attn_conv_b1, L0_attn_proj_w, L0_attn_proj_b,
           L0_attn_ln_w, L0_attn_ln_b, L0_ffn_gate_w, L0_ffn_up_w,
           L0_ffn_down_w, L0_ffn_gate_s, L0_ffn_up_s, L0_ffn_down_s):
    f32 = np.float32
    X = np.ascontiguousarray(np.asarray(L0_attn_x, dtype=f32).reshape(8192, DM))
    qkv_w = np.asarray(L0_attn_qkv_w, dtype=f32)
    consts = _host_consts()
    wq, wk = _host_wq_wk(qkv_w)
    wb = _host_wb_skeleton()

    gate_w = np.asarray(L0_ffn_gate_w, dtype=f32)
    up_w = np.asarray(L0_ffn_up_w, dtype=f32)
    down_w = np.asarray(L0_ffn_down_w, dtype=f32)
    WG = np.ascontiguousarray(
        gate_w.reshape(MC, 128, KC, 128).transpose(0, 3, 2, 1))
    WU = np.ascontiguousarray(
        up_w.reshape(MC, 128, KC, 128).transpose(0, 3, 2, 1))
    WD = np.ascontiguousarray(
        down_w.reshape(KC, 128, MC, 128).transpose(2, 3, 0, 1))

    gsch = np.ascontiguousarray(
        np.asarray(L0_ffn_gate_s, dtype=f32).reshape(MC, 128).T)
    usch = np.ascontiguousarray(
        np.asarray(L0_ffn_up_s, dtype=f32).reshape(MC, 128).T)
    dsch = np.ascontiguousarray(
        np.asarray(L0_ffn_down_s, dtype=f32).reshape(KC, 128).T)

    shared = {
        "wgate": WG, "wup": WU, "wdown": WD,
        "gsc": gsch, "usc": usch, "dsc": dsch,
        "wq_h": np.ascontiguousarray(wq.transpose(1, 0, 2)),
        "wk_h": np.ascontiguousarray(wk.transpose(1, 0, 2)),
        "wb_h": np.ascontiguousarray(wb.transpose(1, 0, 2)),
        "qkvw_h": qkv_w,
        "projw1_h": np.ascontiguousarray(np.asarray(L0_attn_proj_w, f32)[0:8]),
        "projw2_h": np.ascontiguousarray(np.asarray(L0_attn_proj_w, f32)[8:16]),
        "projb_h": np.asarray(L0_attn_proj_b, f32).reshape(1, 16),
        "convw0_h": np.asarray(L0_attn_conv_w0, f32).reshape(8, 9),
        "convb0_h": np.asarray(L0_attn_conv_b0, f32).reshape(8, 1),
        "convw1_h": np.asarray(L0_attn_conv_w1, f32).reshape(8, 9),
        "convb1_h": np.asarray(L0_attn_conv_b1, f32).reshape(8, 1),
        "lnw_h": np.ascontiguousarray(
            np.broadcast_to(np.asarray(L0_attn_ln_w, f32).reshape(1, 16),
                            (128, 16))),
        "lnb_h": np.ascontiguousarray(
            np.broadcast_to(np.asarray(L0_attn_ln_b, f32).reshape(1, 16),
                            (128, 16))),
        "ones_h": np.ones((1, 128), f32),
        "ident_h": consts["ident"], "mask1_h": consts["mask1"],
        "mask2_h": consts["mask2"], "cm1_h": consts["cm1"],
        "cp1_h": consts["cp1"], "cm2_h": consts["cm2"], "cp2_h": consts["cp2"],
    }

    in_maps = []
    rowsets = []
    for k in range(NCORES):
        rows = np.arange(8192)
        sel = (rows % 512 >= 64 * k) & (rows % 512 < 64 * (k + 1))
        rout = rows[sel]
        rowsets.append(rout)
        m = dict(shared)
        m["xatt"] = np.ascontiguousarray(X[1024 * k:1024 * (k + 1)])
        m["xout"] = np.ascontiguousarray(X[rout])
        in_maps.append(m)

    nc = _get_prog()
    res = run_bass_kernel_spmd(nc, in_maps, core_ids=list(range(NCORES)))

    out = np.empty((8192, DM), dtype=f32)
    for k in range(NCORES):
        out[rowsets[k]] = res.results[k]["o"]
    return out.reshape(1, 8192, DM)
